# version 01
# baseline (speedup 1.0000x reference)
from contextlib import ExitStack

import numpy as np
import concourse.bass as bass
import concourse.mybir as mybir
from concourse.bass_utils import run_bass_kernel_spmd

F32 = mybir.dt.float32
F16 = mybir.dt.float16
I32 = mybir.dt.int32
AF = mybir.ActivationFunctionType
ALU = mybir.AluOpType

C = 1024
H = 3072
T = 2048
CC = C // 128
HC = H // 128
TT = T // 128
QG = 256
NQG = T // QG
FG = 512
NFG = T // FG
SG = 256
NSG = T // SG
BHC = 3
NB = HC // BHC
NRS = 4
EPS = 1e-5
QSC = 1.0 / 127.0


class Emitter:

    def __init__(self, core, active, eng):
        self.core = core
        self.active = active
        self.eng = eng
        self.cnt = dict.fromkeys(core.semnames, 0)
        self.state = {}

    def run(self, engine, fn, sem=None, inc=0):
        if sem is not None:
            self.cnt[sem] += inc
        if self.active == engine:
            ins = fn(self.eng)
            if sem is not None:
                assert ins is not None, f"fn on {engine} must return last instruction"
                ins.then_inc(self.core.sem[sem], inc)
        return self.cnt[sem] if sem is not None else None

    def wait(self, engine, sem, val):
        if val is None or val <= 0:
            return
        if self.active == engine:
            self.eng.wait_ge(self.core.sem[sem], val)

    def tok(self, sem):
        return self.cnt[sem]

    def barrier(self):
        for sem in self.core.semnames:
            v = self.cnt[sem]
            if v > 0:
                for e in ("sync", "scalar", "vector", "tensor", "gpsimd"):
                    self.wait(e, sem, v)


class Core:
    semnames = ["ld", "wl", "bc", "mm", "ev", "sc", "gp", "st"]

    def __init__(self, nc, n_seq=4, arena_words=53180, debug=False):
        self.nc = nc
        self.n_seq = n_seq
        es = ExitStack()
        self.es = es
        d = nc.dram_tensor

        self.xin = d("xT", [n_seq, C, T], F32, kind="ExternalInput").ap()
        self.wqt = d("wqt", [C, C], F32, kind="ExternalInput").ap()
        self.wkt = d("wkt", [C, C], F32, kind="ExternalInput").ap()
        self.wvt = d("wvt", [C, C], F32, kind="ExternalInput").ap()
        self.wot = d("wot", [C, C], F32, kind="ExternalInput").ap()
        self.bo = d("bo", [C], F32, kind="ExternalInput").ap()
        self.wgt = d("wgt", [C, H], I32, kind="ExternalInput").ap()
        self.wut = d("wut", [C, H], I32, kind="ExternalInput").ap()
        self.wdt = d("wdt", [H, C], I32, kind="ExternalInput").ap()
        self.out = d("outT", [n_seq, C, T], F32, kind="ExternalOutput").ap()

        dbg = "ExternalOutput" if debug else "Internal"
        self.debug = debug
        self.x1d = d("x1_dram", [C, T], F32, kind=dbg).ap()
        self.x2d = d("x2_dram", [C, T], F32, kind=dbg).ap()
        if debug:
            self.dbg_dacc = d("dbg_dacc", [2, C, T], F32, kind=dbg).ap()
            self.dbg_h = d("dbg_h", [2, C, T], F16, kind=dbg).ap()
            self.dbg_vec = d("dbg_vec", [2, 2, T], F32, kind=dbg).ap()
            self.dbg_bc = d("dbg_bc", [2, 128, 2, SG], F32, kind=dbg).ap()
            self.dbg_k = d("dbg_k", [128, CC, T], F16, kind=dbg).ap()
            self.dbg_v = d("dbg_v", [128, TT, C], F16, kind=dbg).ap()
            self.dbg_den = d("dbg_den", [NQG, QG], F32, kind=dbg).ap()
            self.dbg_y = d("dbg_y", [NQG, 128, CC, QG], F16, kind=dbg).ap()
        self.vecd = d("vec_dram", [8, T], F32).ap()
        self.wg16 = d("wg16_dram", [HC, 128, CC, 128], F16).ap()
        self.wu16 = d("wu16_dram", [HC, 128, CC, 128], F16).ap()
        self.wd16 = d("wd16_dram", [HC, 128, C], F16).ap()
        self.wsp = {}
        for w in ("wq", "wk", "wo"):
            for h in ("hi", "lo"):
                self.wsp[f"{w}_{h}"] = d(f"{w}_{h}_dram", [CC, 128, CC, 128], F16).ap()
        for h in ("hi", "lo"):
            self.wsp[f"wv_{h}"] = d(f"wv_{h}_dram", [4, 128, CC, 256], F16).ap()

        self.arena = es.enter_context(nc.sbuf_tensor("arena", [128, arena_words], F32))
        self._off = 0

        def V(shape, dt=F32, parts=128, at=None):
            n = 1
            for s in shape:
                n *= s
            words = (n + 1) // 2 if dt == F16 else n
            words = (words + 1) & ~1
            off = self._off if at is None else at
            if at is None:
                self._off += words
            ap = self.arena[0:parts, off:off + words]
            ap = ap.bitcast(F16)[:, 0:n] if dt == F16 else ap[:, 0:n]
            if len(shape) > 1:
                letters = list("abcdef")[: len(shape)]
                ap = ap.rearrange(
                    f"p ({' '.join(letters)}) -> p {' '.join(letters)}",
                    **{letters[i]: shape[i] for i in range(len(shape))})
            return ap, off

        self.k_hi, o0 = V([CC, T], F16)
        self.k_lo, o1 = V([CC, T], F16)
        self.v_hi, o2 = V([TT, C], F16)
        self.v_lo, o3 = V([TT, C], F16)
        self.h_hi, _ = V([CC, T], F16, at=o0)
        self.h_lo, _ = V([CC, T], F16, at=o1)
        self.dacc, _ = V([CC, T], F32, at=o2)

        ring0 = self._off
        self.rg, _ = V([NRS, CC, 128], F16)
        self.ru, _ = V([NRS, CC, 128], F16)
        self.rd, _ = V([NRS, C], F16)
        ring_end = self._off
        self.wv_sb, _ = V([2, 4, CC, 256], F16, at=ring0)
        self.rw_hi, _ = V([2, CC, 128], F16, at=ring0 + 4096)
        self.rw_lo, _ = V([2, CC, 128], F16, at=ring0 + 4096 + 1024)
        self._off = max(ring_end, ring0 + 4096 + 2048)

        work0 = self._off
        self.q_hi, _ = V([CC, QG], F16)
        self.q_lo, _ = V([CC, QG], F16)
        self.y_hi, _ = V([CC, QG], F16)
        self.y_lo, _ = V([CC, QG], F16)
        self.th_hi, _ = V([CC, SG], F16)
        self.th_lo, _ = V([CC, SG], F16)
        self.att_hi, _ = V([2, QG], F16)
        self.att_lo, _ = V([2, QG], F16)
        work_end = self._off
        self.act_hi, _ = V([BHC, FG], F16, at=work0)
        self.act_lo, _ = V([BHC, FG], F16, at=work0 + 1024)
        self._off = work_end

        self.xstg, _ = V([CC, SG], F32)
        self.scr, _ = V([CC, SG], F32)

        self.vecs, _ = V([T], F32, parts=128)
        self.bcast, _ = V([2, SG], F32)
        self.recipb, _ = V([QG], F32)
        self.vblk, _ = V([CC, TT], F32)
        self.vrem, _ = V([CC, TT], F32)
        self.masks, _ = V([2, QG], F16)
        self.ones, _ = V([2], F16)
        self.onesf, _ = V([2], F32)
        self.bo_sb, _ = V([CC], F32)
        print(f"[arena] {self._off} words = {self._off * 4 / 1024:.1f} KiB/partition")
        assert self._off <= arena_words, self._off

        ps = lambda name, shape: es.enter_context(nc.psum_tensor(name, shape, F32))
        self.big4 = ps("big4", [128, 2048])
        self.big4_d = self.big4.ap().rearrange("p (a b) -> p a b", a=4)
        self.big4_n = self.big4.ap().rearrange("p (a b) -> p a b", a=8)
        self.pA = ps("pA", [128, 512])
        self.pB = ps("pB", [128, 512])
        self.pC = ps("pC", [128, 512])
        self.pD = ps("pD", [128, 512])
        self.sem = {s: es.enter_context(nc.semaphore(s)) for s in self.semnames}

    ROW_M, ROW_V, ROW_R, ROW_DEN = 0, 32, 64, 96

    def vrow(self, r, sl=slice(None)):
        return self.vecs[r:r + 1, sl]

    def build(self):
        nc = self.nc
        with self.es, nc.Block() as block:
            for name in ("sync", "scalar", "vector", "tensor", "gpsimd"):
                def mk(engname):
                    def body(eng):
                        em = Emitter(self, engname, eng)
                        self.schedule(em)
                    return body
                getattr(block, name)(mk(name))
        return nc

    def schedule(self, em):
        self.setup(em)
        em.barrier()
        for s in range(self.n_seq):
            self.ffn_phase(em, self.xin[s], self.x1d, dbg_idx=0 if s == self.n_seq - 1 else None)
            em.barrier()
            self.attn_phase(em, self.x1d, self.x2d)
            em.barrier()
            self.ffn_phase(em, self.x2d, self.out[s], dbg_idx=1 if s == self.n_seq - 1 else None)
            em.barrier()
        em.run("gpsimd", lambda g: g.wait_ge(self.sem["st"], em.tok("st")) or g.nop())

    def setup(self, em):
        c = self
        em.run("vector", lambda v: v.memset(c.ones, 1.0), sem="ev", inc=1)
        em.run("vector", lambda v: v.memset(c.onesf, 1.0), sem="ev", inc=1)
        em.run("sync", lambda s: s.dma_start(
            out=c.bo_sb, in_=c.bo.rearrange("(d p) -> p d", p=128),
            allow_slow_non_contiguous=True), sem="ld", inc=16)

        def mk_masks(g):
            ins = None
            for var in range(2):
                ins = g.iota(c.masks[:, var, :], [[1, QG]], base=-128 * var,
                             channel_multiplier=-1,
                             allow_small_or_imprecise_dtypes=True)
            return ins
        em.run("gpsimd", mk_masks, sem="gp", inc=1)
        em.wait("vector", "gp", em.tok("gp"))

        def fin_masks(v):
            return v.tensor_scalar(c.masks[:], c.masks[:], -0.5, None, ALU.is_ge)
        em.run("vector", fin_masks, sem="ev", inc=1)

        stg_i = c.xstg.bitcast(I32)[:, :, 0:128]
        t_prev_st = 0
        for wsrc, wdst in ((c.wgt, c.wg16), (c.wut, c.wu16)):
            for hc in range(HC):
                hsl = slice(hc * 128, hc * 128 + 128)
                em.wait("sync", "st", t_prev_st)
                t_ld = em.run("sync", lambda s, wsrc=wsrc, hsl=hsl: s.dma_start(
                    out=stg_i,
                    in_=wsrc.rearrange("(a p) n -> p a n", p=128)[:, :, hsl]),
                    sem="ld", inc=16)
                em.wait("vector", "ld", t_ld)
                em.wait("vector", "st", t_prev_st)
                t_cv = em.run("vector", lambda v: v.tensor_copy(
                    c.th_hi[:, :, 0:128], stg_i), sem="ev", inc=1)
                em.wait("gpsimd", "ev", t_cv)
                t_prev_st = em.run("gpsimd", lambda g, wdst=wdst, hc=hc: g.dma_start(
                    out=wdst[hc], in_=c.th_hi[:, :, 0:128]), sem="st", inc=16)
        stg_i2 = c.xstg.bitcast(I32)[:, 0:4, :]
        for hc in range(HC):
            em.wait("sync", "st", t_prev_st)
            t_ld = em.run("sync", lambda s, hc=hc: s.dma_start(
                out=stg_i2,
                in_=c.wdt.rearrange("(a p) n -> p a n", p=128)[:, hc, :]
                    .rearrange("p (a b) -> p a b", a=4)),
                sem="ld", inc=16)
            em.wait("vector", "ld", t_ld)
            em.wait("vector", "st", t_prev_st)
            t_cv = em.run("vector", lambda v: v.tensor_copy(
                c.th_hi[:, 0:4, :], stg_i2), sem="ev", inc=1)
            em.wait("gpsimd", "ev", t_cv)
            t_prev_st = em.run("gpsimd", lambda g, hc=hc: g.dma_start(
                out=c.wd16[hc],
                in_=c.th_hi[:, 0:4, :].rearrange("p a b -> p (a b)")),
                sem="st", inc=16)

        for wname, wsrc in (("wq", c.wqt), ("wk", c.wkt), ("wo", c.wot), ("wv", c.wvt)):
            ncols = 128 if wname != "wv" else 256
            for blk in range(C // ncols):
                csl = slice(blk * ncols, (blk + 1) * ncols)
                em.wait("sync", "st", t_prev_st)
                t_ld = em.run("sync", lambda s, wsrc=wsrc, csl=csl, ncols=ncols: s.dma_start(
                    out=c.xstg[:, :, 0:ncols],
                    in_=wsrc.rearrange("(a p) n -> p a n", p=128)[:, :, csl]),
                    sem="ld", inc=16)
                em.wait("scalar", "ld", t_ld)
                em.wait("scalar", "st", t_prev_st)
                wscale = 64.0 if wname == "wo" else 1.0
                t_hi = em.run("scalar", lambda sc, ncols=ncols, wscale=wscale: sc.activation(
                    c.th_hi[:, :, 0:ncols], c.xstg[:, :, 0:ncols], AF.Copy,
                    scale=wscale), sem="sc", inc=1)
                em.wait("vector", "sc", t_hi)
                t_lo = em.run("vector", lambda v, ncols=ncols, wscale=wscale: v.scalar_tensor_tensor(
                    c.th_lo[:, :, 0:ncols], c.xstg[:, :, 0:ncols], wscale,
                    c.th_hi[:, :, 0:ncols], ALU.mult, ALU.subtract), sem="ev", inc=1)
                em.wait("gpsimd", "sc", t_hi)
                em.wait("gpsimd", "ev", t_lo)
                em.run("gpsimd", lambda g, wname=wname, blk=blk, ncols=ncols: g.dma_start(
                    out=c.wsp[f"{wname}_hi"][blk], in_=c.th_hi[:, :, 0:ncols]),
                    sem="st", inc=16)
                t_prev_st = em.run("gpsimd", lambda g, wname=wname, blk=blk, ncols=ncols: g.dma_start(
                    out=c.wsp[f"{wname}_lo"][blk], in_=c.th_lo[:, :, 0:ncols]),
                    sem="st", inc=16)

    def ln_stats(self, em, xsrc):
        c = self
        t_cons = 0
        t_statmm = 0
        t_stat_ev = 0
        for g in range(NSG):
            sl = slice(g * SG, (g + 1) * SG)
            em.wait("sync", "sc", t_cons)
            em.wait("sync", "mm", t_statmm)
            t_ld = em.run("sync", lambda s, sl=sl: s.dma_start(
                out=c.xstg, in_=xsrc.rearrange("(a p) t -> p a t", p=128)[:, :, sl]),
                sem="ld", inc=16)
            em.wait("scalar", "ld", t_ld)
            em.wait("scalar", "mm", t_statmm)
            t_cons = em.run("scalar", lambda sc: sc.activation(
                c.scr, c.xstg, AF.Square), sem="sc", inc=1)
            em.wait("tensor", "sc", t_cons)
            em.wait("tensor", "ld", t_ld)
            em.wait("tensor", "ev", t_stat_ev)
            def mm_stats(t):
                ins = None
                for cc in range(CC):
                    ins = t.matmul(c.pC[0:1, 0:SG], c.onesf[:, 0:1], c.xstg[:, cc, :],
                                   start=(cc == 0), stop=(cc == CC - 1))
                for cc in range(CC):
                    ins = t.matmul(c.pD[0:1, 0:SG], c.onesf[:, 0:1], c.scr[:, cc, :],
                                   start=(cc == 0), stop=(cc == CC - 1))
                return ins
            t_mm = em.run("tensor", mm_stats, sem="mm", inc=1)
            t_statmm = t_mm
            em.wait("vector", "mm", t_mm)
            t_stat_ev = em.run("vector", lambda v, sl=sl: (
                v.tensor_copy(c.vrow(self.ROW_M, sl), c.pC[0:1, 0:SG]),
                v.tensor_copy(c.vrow(self.ROW_V, sl), c.pD[0:1, 0:SG]))[-1],
                sem="ev", inc=1)
        em.wait("gpsimd", "ev", t_stat_ev)
        em.run("gpsimd", lambda g: g.dma_start(
            out=c.vecd[0:1, :], in_=c.vrow(self.ROW_M)), sem="st", inc=16)
        t_st = em.run("gpsimd", lambda g: g.dma_start(
            out=c.vecd[1:2, :], in_=c.vrow(self.ROW_V)), sem="st", inc=16)
        em.state["ln_xstg_free"] = t_cons
        em.state["ln_stats_mm"] = t_statmm
        return t_st

    def bcast_mr(self, em, g, t_mr, t_free, width=SG):
        c = self
        sl = slice(g * width, g * width + width)
        em.wait("sync", "st", t_mr)
        em.wait("sync", "ev", t_free)
        return em.run("sync", lambda s, sl=sl, width=width: s.dma_start(
            out=c.bcast[:, :, 0:width],
            in_=c.vecd[0:2, sl].unsqueeze(0).broadcast_to((128, 2, width))),
            sem="bc", inc=16)

    def h_slice(self, em, dst_hi, dst_lo, width=SG):
        c = self
        em.wait("vector", "mm", em.state.get("ln_stats_mm", 0))
        def mk1(v):
            b0 = c.bcast[:, 0, 0:width]
            b1 = c.bcast[:, 1, 0:width]
            t = c.scr[:, 7, 0:width]
            v.tensor_scalar(b0, b0, 1.0 / C, None, ALU.mult)
            v.tensor_tensor(t, b0, b0, ALU.mult)
            v.scalar_tensor_tensor(b1, b1, 1.0 / C, t, ALU.mult, ALU.subtract)
            v.tensor_scalar(b1, b1, EPS, None, ALU.add)
            return v.reciprocal(b1, b1)
        t1 = em.run("vector", mk1, sem="ev", inc=1)
        em.wait("scalar", "ev", t1)
        t2 = em.run("scalar", lambda sc, width=width: sc.activation(
            c.bcast[:, 1, 0:width], c.bcast[:, 1, 0:width], AF.Sqrt),
            sem="sc", inc=1)
        em.wait("vector", "sc", t2)
        def mk2(v):
            v.tensor_tensor(c.scr[:, :, 0:width], c.xstg[:, :, 0:width],
                            c.bcast[:, 0:1, 0:width].broadcast_to((128, CC, width)),
                            ALU.subtract)
            v.tensor_tensor(c.scr[:, :, 0:width], c.scr[:, :, 0:width],
                            c.bcast[:, 1:2, 0:width].broadcast_to((128, CC, width)),
                            ALU.mult)
            v.tensor_copy(dst_hi, c.scr[:, :, 0:width])
            return v.tensor_sub(dst_lo, c.scr[:, :, 0:width], dst_hi)
        return em.run("vector", mk2, sem="ev", inc=1)

    def ffn_phase(self, em, xsrc, xdst, dbg_idx=None):
        c = self
        t_mr = self.ln_stats(em, xsrc)
        if self.debug and dbg_idx is not None:
            em.wait("gpsimd", "st", t_mr)
            em.run("gpsimd", lambda g, dbg_idx=dbg_idx: g.dma_start(
                out=self.dbg_vec[dbg_idx], in_=c.vecd[0:2, :]), sem="st", inc=16)
        t_h = 0
        for g in range(NSG):
            sl = slice(g * SG, (g + 1) * SG)
            em.wait("sync", "ev", t_h)
            em.wait("sync", "sc", em.state["ln_xstg_free"])
            em.wait("sync", "mm", em.state["ln_stats_mm"])
            t_ld = em.run("sync", lambda s, sl=sl: s.dma_start(
                out=c.xstg, in_=xsrc.rearrange("(a p) t -> p a t", p=128)[:, :, sl]),
                sem="ld", inc=16)
            t_bc = self.bcast_mr(em, g, t_mr, t_h)
            em.wait("vector", "ld", t_ld)
            em.wait("vector", "bc", t_bc)
            t_h = self.h_slice(em, c.h_hi[:, :, sl], c.h_lo[:, :, sl])
        if self.debug and dbg_idx is not None:
            em.wait("gpsimd", "ev", t_h)
            em.run("gpsimd", lambda g, dbg_idx=dbg_idx: g.dma_start(
                out=self.dbg_bc[dbg_idx], in_=c.bcast), sem="st", inc=16)

        slot_free = {}
        t_wl = {}

        def load_slot(slot, hc):
            em.wait("sync", "mm", slot_free.get(slot, 0))
            em.run("sync", lambda s: s.dma_start(
                out=c.rg[:, slot], in_=c.wg16[hc]), sem="wl", inc=16)
            em.run("sync", lambda s: s.dma_start(
                out=c.ru[:, slot], in_=c.wu16[hc]), sem="wl", inc=16)
            return em.run("sync", lambda s: s.dma_start(
                out=c.rd[:, slot], in_=c.wd16[hc]), sem="wl", inc=16)

        for i in range(BHC):
            t_wl[(0, i)] = load_slot(i % NRS, i)

        em.wait("tensor", "ev", t_h)
        t_act_prev = 0
        t_down_mm_prev = 0
        t_down_ev_prev = 0
        for b in range(NB):
            for tg in range(NFG):
                sl = slice(tg * FG, (tg + 1) * FG)
                t_act = {}
                for i in range(BHC):
                    slot = (b * BHC + i) % NRS
                    em.wait("tensor", "wl", t_wl[(b, i)])
                    pg, pu = (c.pA, c.pB) if i % 2 == 0 else (c.pC, c.pD)
                    em.wait("tensor", "ev", t_act.get(i - 2, t_act_prev))
                    def mm_gu(t, slot=slot, sl=sl, pg=pg, pu=pu):
                        ins = None
                        for src, ps_ in ((c.rg, pg), (c.ru, pu)):
                            for p, hsp in enumerate((c.h_hi, c.h_lo)):
                                for cc in range(CC):
                                    ins = t.matmul(ps_[:, 0:FG], src[:, slot, cc, :],
                                                   hsp[:, cc, sl],
                                                   start=(p == 0 and cc == 0),
                                                   stop=(p == 1 and cc == CC - 1))
                        return ins
                    t_mm = em.run("tensor", mm_gu, sem="mm", inc=1)
                    em.wait("scalar", "mm", t_mm)
                    t_si = em.run("scalar", lambda sc, pg=pg: sc.activation(
                        c.scr[:, 0:2, :].rearrange("p a b -> p (a b)"), pg[:, 0:FG],
                        AF.Silu, scale=QSC), sem="sc", inc=1)
                    em.wait("vector", "sc", t_si)
                    em.wait("vector", "mm", t_down_mm_prev)
                    def mk_act(v, i=i, pu=pu):
                        z = c.scr[:, 2:4, :].rearrange("p a b -> p (a b)")
                        silu = c.scr[:, 0:2, :].rearrange("p a b -> p (a b)")
                        v.scalar_tensor_tensor(z, pu[:, 0:FG], QSC, silu, ALU.mult, ALU.mult)
                        v.tensor_copy(c.act_hi[:, i, :], z)
                        return v.tensor_sub(c.act_lo[:, i, :], z, c.act_hi[:, i, :])
                    t_act[i] = em.run("vector", mk_act, sem="ev", inc=1)
                t_act_prev = t_act[BHC - 1]
                em.wait("tensor", "ev", t_act_prev)
                for dh in range(2):
                    em.wait("tensor", "ev", t_down_ev_prev)
                    def mm_down(t, b=b, dh=dh):
                        ins = None
                        for i in range(BHC):
                            slot = (b * BHC + i) % NRS
                            for p, asp in enumerate((c.act_hi, c.act_lo)):
                                for dc in range(4):
                                    dsl = slice(dh * 512 + dc * 128,
                                                dh * 512 + (dc + 1) * 128)
                                    ins = t.matmul(c.big4_d[:, dc, 0:FG],
                                                   c.rd[:, slot, dsl], asp[:, i, :],
                                                   start=(i == 0 and p == 0),
                                                   stop=(i == BHC - 1 and p == 1))
                        return ins
                    t_down_mm_prev = em.run("tensor", mm_down, sem="mm", inc=1)
                    em.wait("vector", "mm", t_down_mm_prev)
                    def ev_down(v, b=b, dh=dh, sl=sl):
                        dst = c.dacc[:, dh * 4:(dh + 1) * 4, sl]
                        src = c.big4_d[:, :, 0:FG]
                        if b == 0:
                            return v.tensor_copy(dst, src)
                        return v.tensor_tensor(dst, dst, src, ALU.add)
                    t_down_ev_prev = em.run("vector", ev_down, sem="ev", inc=1)
                if tg == NFG - 1:
                    for i in range(BHC):
                        slot_free[(b * BHC + i) % NRS] = t_down_mm_prev
            if b + 1 < NB:
                for i in range(BHC):
                    hc = (b + 1) * BHC + i
                    t_wl[(b + 1, i)] = load_slot(hc % NRS, hc)

        if self.debug and dbg_idx is not None:
            em.wait("gpsimd", "ev", t_down_ev_prev)
            em.run("gpsimd", lambda g, dbg_idx=dbg_idx: g.dma_start(
                out=self.dbg_dacc[dbg_idx].rearrange("(a p) t -> p a t", p=128),
                in_=c.dacc), sem="st", inc=16)
            em.run("gpsimd", lambda g, dbg_idx=dbg_idx: g.dma_start(
                out=self.dbg_h[dbg_idx].rearrange("(a p) t -> p a t", p=128),
                in_=c.h_hi), sem="st", inc=16)
        t_f = t_down_ev_prev
        t_fin_st = em.tok("st")
        for g in range(NSG):
            sl = slice(g * SG, (g + 1) * SG)
            em.wait("sync", "ev", t_f)
            t_ld = em.run("sync", lambda s, sl=sl: s.dma_start(
                out=c.xstg, in_=xsrc.rearrange("(a p) t -> p a t", p=128)[:, :, sl]),
                sem="ld", inc=16)
            em.wait("vector", "ld", t_ld)
            em.wait("vector", "ev", t_down_ev_prev)
            em.wait("vector", "st", t_fin_st)
            t_f = em.run("vector", lambda v, sl=sl: v.scalar_tensor_tensor(
                c.scr, c.dacc[:, :, sl], 0.5 * QSC, c.xstg, ALU.mult, ALU.add),
                sem="ev", inc=1)
            em.wait("gpsimd", "ev", t_f)
            t_fin_st = em.run("gpsimd", lambda g_, sl=sl: g_.dma_start(
                out=xdst.rearrange("(a p) t -> p a t", p=128)[:, :, sl], in_=c.scr),
                sem="st", inc=16)

    def attn_phase(self, em, xsrc, xdst):
        c = self
        t_mr = self.ln_stats(em, xsrc)

        t_wv = 0
        for h in range(2):
            name = "wv_hi" if h == 0 else "wv_lo"
            for dq in range(4):
                t_wv = em.run("sync", lambda s, h=h, dq=dq, name=name: s.dma_start(
                    out=c.wv_sb[:, h, dq], in_=c.wsp[name][dq]), sem="wl", inc=16)

        rw_free = {0: 0, 1: 0}

        def load_rw(slot, wname, blk):
            em.wait("sync", "mm", rw_free[slot])
            em.run("sync", lambda s: s.dma_start(
                out=c.rw_hi[:, slot], in_=c.wsp[f"{wname}_hi"][blk]), sem="wl", inc=16)
            return em.run("sync", lambda s: s.dma_start(
                out=c.rw_lo[:, slot], in_=c.wsp[f"{wname}_lo"][blk]), sem="wl", inc=16)

        first_vb = True
        t_h = 0
        t_th_cons = 0
        for g in range(NSG):
            sl = slice(g * SG, (g + 1) * SG)
            em.wait("sync", "ev", t_h)
            em.wait("sync", "sc", em.state["ln_xstg_free"])
            em.wait("sync", "mm", em.state["ln_stats_mm"])
            t_ld = em.run("sync", lambda s, sl=sl: s.dma_start(
                out=c.xstg, in_=xsrc.rearrange("(a p) t -> p a t", p=128)[:, :, sl]),
                sem="ld", inc=16)
            t_bc = self.bcast_mr(em, g, t_mr, t_h)
            em.wait("vector", "ld", t_ld)
            em.wait("vector", "bc", t_bc)
            em.wait("vector", "mm", t_th_cons)
            t_h = self.h_slice(em, c.th_hi, c.th_lo)
            em.wait("tensor", "ev", t_h)
            em.wait("tensor", "wl", t_wv)
            rw_tok = {0: load_rw(0, "wk", 0), 1: load_rw(1, "wk", 1)}
            t_evk = {}
            for dkc in range(CC):
                slot = dkc % 2
                em.wait("tensor", "wl", rw_tok[dkc])
                ps_ = c.pA if dkc % 2 == 0 else c.pB
                em.wait("tensor", "ev", t_evk.get(dkc - 2, 0))
                def mm_k(t, slot=slot, ps_=ps_):
                    ins = None
                    for p, (whalf, hsp) in enumerate((
                            (c.rw_hi, c.th_hi), (c.rw_hi, c.th_lo), (c.rw_lo, c.th_hi))):
                        for cc in range(CC):
                            ins = t.matmul(ps_[:, 0:SG], whalf[:, slot, cc, :],
                                           hsp[:, cc, :],
                                           start=(p == 0 and cc == 0),
                                           stop=(p == 2 and cc == CC - 1))
                    return ins
                t_mm = em.run("tensor", mm_k, sem="mm", inc=1)
                rw_free[slot] = t_mm
                if dkc + 2 < CC:
                    rw_tok[dkc + 2] = load_rw(slot, "wk", dkc + 2)
                em.wait("vector", "mm", t_mm)
                def ev_k(v, dkc=dkc, sl=sl, ps_=ps_):
                    v.tensor_scalar(c.k_hi[:, dkc, sl], ps_[:, 0:SG], 0.1, None, ALU.mult)
                    return v.scalar_tensor_tensor(c.k_lo[:, dkc, sl], ps_[:, 0:SG], 0.1,
                                                  c.k_hi[:, dkc, sl], ALU.mult, ALU.subtract)
                t_evk[dkc] = em.run("vector", ev_k, sem="ev", inc=1)
            t_th_cons = em.tok("mm")
            t_evv = {}
            for ti in range(2):
                tt = g * 2 + ti
                for dq in range(4):
                    idx = ti * 4 + dq
                    ps_ = c.pA if idx % 2 == 0 else c.pB
                    em.wait("tensor", "ev", t_evv.get(idx - 2, t_evk[CC - 1]))
                    def mm_v(t, ti=ti, dq=dq, ps_=ps_):
                        ins = None
                        for p, (hsp, wh) in enumerate(((c.th_hi, 0), (c.th_lo, 0),
                                                       (c.th_hi, 1))):
                            for cc in range(CC):
                                ins = t.matmul(ps_[:, 0:256],
                                               hsp[:, cc, ti * 128:(ti + 1) * 128],
                                               c.wv_sb[:, wh, dq, cc, :],
                                               start=(p == 0 and cc == 0),
                                               stop=(p == 2 and cc == CC - 1))
                        return ins
                    t_mm = em.run("tensor", mm_v, sem="mm", inc=1)
                    em.wait("vector", "mm", t_mm)
                    def ev_v(v, tt=tt, dq=dq, ps_=ps_):
                        dsl = slice(dq * 256, (dq + 1) * 256)
                        v.tensor_copy(c.v_hi[:, tt, dsl], ps_[:, 0:256])
                        return v.tensor_sub(c.v_lo[:, tt, dsl], ps_[:, 0:256],
                                            c.v_hi[:, tt, dsl])
                    t_evv[idx] = em.run("vector", ev_v, sem="ev", inc=1)
                em.wait("tensor", "ev", t_evv[ti * 4 + 3])
                def mm_vb(t, tt=tt, first=first_vb):
                    ins = None
                    for dc in range(CC):
                        for k_, vsp in enumerate((c.v_hi, c.v_lo)):
                            ins = t.matmul(c.pD[:, dc * 64 + tt:dc * 64 + tt + 1],
                                           vsp[:, tt, dc * 128:(dc + 1) * 128],
                                           c.ones[:, 0:1],
                                           start=(first and dc == 0 and k_ == 0),
                                           stop=(tt == TT - 1 and dc == CC - 1 and k_ == 1))
                    return ins
                em.run("tensor", mm_vb, sem="mm", inc=1)
                first_vb = False
            t_th_cons = em.tok("mm")
        t_vb = em.run("tensor", lambda t: t.nop(), sem="mm", inc=1)
        em.wait("vector", "mm", t_vb)
        def mk_vrem(v):
            v.tensor_copy(c.vblk,
                          c.pD[:, 0:512].rearrange("p (a b) -> p a b", a=CC)[:, :, 0:TT])
            for tt in range(1, TT):
                v.tensor_add(c.vblk[:, :, tt], c.vblk[:, :, tt], c.vblk[:, :, tt - 1])
            ins = None
            for tt in range(TT):
                ins = v.tensor_sub(c.vrem[:, :, tt], c.vblk[:, :, TT - 1],
                                   c.vblk[:, :, tt])
            return ins
        t_vrem = em.run("vector", mk_vrem, sem="ev", inc=1)
        if self.debug:
            em.wait("gpsimd", "ev", t_vrem)
            em.run("gpsimd", lambda g: g.dma_start(out=self.dbg_k, in_=c.k_hi),
                   sem="st", inc=16)
            em.run("gpsimd", lambda g: g.dma_start(out=self.dbg_v, in_=c.v_hi),
                   sem="st", inc=16)

        t_q_cons = 0
        t_y_cons = 0
        t_stg_cons = 0
        t_att = {}
        t_av = {}
        t_den_prev = 0
        t_y_prev = 0
        t_o_st = em.tok("st")
        for qg in range(NQG):
            sl = slice(qg * QG, (qg + 1) * QG)
            kb_max = 2 * qg + 1
            em.wait("sync", "ev", max(t_stg_cons, t_h))
            t_ld = em.run("sync", lambda s, sl=sl: s.dma_start(
                out=c.xstg, in_=xsrc.rearrange("(a p) t -> p a t", p=128)[:, :, sl]),
                sem="ld", inc=16)
            t_bc = self.bcast_mr(em, qg, t_mr, max(t_stg_cons, t_h))
            em.wait("vector", "ld", t_ld)
            em.wait("vector", "bc", t_bc)
            em.wait("vector", "mm", t_th_cons)
            em.wait("vector", "st", t_o_st)
            t_h = self.h_slice(em, c.th_hi, c.th_lo)
            em.wait("tensor", "ev", t_h)
            rwq = {0: load_rw(0, "wq", 0), 1: load_rw(1, "wq", 1)}
            t_evq = {}
            for dqc in range(CC):
                slot = dqc % 2
                em.wait("tensor", "wl", rwq[dqc])
                ps_ = c.pA if dqc % 2 == 0 else c.pB
                em.wait("tensor", "ev", t_evq.get(dqc - 2, 0))
                def mm_q(t, slot=slot, ps_=ps_):
                    ins = None
                    for p, (whalf, hsp) in enumerate((
                            (c.rw_hi, c.th_hi), (c.rw_hi, c.th_lo), (c.rw_lo, c.th_hi))):
                        for cc in range(CC):
                            ins = t.matmul(ps_[:, 0:QG], whalf[:, slot, cc, :],
                                           hsp[:, cc, :],
                                           start=(p == 0 and cc == 0),
                                           stop=(p == 2 and cc == CC - 1))
                    return ins
                t_mm = em.run("tensor", mm_q, sem="mm", inc=1)
                rw_free[slot] = t_mm
                if dqc + 2 < CC:
                    rwq[dqc + 2] = load_rw(slot, "wq", dqc + 2)
                em.wait("vector", "mm", t_mm)
                em.wait("vector", "mm", t_q_cons)
                def ev_q(v, dqc=dqc, ps_=ps_):
                    v.tensor_scalar(c.q_hi[:, dqc, :], ps_[:, 0:QG], 0.1, None, ALU.mult)
                    return v.scalar_tensor_tensor(c.q_lo[:, dqc, :], ps_[:, 0:QG], 0.1,
                                                  c.q_hi[:, dqc, :], ALU.mult, ALU.subtract)
                t_evq[dqc] = em.run("vector", ev_q, sem="ev", inc=1)
            t_th_cons = em.tok("mm")
            em.wait("tensor", "ev", max(t_evq[CC - 1], t_y_prev, t_den_prev, t_vrem))
            for kb in range(kb_max + 1):
                ps_ = c.pA if kb % 2 == 0 else c.pB
                abuf = kb % 2
                em.wait("tensor", "ev", t_att.get(kb - 2, 0))
                def mm_s(t, kb=kb, ps_=ps_):
                    ins = None
                    for p, (ksp, qsp) in enumerate((
                            (c.k_hi, c.q_hi), (c.k_hi, c.q_lo), (c.k_lo, c.q_hi))):
                        for dkc in range(CC):
                            ins = t.matmul(ps_[:, 0:QG],
                                           ksp[:, dkc, kb * 128:(kb + 1) * 128],
                                           qsp[:, dkc, :],
                                           start=(p == 0 and dkc == 0),
                                           stop=(p == 2 and dkc == CC - 1))
                    return ins
                t_mm = em.run("tensor", mm_s, sem="mm", inc=1)
                em.wait("vector", "mm", max(t_mm, t_av.get(kb - 2, 0)))
                if kb >= 2 * qg:
                    var = kb - 2 * qg
                    def ev_att(v, ps_=ps_, abuf=abuf, var=var):
                        z = c.scr[:, 0, 0:QG]
                        v.tensor_scalar(z, ps_[:, 0:QG], 0.01, 1.0, ALU.mult, ALU.add)
                        v.tensor_tensor(z, z, c.masks[:, var, :], ALU.mult)
                        v.tensor_scalar(c.att_hi[:, abuf, :], z, -1.0, None, ALU.add)
                        return v.scalar_tensor_tensor(c.att_lo[:, abuf, :], z, -1.0,
                                                      c.att_hi[:, abuf, :],
                                                      ALU.add, ALU.subtract)
                else:
                    def ev_att(v, ps_=ps_, abuf=abuf):
                        v.tensor_scalar(c.att_hi[:, abuf, :], ps_[:, 0:QG], 0.01,
                                        None, ALU.mult)
                        return v.scalar_tensor_tensor(c.att_lo[:, abuf, :], ps_[:, 0:QG],
                                                      0.01, c.att_hi[:, abuf, :],
                                                      ALU.mult, ALU.subtract)
                t_att[kb] = em.run("vector", ev_att, sem="ev", inc=1)
                em.wait("tensor", "ev", t_att[kb])
                last = (kb == kb_max)
                def mm_av(t, kb=kb, abuf=abuf, first=(kb == 0), last=last):
                    for k_, asp in enumerate((c.att_hi, c.att_lo)):
                        t.matmul(c.pC[0:1, 0:QG], c.ones[:, 0:1], asp[:, abuf, :],
                                 start=(first and k_ == 0), stop=(last and k_ == 1))
                    ins = None
                    for p, (vsp, asp) in enumerate((
                            (c.v_hi, c.att_hi), (c.v_hi, c.att_lo), (c.v_lo, c.att_hi))):
                        for dc in range(CC):
                            ins = t.matmul(c.big4_n[:, dc, :],
                                           vsp[:, kb, dc * 128:(dc + 1) * 128],
                                           asp[:, abuf, :],
                                           start=(first and p == 0 and dc % 2 == 0),
                                           stop=(last and p == 2 and dc % 2 == 1))
                    return ins
                t_av[kb] = em.run("tensor", mm_av, sem="mm", inc=1)
            t_q_cons = em.tok("mm")
            em.wait("vector", "mm", t_av[kb_max])
            em.wait("vector", "st", em.state.get("t_dst_prev", 0))
            def fin_den(v, kb_max=kb_max):
                const = -(T - 128 * (kb_max + 1)) + EPS
                v.tensor_scalar(c.vrow(self.ROW_DEN, slice(0, QG)), c.pC[0:1, 0:QG],
                                const, None, ALU.add)
                v.reciprocal(c.vrow(self.ROW_DEN, slice(0, QG)),
                             c.vrow(self.ROW_DEN, slice(0, QG)))
                return v.tensor_scalar(c.vrow(self.ROW_DEN, slice(0, QG)),
                                       c.vrow(self.ROW_DEN, slice(0, QG)),
                                       1.0 / 64.0, None, ALU.mult)
            t_den = em.run("vector", fin_den, sem="ev", inc=1)
            t_den_prev = t_den
            em.wait("gpsimd", "ev", t_den)
            t_dst = em.run("gpsimd", lambda g: g.dma_start(
                out=c.vecd[4:5, 0:QG], in_=c.vrow(self.ROW_DEN, slice(0, QG))),
                sem="st", inc=16)
            em.state["t_dst_prev"] = t_dst
            if self.debug:
                em.run("gpsimd", lambda g, qg=qg: g.dma_start(
                    out=self.dbg_den[qg:qg + 1, :], in_=c.vrow(self.ROW_DEN, slice(0, QG))),
                    sem="st", inc=16)
            em.wait("sync", "st", t_dst)
            em.wait("sync", "ev", t_y_prev)
            t_rb = em.run("sync", lambda s: s.dma_start(
                out=c.recipb, in_=c.vecd[4:5, 0:QG].broadcast_to((128, QG))),
                sem="bc", inc=16)
            em.wait("vector", "bc", t_rb)
            em.wait("vector", "mm", t_y_cons)
            def ev_y(v, kb_max=kb_max):
                ins = None
                for dc in range(CC):
                    z = c.scr[:, 1, 0:QG]
                    v.tensor_scalar(z, c.big4_n[:, dc, :],
                                    c.vrem[:, dc, kb_max:kb_max + 1], None, ALU.subtract)
                    v.tensor_tensor(z, z, c.recipb, ALU.mult)
                    v.tensor_copy(c.y_hi[:, dc, :], z)
                    ins = v.tensor_sub(c.y_lo[:, dc, :], z, c.y_hi[:, dc, :])
                return ins
            t_y = em.run("vector", ev_y, sem="ev", inc=1)
            t_y_prev = t_y
            if self.debug:
                em.wait("gpsimd", "ev", t_y)
                em.run("gpsimd", lambda g, qg=qg: g.dma_start(
                    out=self.dbg_y[qg], in_=c.y_hi), sem="st", inc=16)
            em.wait("tensor", "ev", t_y)
            rwo = {0: load_rw(0, "wo", 0), 1: load_rw(1, "wo", 1)}
            t_evo = {}
            for doc in range(CC):
                slot = doc % 2
                em.wait("tensor", "wl", rwo[doc])
                ps_ = c.pA if doc % 2 == 0 else c.pB
                em.wait("tensor", "ev", t_evo.get(doc - 2, 0))
                def mm_o(t, slot=slot, ps_=ps_):
                    ins = None
                    for p, (whalf, ysp) in enumerate((
                            (c.rw_hi, c.y_hi), (c.rw_hi, c.y_lo), (c.rw_lo, c.y_hi))):
                        for cc in range(CC):
                            ins = t.matmul(ps_[:, 0:QG], whalf[:, slot, cc, :],
                                           ysp[:, cc, :],
                                           start=(p == 0 and cc == 0),
                                           stop=(p == 2 and cc == CC - 1))
                    return ins
                t_mm = em.run("tensor", mm_o, sem="mm", inc=1)
                rw_free[slot] = t_mm
                if doc + 2 < CC:
                    rwo[doc + 2] = load_rw(slot, "wo", doc + 2)
                em.wait("vector", "mm", t_mm)
                em.wait("vector", "st", t_o_st)
                def ev_o(v, doc=doc, ps_=ps_):
                    return v.scalar_tensor_tensor(c.scr[:, doc, :], ps_[:, 0:QG],
                                                  c.bo_sb[:, doc:doc + 1],
                                                  c.xstg[:, doc, :], ALU.add, ALU.add)
                t_evo[doc] = em.run("vector", ev_o, sem="ev", inc=1)
            t_y_cons = em.tok("mm")
            t_stg_cons = t_evo[CC - 1]
            em.wait("gpsimd", "ev", t_evo[CC - 1])
            t_o_st = em.run("gpsimd", lambda g_, sl=sl: g_.dma_start(
                out=xdst.rearrange("(a p) t -> p a t", p=128)[:, :, sl], in_=c.scr),
                sem="st", inc=16)


def build_nc(n_seq=4, debug=False):
    nc = bass.Bass("TRN2", target_bir_lowering=False)
    core = Core(nc, n_seq=n_seq, debug=debug)
    return core.build()




def kernel(L0_attn_x, L0_attn_Wq, L0_attn_Wk, L0_attn_Wv, L0_attn_Wo, L0_attn_bo,
           L0_ffn_w_gate, L0_ffn_w_up, L0_ffn_w_down):
    n_cores = 8
    n_seq_per_core = 4
    x = np.asarray(L0_attn_x)
    B = x.shape[0]
    assert B == n_cores * n_seq_per_core

    xT = np.ascontiguousarray(x.transpose(0, 2, 1))
    wqt = np.ascontiguousarray(np.asarray(L0_attn_Wq).T)
    wkt = np.ascontiguousarray(np.asarray(L0_attn_Wk).T)
    wvt = np.ascontiguousarray(np.asarray(L0_attn_Wv).T)
    wot = np.ascontiguousarray(np.asarray(L0_attn_Wo).T)
    bo = np.ascontiguousarray(np.asarray(L0_attn_bo))
    wgt = np.ascontiguousarray(np.asarray(L0_ffn_w_gate).T)
    wut = np.ascontiguousarray(np.asarray(L0_ffn_w_up).T)
    wdt = np.ascontiguousarray(np.asarray(L0_ffn_w_down).T)

    nc = bass.Bass("TRN2", target_bir_lowering=False)
    core = Core(nc, n_seq=n_seq_per_core)
    nc = core.build()

    in_maps = []
    for ci in range(n_cores):
        in_maps.append({
            "xT": xT[ci * n_seq_per_core:(ci + 1) * n_seq_per_core],
            "wqt": wqt, "wkt": wkt, "wvt": wvt, "wot": wot, "bo": bo,
            "wgt": wgt, "wut": wut, "wdt": wdt,
        })
    res = run_bass_kernel_spmd(nc, in_maps, core_ids=list(range(n_cores)))
    outs = [r["outT"] for r in res.results]
    out = np.concatenate(outs, axis=0).transpose(0, 2, 1)
    return np.ascontiguousarray(out.astype(np.float32))


# version 06
# speedup vs baseline: 1.0342x; 1.0342x over previous
from contextlib import ExitStack

import numpy as np
import concourse.bass as bass
import concourse.mybir as mybir
from concourse.bass_utils import run_bass_kernel_spmd

F32 = mybir.dt.float32
F16 = mybir.dt.float16
I32 = mybir.dt.int32
AF = mybir.ActivationFunctionType
ALU = mybir.AluOpType

C = 1024
H = 3072
T = 2048
CC = C // 128
HC = H // 128
TT = T // 128
QG = 256
NQG = T // QG
FG = 512
NFG = T // FG
SG = 256
NSG = T // SG
BHC = 3
NB = HC // BHC
NRS = 4
EPS = 1e-5
QSC = 1.0 / 127.0


class Emitter:

    def __init__(self, core, active, eng):
        self.core = core
        self.active = active
        self.eng = eng
        self.cnt = dict.fromkeys(core.semnames, 0)
        self.state = {}

    def run(self, engine, fn, sem=None, inc=0):
        if sem is not None:
            self.cnt[sem] += inc
        if self.active == engine:
            ins = fn(self.eng)
            if sem is not None:
                assert ins is not None, f"fn on {engine} must return last instruction"
                ins.then_inc(self.core.sem[sem], inc)
        return self.cnt[sem] if sem is not None else None

    def wait(self, engine, sem, val):
        if val is None or val <= 0:
            return
        if self.active == engine:
            self.eng.wait_ge(self.core.sem[sem], val)

    def tok(self, sem):
        return self.cnt[sem]

    def barrier(self):
        for sem in self.core.semnames:
            v = self.cnt[sem]
            if v > 0:
                for e in ("sync", "scalar", "vector", "tensor", "gpsimd"):
                    self.wait(e, sem, v)


class Core:
    semnames = ["ld", "wl", "bc", "mm", "ev", "sc", "gp", "st"]

    def __init__(self, nc, n_seq=4, arena_words=53180, debug=False, paranoid=0, phases='faf', vmode=0):
        self.nc = nc
        self.n_seq = n_seq
        self.paranoid = paranoid
        self.phases = phases
        self.vmode = vmode
        es = ExitStack()
        self.es = es
        d = nc.dram_tensor

        self.xin = d("xT", [n_seq, C, T], F32, kind="ExternalInput").ap()
        self.wqt = d("wqt", [C, C], F32, kind="ExternalInput").ap()
        self.wkt = d("wkt", [C, C], F32, kind="ExternalInput").ap()
        self.wvt = d("wvt", [C, C], F32, kind="ExternalInput").ap()
        self.wot = d("wot", [C, C], F32, kind="ExternalInput").ap()
        self.bo = d("bo", [C], F32, kind="ExternalInput").ap()
        self.wgt = d("wgt", [C, H], I32, kind="ExternalInput").ap()
        self.wut = d("wut", [C, H], I32, kind="ExternalInput").ap()
        self.wdt = d("wdt", [H, C], I32, kind="ExternalInput").ap()
        self.out = d("outT", [n_seq, C, T], F32, kind="ExternalOutput").ap()

        dbg = "ExternalOutput" if debug else "Internal"
        self.debug = debug
        self.x1d = d("x1_dram", [C, T], F32, kind=dbg).ap()
        self.x2d = d("x2_dram", [C, T], F32, kind=dbg).ap()
        if debug:
            self.dbg_dacc = d("dbg_dacc", [2, C, T], F32, kind=dbg).ap()
            self.dbg_h = d("dbg_h", [2, C, T], F16, kind=dbg).ap()
            self.dbg_vec = d("dbg_vec", [2, 2, T], F32, kind=dbg).ap()
            self.dbg_bc = d("dbg_bc", [2, 128, 2, SG], F32, kind=dbg).ap()
            self.dbg_k = d("dbg_k", [128, CC, T], F16, kind=dbg).ap()
            self.dbg_v = d("dbg_v", [128, TT, C], F16, kind=dbg).ap()
            self.dbg_den = d("dbg_den", [NQG, QG], F32, kind=dbg).ap()
            self.dbg_y = d("dbg_y", [NQG, 128, CC, QG], F16, kind=dbg).ap()
        self.vecd = d("vec_dram", [8, T], F32).ap()
        self.wg16 = d("wg16_dram", [HC, 128, CC, 128], F16).ap()
        self.wu16 = d("wu16_dram", [HC, 128, CC, 128], F16).ap()
        self.wd16 = d("wd16_dram", [HC, 128, C], F16).ap()
        self.wsp = {}
        for w in ("wq", "wk", "wo"):
            for h in ("hi", "lo"):
                self.wsp[f"{w}_{h}"] = d(f"{w}_{h}_dram", [CC, 128, CC, 128], F16).ap()
        for h in ("hi", "lo"):
            self.wsp[f"wv_{h}"] = d(f"wv_{h}_dram", [4, 128, CC, 256], F16).ap()

        self.arena = es.enter_context(nc.sbuf_tensor("arena", [128, arena_words], F32))
        self._off = 0

        def V(shape, dt=F32, parts=128, at=None):
            n = 1
            for s in shape:
                n *= s
            words = (n + 1) // 2 if dt == F16 else n
            words = (words + 1) & ~1
            off = self._off if at is None else at
            if at is None:
                self._off += words
            ap = self.arena[0:parts, off:off + words]
            ap = ap.bitcast(F16)[:, 0:n] if dt == F16 else ap[:, 0:n]
            if len(shape) > 1:
                letters = list("abcdef")[: len(shape)]
                ap = ap.rearrange(
                    f"p ({' '.join(letters)}) -> p {' '.join(letters)}",
                    **{letters[i]: shape[i] for i in range(len(shape))})
            return ap, off

        self.k_hi, o0 = V([CC, T], F16)
        self.k_lo, o1 = V([CC, T], F16)
        self.v_hi, o2 = V([TT, C], F16)
        self.v_lo, o3 = V([TT, C], F16)
        self.h_hi, _ = V([CC, T], F16, at=o0)
        self.h_lo, _ = V([CC, T], F16, at=o1)
        self.dacc, _ = V([CC, T], F32, at=o2)

        ring0 = self._off
        self.rg, _ = V([NRS, CC, 128], F16)
        self.ru, _ = V([NRS, CC, 128], F16)
        self.rd, _ = V([NRS, C], F16)
        ring_end = self._off
        self.wvr, _ = V([2, 2, CC, 256], F16, at=ring0)
        self.rw_hi, _ = V([2, CC, 128], F16, at=ring0 + 4096)
        self.rw_lo, _ = V([2, CC, 128], F16, at=ring0 + 4096 + 1024)
        self._off = max(ring_end, ring0 + 4096 + 2048)

        work0 = self._off
        self.q_hi, _ = V([CC, QG], F16)
        self.q_lo, _ = V([CC, QG], F16)
        self.y_hi, yo_ = V([CC, QG], F16)
        self.y_lo, _ = V([CC, QG], F16)
        self._yhi_off = yo_
        self.th_hi, _ = V([CC, SG], F16)
        self.th_lo, _ = V([CC, SG], F16)
        self.att_hi, _ = V([2, QG], F16)
        self.att_lo, _ = V([2, QG], F16)
        work_end = self._off
        self.act_hi, _ = V([BHC, FG], F16, at=work0)
        self.act_lo, _ = V([BHC, FG], F16, at=work0 + 1024)
        self._off = work_end

        self.xstg, _ = V([CC, SG], F32)
        self.scr, _ = V([CC, SG], F32)

        self.vecs, _ = V([T], F32, parts=128)
        self.bcast, _ = V([2, SG], F32)
        self.recipb, _ = V([QG], F32)
        self.sels, _ = V([384], F32)
        self.masks, _ = V([2, QG], F16)
        self.ones, _ = V([2], F16)
        self.onesf, _ = V([2], F32)
        self.bo_sb, _ = V([CC], F32)
        print(f"[arena] {self._off} words = {self._off * 4 / 1024:.1f} KiB/partition")
        assert self._off <= arena_words, self._off

        ps = lambda name, shape: es.enter_context(nc.psum_tensor(name, shape, F32))
        self.big4 = ps("big4", [128, 2048])
        self.big4_d = self.big4.ap().rearrange("p (a b) -> p a b", a=4)
        self.big4_n = self.big4.ap().rearrange("p (a b) -> p a b", a=8)
        self.pA = ps("pA", [128, 512])
        self.pB = ps("pB", [128, 512])
        self.pC = ps("pC", [128, 512])
        self.pD = ps("pD", [128, 512])
        self.sem = {s: es.enter_context(nc.semaphore(s)) for s in self.semnames}

    ROW_M, ROW_V, ROW_R, ROW_DEN = 0, 32, 64, 96

    def vrow(self, r, sl=slice(None)):
        return self.vecs[r:r + 1, sl]

    def build(self):
        nc = self.nc
        with self.es, nc.Block() as block:
            for name in ("sync", "scalar", "vector", "tensor", "gpsimd"):
                def mk(engname):
                    def body(eng):
                        em = Emitter(self, engname, eng)
                        self.schedule(em)
                    return body
                getattr(block, name)(mk(name))
        return nc

    def schedule(self, em):
        self.setup(em)
        em.barrier()
        for s in range(self.n_seq):
            if self.phases == 'faf':
                self.ffn_phase(em, self.xin[s], self.x1d, dbg_idx=0 if s == self.n_seq - 1 else None)
                em.barrier()
                self.attn_phase(em, self.x1d, self.x2d)
                em.barrier()
                self.ffn_phase(em, self.x2d, self.out[s], dbg_idx=1 if s == self.n_seq - 1 else None)
            elif self.phases == 'f':
                self.ffn_phase(em, self.xin[s], self.out[s])
            elif self.phases == 'a':
                self.attn_phase(em, self.xin[s], self.out[s])
            elif self.phases == 'a1':
                self.attn_phase(em, self.xin[s], self.out[s], upto=1)
            elif self.phases == 'fa':
                self.ffn_phase(em, self.xin[s], self.x1d)
                em.barrier()
                self.attn_phase(em, self.x1d, self.out[s])
            em.barrier()
        em.run("gpsimd", lambda g: g.wait_ge(self.sem["st"], em.tok("st")) or g.nop())

    def setup(self, em):
        c = self
        em.run("vector", lambda v: v.memset(c.ones, 1.0), sem="ev", inc=1)
        em.run("vector", lambda v: v.memset(c.onesf, 1.0), sem="ev", inc=1)
        em.run("gpsimd", lambda g: g.iota(c.sels, [[0, 384]], channel_multiplier=1,
               allow_small_or_imprecise_dtypes=True), sem="gp", inc=1)
        em.wait("vector", "gp", em.tok("gp"))
        def mk_sels(v):
            v.tensor_scalar(c.sels[:, 0:128], c.sels[:, 0:128], 0.0, None, ALU.is_equal)
            v.tensor_scalar(c.sels[:, 128:256], c.sels[:, 128:256], 32.0, None, ALU.is_equal)
            v.tensor_scalar(c.sels[:, 256:384], c.sels[:, 256:384], 64.0, None, ALU.is_equal)
            return v.memset(c.vecs[0:96, :], 0.0)
        em.run("vector", mk_sels, sem="ev", inc=1)
        em.run("sync", lambda s: s.dma_start(
            out=c.bo_sb, in_=c.bo.rearrange("(d p) -> p d", p=128),
            allow_slow_non_contiguous=True), sem="ld", inc=16)

        def mk_masks(g):
            ins = None
            for var in range(2):
                ins = g.iota(c.masks[:, var, :], [[1, QG]], base=-128 * var,
                             channel_multiplier=-1,
                             allow_small_or_imprecise_dtypes=True)
            return ins
        em.run("gpsimd", mk_masks, sem="gp", inc=1)
        em.wait("vector", "gp", em.tok("gp"))

        def fin_masks(v):
            return v.tensor_scalar(c.masks[:], c.masks[:], -0.5, None, ALU.is_ge)
        em.run("vector", fin_masks, sem="ev", inc=1)

        stg_i = c.xstg.bitcast(I32)[:, :, 0:128]
        t_prev_st = 0
        for wsrc, wdst in ((c.wgt, c.wg16), (c.wut, c.wu16)):
            for hc in range(HC):
                hsl = slice(hc * 128, hc * 128 + 128)
                em.wait("sync", "st", t_prev_st)
                t_ld = em.run("sync", lambda s, wsrc=wsrc, hsl=hsl: s.dma_start(
                    out=stg_i,
                    in_=wsrc.rearrange("(a p) n -> p a n", p=128)[:, :, hsl]),
                    sem="ld", inc=16)
                em.wait("vector", "ld", t_ld)
                em.wait("vector", "st", t_prev_st)
                t_cv = em.run("vector", lambda v: v.tensor_copy(
                    c.th_hi[:, :, 0:128], stg_i), sem="ev", inc=1)
                em.wait("gpsimd", "ev", t_cv)
                t_prev_st = em.run("gpsimd", lambda g, wdst=wdst, hc=hc: g.dma_start(
                    out=wdst[hc], in_=c.th_hi[:, :, 0:128]), sem="st", inc=16)
        stg_i2 = c.xstg.bitcast(I32)[:, 0:4, :]
        for hc in range(HC):
            em.wait("sync", "st", t_prev_st)
            t_ld = em.run("sync", lambda s, hc=hc: s.dma_start(
                out=stg_i2,
                in_=c.wdt.rearrange("(a p) n -> p a n", p=128)[:, hc, :]
                    .rearrange("p (a b) -> p a b", a=4)),
                sem="ld", inc=16)
            em.wait("vector", "ld", t_ld)
            em.wait("vector", "st", t_prev_st)
            t_cv = em.run("vector", lambda v: v.tensor_copy(
                c.th_hi[:, 0:4, :], stg_i2), sem="ev", inc=1)
            em.wait("gpsimd", "ev", t_cv)
            t_prev_st = em.run("gpsimd", lambda g, hc=hc: g.dma_start(
                out=c.wd16[hc],
                in_=c.th_hi[:, 0:4, :].rearrange("p a b -> p (a b)")),
                sem="st", inc=16)

        for wname, wsrc in (("wq", c.wqt), ("wk", c.wkt), ("wo", c.wot), ("wv", c.wvt)):
            ncols = 128 if wname != "wv" else 256
            for blk in range(C // ncols):
                csl = slice(blk * ncols, (blk + 1) * ncols)
                em.wait("sync", "st", t_prev_st)
                t_ld = em.run("sync", lambda s, wsrc=wsrc, csl=csl, ncols=ncols: s.dma_start(
                    out=c.xstg[:, :, 0:ncols],
                    in_=wsrc.rearrange("(a p) n -> p a n", p=128)[:, :, csl]),
                    sem="ld", inc=16)
                em.wait("scalar", "ld", t_ld)
                em.wait("scalar", "st", t_prev_st)
                wscale = 64.0 if wname == "wo" else 1.0
                t_hi = em.run("scalar", lambda sc, ncols=ncols, wscale=wscale: sc.activation(
                    c.th_hi[:, :, 0:ncols], c.xstg[:, :, 0:ncols], AF.Copy,
                    scale=wscale), sem="sc", inc=1)
                em.wait("vector", "sc", t_hi)
                t_lo = em.run("vector", lambda v, ncols=ncols, wscale=wscale: v.scalar_tensor_tensor(
                    c.th_lo[:, :, 0:ncols], c.xstg[:, :, 0:ncols], wscale,
                    c.th_hi[:, :, 0:ncols], ALU.mult, ALU.subtract), sem="ev", inc=1)
                em.wait("gpsimd", "sc", t_hi)
                em.wait("gpsimd", "ev", t_lo)
                em.run("gpsimd", lambda g, wname=wname, blk=blk, ncols=ncols: g.dma_start(
                    out=c.wsp[f"{wname}_hi"][blk], in_=c.th_hi[:, :, 0:ncols]),
                    sem="st", inc=16)
                t_prev_st = em.run("gpsimd", lambda g, wname=wname, blk=blk, ncols=ncols: g.dma_start(
                    out=c.wsp[f"{wname}_lo"][blk], in_=c.th_lo[:, :, 0:ncols]),
                    sem="st", inc=16)

    def ln_stats(self, em, xsrc):
        c = self
        t_cons = 0
        t_statmm = 0
        t_stat_ev = 0
        for g in range(NSG):
            sl = slice(g * SG, (g + 1) * SG)
            em.wait("sync", "sc", t_cons)
            em.wait("sync", "mm", t_statmm)
            t_ld = em.run("sync", lambda s, sl=sl: s.dma_start(
                out=c.xstg, in_=xsrc.rearrange("(a p) t -> p a t", p=128)[:, :, sl]),
                sem="ld", inc=16)
            em.wait("scalar", "ld", t_ld)
            em.wait("scalar", "mm", t_statmm)
            t_cons = em.run("scalar", lambda sc: sc.activation(
                c.scr, c.xstg, AF.Square), sem="sc", inc=1)
            em.wait("tensor", "sc", t_cons)
            em.wait("tensor", "ld", t_ld)
            em.wait("tensor", "ev", t_stat_ev)
            def mm_stats(t):
                ins = None
                for cc in range(CC):
                    ins = t.matmul(c.pC[0:1, 0:SG], c.onesf[:, 0:1], c.xstg[:, cc, :],
                                   start=(cc == 0), stop=(cc == CC - 1))
                for cc in range(CC):
                    ins = t.matmul(c.pD[0:1, 0:SG], c.onesf[:, 0:1], c.scr[:, cc, :],
                                   start=(cc == 0), stop=(cc == CC - 1))
                return ins
            t_mm = em.run("tensor", mm_stats, sem="mm", inc=1)
            t_statmm = t_mm
            em.wait("vector", "mm", t_mm)
            t_stat_ev = em.run("vector", lambda v, sl=sl: (
                v.tensor_copy(c.vrow(self.ROW_M, sl), c.pC[0:1, 0:SG]),
                v.tensor_copy(c.vrow(self.ROW_V, sl), c.pD[0:1, 0:SG]))[-1],
                sem="ev", inc=1)
        em.state["ln_xstg_free"] = t_cons
        em.state["ln_stats_mm"] = t_statmm
        return t_stat_ev

    def bcast_mr(self, em, g, t_mr, t_free, width=SG):
        c = self
        sl = slice(g * width, g * width + width)
        em.wait("tensor", "ev", max(t_mr, em.state.get("bc_ev", 0),
                                    em.state.get("rb_ev", 0)))
        def mm_bc(t, sl=sl, width=width):
            t.matmul(c.pD[:, 0:width], c.sels[0:33, 0:128], c.vecs[0:33, sl],
                     start=True, stop=False)
            return t.matmul(c.pD[:, 256:256 + width], c.sels[0:33, 128:256],
                            c.vecs[0:33, sl], start=False, stop=True)
        t_bm = em.run("tensor", mm_bc, sem="mm", inc=1)
        em.wait("vector", "mm", t_bm)
        em.wait("vector", "ev", t_free)
        t_ev = em.run("vector", lambda v, width=width: (
            v.tensor_copy(c.bcast[:, 0, 0:width], c.pD[:, 0:width]),
            v.tensor_copy(c.bcast[:, 1, 0:width], c.pD[:, 256:256 + width]))[-1],
            sem="ev", inc=1)
        em.state["bc_ev"] = t_ev
        return t_ev

    def h_slice(self, em, dst_hi, dst_lo, width=SG):
        c = self
        em.wait("vector", "mm", em.state.get("ln_stats_mm", 0))
        def mk1(v):
            b0 = c.bcast[:, 0, 0:width]
            b1 = c.bcast[:, 1, 0:width]
            t = c.scr[:, 7, 0:width]
            v.tensor_scalar(b0, b0, 1.0 / C, None, ALU.mult)
            v.tensor_tensor(t, b0, b0, ALU.mult)
            v.scalar_tensor_tensor(b1, b1, 1.0 / C, t, ALU.mult, ALU.subtract)
            v.tensor_scalar(b1, b1, EPS, None, ALU.add)
            return v.reciprocal(b1, b1)
        t1 = em.run("vector", mk1, sem="ev", inc=1)
        em.wait("scalar", "ev", t1)
        t2 = em.run("scalar", lambda sc, width=width: sc.activation(
            c.bcast[:, 1, 0:width], c.bcast[:, 1, 0:width], AF.Sqrt),
            sem="sc", inc=1)
        em.wait("vector", "sc", t2)
        def mk2(v):
            v.tensor_tensor(c.scr[:, :, 0:width], c.xstg[:, :, 0:width],
                            c.bcast[:, 0:1, 0:width].broadcast_to((128, CC, width)),
                            ALU.subtract)
            v.tensor_tensor(c.scr[:, :, 0:width], c.scr[:, :, 0:width],
                            c.bcast[:, 1:2, 0:width].broadcast_to((128, CC, width)),
                            ALU.mult)
            v.tensor_copy(dst_hi, c.scr[:, :, 0:width])
            return v.tensor_sub(dst_lo, c.scr[:, :, 0:width], dst_hi)
        return em.run("vector", mk2, sem="ev", inc=1)

    def ffn_phase(self, em, xsrc, xdst, dbg_idx=None):
        c = self
        t_mr = self.ln_stats(em, xsrc)
        if self.debug and dbg_idx is not None:
            em.wait("gpsimd", "st", t_mr)
            em.run("gpsimd", lambda g, dbg_idx=dbg_idx: g.dma_start(
                out=self.dbg_vec[dbg_idx], in_=c.vecd[0:2, :]), sem="st", inc=16)
        t_h = 0
        for g in range(NSG):
            sl = slice(g * SG, (g + 1) * SG)
            em.wait("sync", "ev", t_h)
            em.wait("sync", "sc", em.state["ln_xstg_free"])
            em.wait("sync", "mm", em.state["ln_stats_mm"])
            t_ld = em.run("sync", lambda s, sl=sl: s.dma_start(
                out=c.xstg, in_=xsrc.rearrange("(a p) t -> p a t", p=128)[:, :, sl]),
                sem="ld", inc=16)
            t_bc = self.bcast_mr(em, g, t_mr, t_h)
            em.wait("vector", "ld", t_ld)
            t_h = self.h_slice(em, c.h_hi[:, :, sl], c.h_lo[:, :, sl])
            if self.paranoid & 16:
                em.barrier()
        if self.debug and dbg_idx is not None:
            em.wait("gpsimd", "ev", t_h)
            em.run("gpsimd", lambda g, dbg_idx=dbg_idx: g.dma_start(
                out=self.dbg_bc[dbg_idx], in_=c.bcast), sem="st", inc=16)

        slot_free = {}
        t_wl = {}

        def load_slot(slot, hc):
            em.wait("sync", "mm", slot_free.get(slot, 0))
            em.run("sync", lambda s: s.dma_start(
                out=c.rg[:, slot], in_=c.wg16[hc]), sem="wl", inc=16)
            em.run("sync", lambda s: s.dma_start(
                out=c.ru[:, slot], in_=c.wu16[hc]), sem="wl", inc=16)
            return em.run("sync", lambda s: s.dma_start(
                out=c.rd[:, slot], in_=c.wd16[hc]), sem="wl", inc=16)

        for i in range(BHC):
            t_wl[(0, i)] = load_slot(i % NRS, i)

        em.wait("tensor", "ev", t_h)
        t_act_prev = 0
        t_down_mm_prev = 0
        t_down_ev_prev = 0
        for b in range(NB):
            for tg in range(NFG):
                sl = slice(tg * FG, (tg + 1) * FG)
                t_act = {}
                for i in range(BHC):
                    slot = (b * BHC + i) % NRS
                    em.wait("tensor", "wl", t_wl[(b, i)])
                    pg, pu = (c.pA, c.pB) if i % 2 == 0 else (c.pC, c.pD)
                    em.wait("tensor", "ev", t_act.get(i - 2, t_act_prev))
                    def mm_gu(t, slot=slot, sl=sl, pg=pg, pu=pu):
                        ins = None
                        for src, ps_ in ((c.rg, pg), (c.ru, pu)):
                            for p, hsp in enumerate((c.h_hi, c.h_lo)):
                                for cc in range(CC):
                                    ins = t.matmul(ps_[:, 0:FG], src[:, slot, cc, :],
                                                   hsp[:, cc, sl],
                                                   start=(p == 0 and cc == 0),
                                                   stop=(p == 1 and cc == CC - 1))
                        return ins
                    t_mm = em.run("tensor", mm_gu, sem="mm", inc=1)
                    em.wait("scalar", "mm", t_mm)
                    t_si = em.run("scalar", lambda sc, pg=pg: sc.activation(
                        c.scr[:, 0:2, :].rearrange("p a b -> p (a b)"), pg[:, 0:FG],
                        AF.Silu, scale=QSC), sem="sc", inc=1)
                    em.wait("vector", "sc", t_si)
                    em.wait("vector", "mm", t_down_mm_prev)
                    def mk_act(v, i=i, pu=pu):
                        z = c.scr[:, 2:4, :].rearrange("p a b -> p (a b)")
                        silu = c.scr[:, 0:2, :].rearrange("p a b -> p (a b)")
                        v.scalar_tensor_tensor(z, pu[:, 0:FG], QSC, silu, ALU.mult, ALU.mult)
                        v.tensor_copy(c.act_hi[:, i, :], z)
                        return v.tensor_sub(c.act_lo[:, i, :], z, c.act_hi[:, i, :])
                    t_act[i] = em.run("vector", mk_act, sem="ev", inc=1)
                t_act_prev = t_act[BHC - 1]
                for dh in range(2):
                    em.wait("tensor", "ev", t_down_ev_prev)
                    def mm_down_i(t, b=b, dh=dh, i=0, sub=False):
                        ins = None
                        rng = (i,) if sub else range(BHC)
                        for ii in rng:
                            slot = (b * BHC + ii) % NRS
                            for p, asp in enumerate((c.act_hi, c.act_lo)):
                                for dc in range(4):
                                    dsl = slice(dh * 512 + dc * 128,
                                                dh * 512 + (dc + 1) * 128)
                                    ins = t.matmul(c.big4_d[:, dc, 0:FG],
                                                   c.rd[:, slot, dsl], asp[:, ii, :],
                                                   start=(ii == 0 and p == 0),
                                                   stop=(ii == BHC - 1 and p == 1))
                        return ins
                    if dh == 0:
                        for i in range(BHC):
                            em.wait("tensor", "ev", t_act[i])
                            t_down_mm_prev = em.run(
                                "tensor",
                                (lambda ii, bb=b, dd=dh: (
                                    lambda t: mm_down_i(t, b=bb, dh=dd, i=ii, sub=True)))(i),
                                sem="mm", inc=1)
                    else:
                        t_down_mm_prev = em.run("tensor", mm_down_i, sem="mm", inc=1)
                    em.wait("vector", "mm", t_down_mm_prev)
                    def ev_down(v, b=b, dh=dh, sl=sl):
                        dst = c.dacc[:, dh * 4:(dh + 1) * 4, sl]
                        src = c.big4_d[:, :, 0:FG]
                        if b == 0:
                            return v.tensor_copy(dst, src)
                        return v.tensor_tensor(dst, dst, src, ALU.add)
                    t_down_ev_prev = em.run("vector", ev_down, sem="ev", inc=1)
                if tg == NFG - 1:
                    for i in range(BHC):
                        slot_free[(b * BHC + i) % NRS] = t_down_mm_prev
                if self.paranoid & 8:
                    em.barrier()
            if self.paranoid & 4:
                em.barrier()
            if b + 1 < NB:
                for i in range(BHC):
                    hc = (b + 1) * BHC + i
                    t_wl[(b + 1, i)] = load_slot(hc % NRS, hc)

        if self.debug and dbg_idx is not None:
            em.wait("gpsimd", "ev", t_down_ev_prev)
            em.run("gpsimd", lambda g, dbg_idx=dbg_idx: g.dma_start(
                out=self.dbg_dacc[dbg_idx].rearrange("(a p) t -> p a t", p=128),
                in_=c.dacc), sem="st", inc=16)
            em.run("gpsimd", lambda g, dbg_idx=dbg_idx: g.dma_start(
                out=self.dbg_h[dbg_idx].rearrange("(a p) t -> p a t", p=128),
                in_=c.h_hi), sem="st", inc=16)
        t_f = t_down_ev_prev
        t_fin_st = em.tok("st")
        for g in range(NSG):
            sl = slice(g * SG, (g + 1) * SG)
            em.wait("sync", "ev", t_f)
            t_ld = em.run("sync", lambda s, sl=sl: s.dma_start(
                out=c.xstg, in_=xsrc.rearrange("(a p) t -> p a t", p=128)[:, :, sl]),
                sem="ld", inc=16)
            em.wait("vector", "ld", t_ld)
            em.wait("vector", "ev", t_down_ev_prev)
            em.wait("vector", "st", t_fin_st)
            t_f = em.run("vector", lambda v, sl=sl: v.scalar_tensor_tensor(
                c.scr, c.dacc[:, :, sl], 0.5 * QSC, c.xstg, ALU.mult, ALU.add),
                sem="ev", inc=1)
            em.wait("gpsimd", "ev", t_f)
            t_fin_st = em.run("gpsimd", lambda g_, sl=sl: g_.dma_start(
                out=xdst.rearrange("(a p) t -> p a t", p=128)[:, :, sl], in_=c.scr),
                sem="st", inc=16)

    def attn_phase(self, em, xsrc, xdst, upto=99):
        c = self
        t_mr = self.ln_stats(em, xsrc)

        rw_free = {0: 0, 1: 0}
        wv_free = {0: 0, 1: 0}

        def load_wv(slot, dq):
            em.wait("sync", "mm", wv_free[slot])
            em.run("sync", lambda s: s.dma_start(
                out=c.wvr[:, slot, 0], in_=c.wsp["wv_hi"][dq]), sem="wl", inc=16)
            return em.run("sync", lambda s: s.dma_start(
                out=c.wvr[:, slot, 1], in_=c.wsp["wv_lo"][dq]), sem="wl", inc=16)

        def load_rw(slot, wname, blk):
            em.wait("sync", "mm", rw_free[slot])
            em.run("sync", lambda s: s.dma_start(
                out=c.rw_hi[:, slot], in_=c.wsp[f"{wname}_hi"][blk]), sem="wl", inc=16)
            return em.run("sync", lambda s: s.dma_start(
                out=c.rw_lo[:, slot], in_=c.wsp[f"{wname}_lo"][blk]), sem="wl", inc=16)

        first_vb = True
        t_h = 0
        t_th_cons = 0
        for g in range(NSG):
            sl = slice(g * SG, (g + 1) * SG)
            em.wait("sync", "ev", t_h)
            em.wait("sync", "sc", em.state["ln_xstg_free"])
            em.wait("sync", "mm", em.state["ln_stats_mm"])
            t_ld = em.run("sync", lambda s, sl=sl: s.dma_start(
                out=c.xstg, in_=xsrc.rearrange("(a p) t -> p a t", p=128)[:, :, sl]),
                sem="ld", inc=16)
            t_bc = self.bcast_mr(em, g, t_mr, t_h)
            em.wait("vector", "ld", t_ld)
            em.wait("vector", "mm", t_th_cons)
            t_h = self.h_slice(em, c.th_hi, c.th_lo)
            em.wait("tensor", "ev", t_h)
            rw_tok = {0: load_rw(0, "wk", 0), 1: load_rw(1, "wk", 1)}
            t_evk = {}
            for dkc in range(CC):
                slot = dkc % 2
                em.wait("tensor", "wl", rw_tok[dkc])
                ps_ = c.pA if dkc % 2 == 0 else c.pB
                em.wait("tensor", "ev", t_evk.get(dkc - 2, 0))
                def mm_k(t, slot=slot, ps_=ps_):
                    ins = None
                    for p, (whalf, hsp) in enumerate((
                            (c.rw_hi, c.th_hi), (c.rw_hi, c.th_lo), (c.rw_lo, c.th_hi))):
                        for cc in range(CC):
                            ins = t.matmul(ps_[:, 0:SG], whalf[:, slot, cc, :],
                                           hsp[:, cc, :],
                                           start=(p == 0 and cc == 0),
                                           stop=(p == 2 and cc == CC - 1))
                    return ins
                t_mm = em.run("tensor", mm_k, sem="mm", inc=1)
                rw_free[slot] = t_mm
                if dkc + 2 < CC:
                    rw_tok[dkc + 2] = load_rw(slot, "wk", dkc + 2)
                em.wait("scalar", "mm", t_mm)
                t_hi = em.run("scalar", lambda sc, dkc=dkc, sl=sl, ps_=ps_: sc.activation(
                    c.k_hi[:, dkc, sl], ps_[:, 0:SG], AF.Copy, scale=0.1),
                    sem="sc", inc=1)
                em.wait("vector", "sc", t_hi)
                def ev_k(v, dkc=dkc, sl=sl, ps_=ps_):
                    return v.scalar_tensor_tensor(c.k_lo[:, dkc, sl], ps_[:, 0:SG], 0.1,
                                                  c.k_hi[:, dkc, sl], ALU.mult, ALU.subtract)
                t_evk[dkc] = em.run("vector", ev_k, sem="ev", inc=1)
            t_th_cons = em.tok("mm")
            t_evv = {}
            wv_tok = {0: load_wv(0, 0), 1: load_wv(1, 1)}
            for dq in range(4):
                wslot = dq % 2
                for ti in range(2):
                    tt = g * 2 + ti
                    idx = dq * 2 + ti
                    ps_ = c.pA if idx % 2 == 0 else c.pB
                    em.wait("tensor", "wl", wv_tok[dq])
                    em.wait("tensor", "ev", t_evv.get(idx - 2, t_evk[CC - 1]))
                    def mm_v(t, ti=ti, wslot=wslot, ps_=ps_):
                        ins = None
                        for p, (hsp, wh) in enumerate(((c.th_hi, 0), (c.th_lo, 0),
                                                       (c.th_hi, 1))):
                            for cc in range(CC):
                                ins = t.matmul(ps_[:, 0:256],
                                               hsp[:, cc, ti * 128:(ti + 1) * 128],
                                               c.wvr[:, wslot, wh, cc, :],
                                               start=(p == 0 and cc == 0),
                                               stop=(p == 2 and cc == CC - 1))
                        return ins
                    t_mm = em.run("tensor", mm_v, sem="mm", inc=1)
                    wv_free[wslot] = t_mm
                    if ti == 1 and dq + 2 < 4:
                        wv_tok[dq + 2] = load_wv(wslot, dq + 2)
                    em.wait("scalar", "mm", t_mm)
                    t_hi = em.run("scalar", lambda sc, tt=tt, dq=dq, ps_=ps_: sc.copy(
                        c.v_hi[:, tt, dq * 256:(dq + 1) * 256], ps_[:, 0:256]),
                        sem="sc", inc=1)
                    em.wait("vector", "sc", t_hi)
                    def ev_v(v, tt=tt, dq=dq, ps_=ps_):
                        dsl = slice(dq * 256, (dq + 1) * 256)
                        return v.tensor_sub(c.v_lo[:, tt, dsl], ps_[:, 0:256],
                                            c.v_hi[:, tt, dsl])
                    t_evv[idx] = em.run("vector", ev_v, sem="ev", inc=1)
            t_th_cons = em.tok("mm")
            if self.paranoid & 2:
                em.barrier()
        t_vrem = em.tok("ev")
        if upto == 1:
            em.wait("gpsimd", "ev", t_vrem)
            flat = xdst.rearrange("(a p) t -> p a t", p=128)
            for i in range(4):
                src = self.arena[0:128, 8192 * i:8192 * i + 4096].rearrange(
                    "p (a t) -> p a t", a=2)
                em.run("gpsimd", lambda g, i=i, src=src: g.dma_start(
                    out=flat[:, 2 * i:2 * i + 2, :], in_=src), sem="st", inc=16)
            return
        if self.debug:
            em.wait("gpsimd", "ev", t_vrem)
            em.run("gpsimd", lambda g: g.dma_start(out=self.dbg_k, in_=c.k_hi),
                   sem="st", inc=16)
            em.run("gpsimd", lambda g: g.dma_start(out=self.dbg_v, in_=c.v_hi),
                   sem="st", inc=16)

        t_q_cons = 0
        t_y_cons = 0
        t_stg_cons = 0
        t_att = {}
        t_av = {}
        t_den_prev = 0
        t_y_prev = 0
        t_o_st = em.tok("st")
        for qg in range(NQG):
            sl = slice(qg * QG, (qg + 1) * QG)
            kb_max = 2 * qg + 1
            em.wait("sync", "ev", max(t_stg_cons, t_h))
            t_ld = em.run("sync", lambda s, sl=sl: s.dma_start(
                out=c.xstg, in_=xsrc.rearrange("(a p) t -> p a t", p=128)[:, :, sl]),
                sem="ld", inc=16)
            t_bc = self.bcast_mr(em, qg, t_mr, max(t_stg_cons, t_h))
            em.wait("vector", "ld", t_ld)
            em.wait("vector", "mm", t_th_cons)
            em.wait("vector", "st", t_o_st)
            t_h = self.h_slice(em, c.th_hi, c.th_lo)
            em.wait("tensor", "ev", t_h)
            rwq = {0: load_rw(0, "wq", 0), 1: load_rw(1, "wq", 1)}
            t_evq = {}
            for dqc in range(CC):
                slot = dqc % 2
                em.wait("tensor", "wl", rwq[dqc])
                ps_ = c.pA if dqc % 2 == 0 else c.pB
                em.wait("tensor", "ev", t_evq.get(dqc - 2, 0))
                def mm_q(t, slot=slot, ps_=ps_):
                    ins = None
                    for p, (whalf, hsp) in enumerate((
                            (c.rw_hi, c.th_hi), (c.rw_hi, c.th_lo), (c.rw_lo, c.th_hi))):
                        for cc in range(CC):
                            ins = t.matmul(ps_[:, 0:QG], whalf[:, slot, cc, :],
                                           hsp[:, cc, :],
                                           start=(p == 0 and cc == 0),
                                           stop=(p == 2 and cc == CC - 1))
                    return ins
                t_mm = em.run("tensor", mm_q, sem="mm", inc=1)
                rw_free[slot] = t_mm
                if dqc + 2 < CC:
                    rwq[dqc + 2] = load_rw(slot, "wq", dqc + 2)
                em.wait("scalar", "mm", t_mm)
                em.wait("scalar", "mm", t_q_cons)
                t_hi = em.run("scalar", lambda sc, dqc=dqc, ps_=ps_: sc.activation(
                    c.q_hi[:, dqc, :], ps_[:, 0:QG], AF.Copy, scale=0.1),
                    sem="sc", inc=1)
                em.wait("vector", "sc", t_hi)
                def ev_q(v, dqc=dqc, ps_=ps_):
                    return v.scalar_tensor_tensor(c.q_lo[:, dqc, :], ps_[:, 0:QG], 0.1,
                                                  c.q_hi[:, dqc, :], ALU.mult, ALU.subtract)
                t_evq[dqc] = em.run("vector", ev_q, sem="ev", inc=1)
            t_th_cons = em.tok("mm")
            if self.vmode & 32:
                em.barrier()
            em.wait("tensor", "ev", max(t_evq[CC - 1], t_y_prev, t_den_prev, t_vrem))
            av_fns = {}
            prev_kb = None
            for kb in range(kb_max + 1):
                ps_ = c.pA if kb % 2 == 0 else c.pB
                abuf = kb % 2
                em.wait("tensor", "ev", t_att.get(kb - 2, 0))
                def mm_s(t, kb=kb, ps_=ps_):
                    ins = None
                    for p, (ksp, qsp) in enumerate((
                            (c.k_hi, c.q_hi), (c.k_hi, c.q_lo), (c.k_lo, c.q_hi))):
                        for dkc in range(CC):
                            ins = t.matmul(ps_[:, 0:QG],
                                           ksp[:, dkc, kb * 128:(kb + 1) * 128],
                                           qsp[:, dkc, :],
                                           start=(p == 0 and dkc == 0),
                                           stop=(p == 2 and dkc == CC - 1))
                    return ins
                t_mm = em.run("tensor", mm_s, sem="mm", inc=1)
                if prev_kb is not None:
                    em.wait("tensor", "ev", t_att[prev_kb])
                    t_av[prev_kb] = em.run("tensor", av_fns[prev_kb], sem="mm", inc=1)
                em.wait("vector", "mm", max(t_mm, t_av.get(kb - 2, 0)))
                if kb >= 2 * qg:
                    var = kb - 2 * qg
                    def ev_att(v, ps_=ps_, abuf=abuf, var=var):
                        z = c.scr[:, 0, 0:QG]
                        v.tensor_scalar(z, ps_[:, 0:QG], 0.01, 1.0, ALU.mult, ALU.add)
                        v.tensor_tensor(z, z, c.masks[:, var, :], ALU.mult)
                        v.tensor_scalar(c.att_hi[:, abuf, :], z, -1.0, None, ALU.add)
                        return v.scalar_tensor_tensor(c.att_lo[:, abuf, :], z, -1.0,
                                                      c.att_hi[:, abuf, :],
                                                      ALU.add, ALU.subtract)
                else:
                    def ev_att(v, ps_=ps_, abuf=abuf):
                        v.tensor_scalar(c.att_hi[:, abuf, :], ps_[:, 0:QG], 0.01,
                                        None, ALU.mult)
                        return v.scalar_tensor_tensor(c.att_lo[:, abuf, :], ps_[:, 0:QG],
                                                      0.01, c.att_hi[:, abuf, :],
                                                      ALU.mult, ALU.subtract)
                t_att[kb] = em.run("vector", ev_att, sem="ev", inc=1)
                last = (kb == kb_max)
                final = (kb == TT - 1)
                def mm_av(t, kb=kb, abuf=abuf, first=(kb == 0), last=last, final=final):
                    for k_, asp in enumerate((c.att_hi, c.att_lo)):
                        t.matmul(c.pC[0:1, 0:QG], c.ones[:, 0:1], asp[:, abuf, :],
                                 start=(first and k_ == 0), stop=(last and k_ == 1))
                    ins = None
                    for p, (vsp, asp) in enumerate((
                            (c.v_hi, c.att_hi), (c.v_hi, c.att_lo), (c.v_lo, c.att_hi))):
                        for dc in range(CC):
                            ins = t.matmul(c.big4_n[:, dc, :],
                                           vsp[:, kb, dc * 128:(dc + 1) * 128],
                                           asp[:, abuf, :],
                                           start=(first and p == 0 and dc % 2 == 0),
                                           stop=(final and p == 2 and dc % 2 == 1))
                    return ins
                av_fns[kb] = mm_av
                prev_kb = kb
            em.wait("tensor", "ev", t_att[kb_max])
            t_av[kb_max] = em.run("tensor", av_fns[kb_max], sem="mm", inc=1)
            t_corr = t_av[kb_max]
            if kb_max < TT - 1:
                em.wait("vector", "mm", t_av[kb_max - 1])
                t_no = em.run("vector", lambda v: v.memset(c.att_hi[:, 0, :], -1.0),
                              sem="ev", inc=1)
                em.wait("tensor", "ev", t_no)
                def mm_corr(t, kb_max=kb_max):
                    ins = None
                    for ckb in range(kb_max + 1, TT):
                        for p, vsp in enumerate((c.v_hi, c.v_lo)):
                            for dc in range(CC):
                                ins = t.matmul(c.big4_n[:, dc, :],
                                               vsp[:, ckb, dc * 128:(dc + 1) * 128],
                                               c.att_hi[:, 0, :],
                                               start=False,
                                               stop=(ckb == TT - 1 and p == 1 and dc % 2 == 1))
                    return ins
                t_corr = em.run("tensor", mm_corr, sem="mm", inc=1)
            t_q_cons = em.tok("mm")
            em.wait("vector", "mm", t_corr)
            def fin_den(v, kb_max=kb_max):
                const = -(T - 128 * (kb_max + 1)) + EPS
                v.tensor_scalar(c.vecs[64:65, 0:QG], c.pC[0:1, 0:QG],
                                const, None, ALU.add)
                v.reciprocal(c.vecs[64:65, 0:QG], c.vecs[64:65, 0:QG])
                return v.tensor_scalar(c.vecs[64:65, 0:QG], c.vecs[64:65, 0:QG],
                                       1.0 / 64.0, None, ALU.mult)
            t_den = em.run("vector", fin_den, sem="ev", inc=1)
            t_den_prev = t_den
            em.wait("tensor", "ev", max(t_den, em.state.get("rb_ev", 0)))
            def mm_rb(t):
                return t.matmul(c.pD[:, 0:QG], c.sels[0:65, 256:384],
                                c.vecs[0:65, 0:QG], start=True, stop=True)
            t_rbm = em.run("tensor", mm_rb, sem="mm", inc=1)
            em.wait("vector", "mm", t_rbm)
            em.wait("vector", "ev", t_y_prev)
            t_rb = em.run("vector", lambda v: v.tensor_copy(c.recipb, c.pD[:, 0:QG]),
                          sem="ev", inc=1)
            em.state["rb_ev"] = t_rb
            em.wait("vector", "mm", t_y_cons)
            def ev_y(v, kb_max=kb_max):
                ins = None
                for dc in range(CC):
                    z = c.scr[:, 1, 0:QG]
                    v.tensor_tensor(z, c.big4_n[:, dc, :], c.recipb, ALU.mult)
                    v.tensor_copy(c.y_hi[:, dc, :], z)
                    ins = v.tensor_sub(c.y_lo[:, dc, :], z, c.y_hi[:, dc, :])
                return ins
            t_y = em.run("vector", ev_y, sem="ev", inc=1)
            t_y_prev = t_y
            if self.vmode & 16:
                em.wait("gpsimd", "ev", t_y)
                yw = self.arena[0:128, self._yhi_off:self._yhi_off + 2048].rearrange(
                    "p (a t) -> p a t", a=1)
                em.run("gpsimd", lambda g_, qg=qg, yw=yw: g_.dma_start(
                    out=xdst.rearrange("(a p) t -> p a t", p=128)[:, qg:qg + 1, :],
                    in_=yw), sem="st", inc=16)
                em.barrier()
                t_y_cons = em.tok("mm")
                t_stg_cons = t_y
                t_o_st = em.tok("st")
                continue
            if self.debug:
                em.wait("gpsimd", "ev", t_y)
                em.run("gpsimd", lambda g, qg=qg: g.dma_start(
                    out=self.dbg_y[qg], in_=c.y_hi), sem="st", inc=16)
            em.wait("tensor", "ev", t_y)
            rwo = {0: load_rw(0, "wo", 0), 1: load_rw(1, "wo", 1)}
            t_evo = {}
            for doc in range(CC):
                slot = doc % 2
                em.wait("tensor", "wl", rwo[doc])
                ps_ = c.pA if doc % 2 == 0 else c.pB
                em.wait("tensor", "ev", t_evo.get(doc - 2, 0))
                def mm_o(t, slot=slot, ps_=ps_):
                    ins = None
                    for p, (whalf, ysp) in enumerate((
                            (c.rw_hi, c.y_hi), (c.rw_hi, c.y_lo), (c.rw_lo, c.y_hi))):
                        for cc in range(CC):
                            ins = t.matmul(ps_[:, 0:QG], whalf[:, slot, cc, :],
                                           ysp[:, cc, :],
                                           start=(p == 0 and cc == 0),
                                           stop=(p == 2 and cc == CC - 1))
                    return ins
                t_mm = em.run("tensor", mm_o, sem="mm", inc=1)
                rw_free[slot] = t_mm
                if doc + 2 < CC:
                    rwo[doc + 2] = load_rw(slot, "wo", doc + 2)
                em.wait("vector", "mm", t_mm)
                em.wait("vector", "st", t_o_st)
                def ev_o(v, doc=doc, ps_=ps_):
                    return v.scalar_tensor_tensor(c.scr[:, doc, :], ps_[:, 0:QG],
                                                  c.bo_sb[:, doc:doc + 1],
                                                  c.xstg[:, doc, :], ALU.add, ALU.add)
                t_evo[doc] = em.run("vector", ev_o, sem="ev", inc=1)
            t_y_cons = em.tok("mm")
            t_stg_cons = t_evo[CC - 1]
            em.wait("gpsimd", "ev", t_evo[CC - 1])
            t_o_st = em.run("gpsimd", lambda g_, sl=sl: g_.dma_start(
                out=xdst.rearrange("(a p) t -> p a t", p=128)[:, :, sl], in_=c.scr),
                sem="st", inc=16)
            if self.paranoid & 1:
                em.barrier()


def build_nc(n_seq=4, debug=False):
    nc = bass.Bass("TRN2", target_bir_lowering=False)
    core = Core(nc, n_seq=n_seq, debug=debug)
    return core.build()




def kernel(L0_attn_x, L0_attn_Wq, L0_attn_Wk, L0_attn_Wv, L0_attn_Wo, L0_attn_bo,
           L0_ffn_w_gate, L0_ffn_w_up, L0_ffn_w_down):
    n_cores = 8
    n_seq_per_core = 4
    x = np.asarray(L0_attn_x)
    B = x.shape[0]
    assert B == n_cores * n_seq_per_core

    xT = np.ascontiguousarray(x.transpose(0, 2, 1))
    wqt = np.ascontiguousarray(np.asarray(L0_attn_Wq).T)
    wkt = np.ascontiguousarray(np.asarray(L0_attn_Wk).T)
    wvt = np.ascontiguousarray(np.asarray(L0_attn_Wv).T)
    wot = np.ascontiguousarray(np.asarray(L0_attn_Wo).T)
    bo = np.ascontiguousarray(np.asarray(L0_attn_bo))
    wgt = np.ascontiguousarray(np.asarray(L0_ffn_w_gate).T)
    wut = np.ascontiguousarray(np.asarray(L0_ffn_w_up).T)
    wdt = np.ascontiguousarray(np.asarray(L0_ffn_w_down).T)

    nc = bass.Bass("TRN2", target_bir_lowering=False)
    core = Core(nc, n_seq=n_seq_per_core)
    nc = core.build()

    in_maps = []
    for ci in range(n_cores):
        in_maps.append({
            "xT": xT[ci * n_seq_per_core:(ci + 1) * n_seq_per_core],
            "wqt": wqt, "wkt": wkt, "wvt": wvt, "wot": wot, "bo": bo,
            "wgt": wgt, "wut": wut, "wdt": wdt,
        })
    res = run_bass_kernel_spmd(nc, in_maps, core_ids=list(range(n_cores)))
    outs = [r["outT"] for r in res.results]
    out = np.concatenate(outs, axis=0).transpose(0, 2, 1)
    return np.ascontiguousarray(out.astype(np.float32))


# version 07
# speedup vs baseline: 1.0681x; 1.0327x over previous
from contextlib import ExitStack

import numpy as np
import concourse.bass as bass
import concourse.mybir as mybir
from concourse.bass_utils import run_bass_kernel_spmd

F32 = mybir.dt.float32
F16 = mybir.dt.float16
I32 = mybir.dt.int32
AF = mybir.ActivationFunctionType
ALU = mybir.AluOpType

C = 1024
H = 3072
T = 2048
CC = C // 128
HC = H // 128
TT = T // 128
QG = 256
NQG = T // QG
FG = 512
NFG = T // FG
SG = 256
NSG = T // SG
BHC = 3
NB = HC // BHC
NRS = 4
EPS = 1e-5
QSC = 1.0 / 127.0


class Emitter:

    def __init__(self, core, active, eng):
        self.core = core
        self.active = active
        self.eng = eng
        self.cnt = dict.fromkeys(core.semnames, 0)
        self.state = {}

    def run(self, engine, fn, sem=None, inc=0):
        if sem is not None:
            self.cnt[sem] += inc
        if self.active == engine:
            ins = fn(self.eng)
            if sem is not None:
                assert ins is not None, f"fn on {engine} must return last instruction"
                ins.then_inc(self.core.sem[sem], inc)
        return self.cnt[sem] if sem is not None else None

    def wait(self, engine, sem, val):
        if val is None or val <= 0:
            return
        if self.active == engine:
            self.eng.wait_ge(self.core.sem[sem], val)

    def tok(self, sem):
        return self.cnt[sem]

    def barrier(self):
        for sem in self.core.semnames:
            v = self.cnt[sem]
            if v > 0:
                for e in ("sync", "scalar", "vector", "tensor", "gpsimd"):
                    self.wait(e, sem, v)


class Core:
    semnames = ["ld", "wl", "bc", "mm", "ev", "sc", "gp", "st"]

    def __init__(self, nc, n_seq=4, arena_words=53180, debug=False, paranoid=0, phases='faf', vmode=0):
        self.nc = nc
        self.n_seq = n_seq
        self.paranoid = paranoid
        self.phases = phases
        self.vmode = vmode
        es = ExitStack()
        self.es = es
        d = nc.dram_tensor

        self.xin = d("xT", [n_seq, C, T], F32, kind="ExternalInput").ap()
        self.wqt = d("wqt", [C, C], F32, kind="ExternalInput").ap()
        self.wkt = d("wkt", [C, C], F32, kind="ExternalInput").ap()
        self.wvt = d("wvt", [C, C], F32, kind="ExternalInput").ap()
        self.wot = d("wot", [C, C], F32, kind="ExternalInput").ap()
        self.bo = d("bo", [C], F32, kind="ExternalInput").ap()
        self.wgt = d("wgt", [C, H], I32, kind="ExternalInput").ap()
        self.wut = d("wut", [C, H], I32, kind="ExternalInput").ap()
        self.wdt = d("wdt", [H, C], I32, kind="ExternalInput").ap()
        self.out = d("outT", [n_seq, C, T], F32, kind="ExternalOutput").ap()

        dbg = "ExternalOutput" if debug else "Internal"
        self.debug = debug
        self.x1d = d("x1_dram", [C, T], F32, kind=dbg).ap()
        self.x2d = d("x2_dram", [C, T], F32, kind=dbg).ap()
        if debug:
            self.dbg_dacc = d("dbg_dacc", [2, C, T], F32, kind=dbg).ap()
            self.dbg_h = d("dbg_h", [2, C, T], F16, kind=dbg).ap()
            self.dbg_vec = d("dbg_vec", [2, 2, T], F32, kind=dbg).ap()
            self.dbg_bc = d("dbg_bc", [2, 128, 2, SG], F32, kind=dbg).ap()
            self.dbg_k = d("dbg_k", [128, CC, T], F16, kind=dbg).ap()
            self.dbg_v = d("dbg_v", [128, TT, C], F16, kind=dbg).ap()
            self.dbg_den = d("dbg_den", [NQG, QG], F32, kind=dbg).ap()
            self.dbg_y = d("dbg_y", [NQG, 128, CC, QG], F16, kind=dbg).ap()
        self.vecd = d("vec_dram", [8, T], F32).ap()
        self.wg16 = d("wg16_dram", [HC, 128, CC, 128], F16).ap()
        self.wu16 = d("wu16_dram", [HC, 128, CC, 128], F16).ap()
        self.wd16 = d("wd16_dram", [HC, 128, C], F16).ap()
        self.wsp = {}
        for w in ("wq", "wk", "wo"):
            for h in ("hi", "lo"):
                self.wsp[f"{w}_{h}"] = d(f"{w}_{h}_dram", [CC, 128, CC, 128], F16).ap()
        for h in ("hi", "lo"):
            self.wsp[f"wv_{h}"] = d(f"wv_{h}_dram", [4, 128, CC, 256], F16).ap()

        self.arena = es.enter_context(nc.sbuf_tensor("arena", [128, arena_words], F32))
        self._off = 0

        def V(shape, dt=F32, parts=128, at=None):
            n = 1
            for s in shape:
                n *= s
            words = (n + 1) // 2 if dt == F16 else n
            words = (words + 1) & ~1
            off = self._off if at is None else at
            if at is None:
                self._off += words
            ap = self.arena[0:parts, off:off + words]
            ap = ap.bitcast(F16)[:, 0:n] if dt == F16 else ap[:, 0:n]
            if len(shape) > 1:
                letters = list("abcdef")[: len(shape)]
                ap = ap.rearrange(
                    f"p ({' '.join(letters)}) -> p {' '.join(letters)}",
                    **{letters[i]: shape[i] for i in range(len(shape))})
            return ap, off

        self.k_hi, o0 = V([CC, T], F16)
        self.k_lo, o1 = V([CC, T], F16)
        self.v_hi, o2 = V([TT, C], F16)
        self.v_lo, o3 = V([TT, C], F16)
        self.h_hi, _ = V([CC, T], F16, at=o0)
        self.h_lo, _ = V([CC, T], F16, at=o1)
        self.dacc, _ = V([CC, T], F32, at=o2)

        ring0 = self._off
        self.rg, _ = V([NRS, CC, 128], F16)
        self.ru, _ = V([NRS, CC, 128], F16)
        self.rd, _ = V([NRS, C], F16)
        ring_end = self._off
        self.wvr, _ = V([2, 2, CC, 256], F16, at=ring0)
        self.rw_hi, _ = V([2, CC, 128], F16, at=ring0 + 4096)
        self.rw_lo, _ = V([2, CC, 128], F16, at=ring0 + 4096 + 1024)
        self._off = max(ring_end, ring0 + 4096 + 2048)

        work0 = self._off
        self.q_hi, _ = V([CC, QG], F16)
        self.q_lo, _ = V([CC, QG], F16)
        self.y_hi, yo_ = V([CC, QG], F16)
        self.y_lo, _ = V([CC, QG], F16)
        self._yhi_off = yo_
        self.th_hi, _ = V([CC, SG], F16)
        self.th_lo, _ = V([CC, SG], F16)
        self.att_hi, _ = V([2, QG], F16)
        self.att_lo, _ = V([2, QG], F16)
        work_end = self._off
        self.act_hi, _ = V([BHC, FG], F16, at=work0)
        self.act_lo, _ = V([BHC, FG], F16, at=work0 + 1024)
        self._off = work_end

        self.xstg, _ = V([CC, SG], F32)
        self.scr, _ = V([CC, SG], F32)

        self.vecs, _ = V([T], F32, parts=128)
        self.bcast, _ = V([2, SG], F32)
        self.recipb, _ = V([QG], F32)
        self.sels, _ = V([384], F32)
        self.masks, _ = V([2, QG], F16)
        self.ones, _ = V([2], F16)
        self.onesf, _ = V([2], F32)
        self.bo_sb, _ = V([CC], F32)
        print(f"[arena] {self._off} words = {self._off * 4 / 1024:.1f} KiB/partition")
        assert self._off <= arena_words, self._off

        ps = lambda name, shape: es.enter_context(nc.psum_tensor(name, shape, F32))
        self.big4 = ps("big4", [128, 2048])
        self.big4_d = self.big4.ap().rearrange("p (a b) -> p a b", a=4)
        self.big4_n = self.big4.ap().rearrange("p (a b) -> p a b", a=8)
        self.pA = ps("pA", [128, 512])
        self.pB = ps("pB", [128, 512])
        self.pC = ps("pC", [128, 512])
        self.pD = ps("pD", [128, 512])
        self.sem = {s: es.enter_context(nc.semaphore(s)) for s in self.semnames}

    ROW_M, ROW_V, ROW_R, ROW_DEN = 0, 32, 64, 96

    def vrow(self, r, sl=slice(None)):
        return self.vecs[r:r + 1, sl]

    def build(self):
        nc = self.nc
        with self.es, nc.Block() as block:
            for name in ("sync", "scalar", "vector", "tensor", "gpsimd"):
                def mk(engname):
                    def body(eng):
                        em = Emitter(self, engname, eng)
                        self.schedule(em)
                    return body
                getattr(block, name)(mk(name))
        return nc

    def schedule(self, em):
        self.setup(em)
        em.barrier()
        for s in range(self.n_seq):
            if self.phases == 'faf':
                self.ffn_phase(em, self.xin[s], self.x1d, dbg_idx=0 if s == self.n_seq - 1 else None)
                em.barrier()
                self.attn_phase(em, self.x1d, self.x2d)
                em.barrier()
                self.ffn_phase(em, self.x2d, self.out[s], dbg_idx=1 if s == self.n_seq - 1 else None)
            elif self.phases == 'f':
                self.ffn_phase(em, self.xin[s], self.out[s])
            elif self.phases == 'a':
                self.attn_phase(em, self.xin[s], self.out[s])
            elif self.phases == 'a1':
                self.attn_phase(em, self.xin[s], self.out[s], upto=1)
            elif self.phases == 'fa':
                self.ffn_phase(em, self.xin[s], self.x1d)
                em.barrier()
                self.attn_phase(em, self.x1d, self.out[s])
            em.barrier()
        em.run("gpsimd", lambda g: g.wait_ge(self.sem["st"], em.tok("st")) or g.nop())

    def setup(self, em):
        c = self
        em.run("vector", lambda v: v.memset(c.ones, 1.0), sem="ev", inc=1)
        em.run("vector", lambda v: v.memset(c.onesf, 1.0), sem="ev", inc=1)
        em.run("gpsimd", lambda g: g.iota(c.sels, [[0, 384]], channel_multiplier=1,
               allow_small_or_imprecise_dtypes=True), sem="gp", inc=1)
        em.wait("vector", "gp", em.tok("gp"))
        def mk_sels(v):
            v.tensor_scalar(c.sels[:, 0:128], c.sels[:, 0:128], 0.0, None, ALU.is_equal)
            v.tensor_scalar(c.sels[:, 128:256], c.sels[:, 128:256], 32.0, None, ALU.is_equal)
            v.tensor_scalar(c.sels[:, 256:384], c.sels[:, 256:384], 64.0, None, ALU.is_equal)
            return v.memset(c.vecs[0:96, :], 0.0)
        em.run("vector", mk_sels, sem="ev", inc=1)
        em.run("sync", lambda s: s.dma_start(
            out=c.bo_sb, in_=c.bo.rearrange("(d p) -> p d", p=128),
            allow_slow_non_contiguous=True), sem="ld", inc=16)

        def mk_masks(g):
            ins = None
            for var in range(2):
                ins = g.iota(c.masks[:, var, :], [[1, QG]], base=-128 * var,
                             channel_multiplier=-1,
                             allow_small_or_imprecise_dtypes=True)
            return ins
        em.run("gpsimd", mk_masks, sem="gp", inc=1)
        em.wait("vector", "gp", em.tok("gp"))

        def fin_masks(v):
            return v.tensor_scalar(c.masks[:], c.masks[:], -0.5, None, ALU.is_ge)
        em.run("vector", fin_masks, sem="ev", inc=1)

        stgs_i = (c.xstg.bitcast(I32)[:, :, 0:128], c.scr.bitcast(I32)[:, :, 0:128])
        outs_f = (c.th_hi, c.th_lo)
        st_tok = [0, 0]
        idx = 0
        for wsrc, wdst in ((c.wgt, c.wg16), (c.wut, c.wu16)):
            for hc in range(HC):
                p = idx % 2
                idx += 1
                hsl = slice(hc * 128, hc * 128 + 128)
                em.wait("sync", "st", st_tok[p])
                t_ld = em.run("sync", lambda s, wsrc=wsrc, hsl=hsl, p=p: s.dma_start(
                    out=stgs_i[p],
                    in_=wsrc.rearrange("(a p) n -> p a n", p=128)[:, :, hsl]),
                    sem="ld", inc=16)
                em.wait("vector", "ld", t_ld)
                em.wait("vector", "st", st_tok[p])
                t_cv = em.run("vector", lambda v, p=p: v.tensor_copy(
                    outs_f[p][:, :, 0:128], stgs_i[p]), sem="ev", inc=1)
                em.wait("gpsimd", "ev", t_cv)
                st_tok[p] = em.run("gpsimd", lambda g, wdst=wdst, hc=hc, p=p: g.dma_start(
                    out=wdst[hc], in_=outs_f[p][:, :, 0:128]), sem="st", inc=16)
        stgs_i2 = (c.xstg.bitcast(I32)[:, 0:4, :], c.scr.bitcast(I32)[:, 0:4, :])
        for hc in range(HC):
            p = idx % 2
            idx += 1
            em.wait("sync", "st", st_tok[p])
            t_ld = em.run("sync", lambda s, hc=hc, p=p: s.dma_start(
                out=stgs_i2[p],
                in_=c.wdt.rearrange("(a p) n -> p a n", p=128)[:, hc, :]
                    .rearrange("p (a b) -> p a b", a=4)),
                sem="ld", inc=16)
            em.wait("vector", "ld", t_ld)
            em.wait("vector", "st", st_tok[p])
            t_cv = em.run("vector", lambda v, p=p: v.tensor_copy(
                outs_f[p][:, 0:4, :], stgs_i2[p]), sem="ev", inc=1)
            em.wait("gpsimd", "ev", t_cv)
            st_tok[p] = em.run("gpsimd", lambda g, hc=hc, p=p: g.dma_start(
                out=c.wd16[hc],
                in_=outs_f[p][:, 0:4, :].rearrange("p a b -> p (a b)")),
                sem="st", inc=16)

        stgs_f = (c.xstg, c.scr)
        outs_hi = (c.th_hi, c.q_hi)
        outs_lo = (c.th_lo, c.q_lo)
        for wname, wsrc in (("wq", c.wqt), ("wk", c.wkt), ("wo", c.wot), ("wv", c.wvt)):
            ncols = 128 if wname != "wv" else 256
            for blk in range(C // ncols):
                p = idx % 2
                idx += 1
                csl = slice(blk * ncols, (blk + 1) * ncols)
                em.wait("sync", "st", st_tok[p])
                t_ld = em.run("sync", lambda s, wsrc=wsrc, csl=csl, ncols=ncols, p=p: s.dma_start(
                    out=stgs_f[p][:, :, 0:ncols],
                    in_=wsrc.rearrange("(a p) n -> p a n", p=128)[:, :, csl]),
                    sem="ld", inc=16)
                em.wait("scalar", "ld", t_ld)
                em.wait("scalar", "st", st_tok[p])
                wscale = 64.0 if wname == "wo" else 1.0
                t_hi = em.run("scalar", lambda sc, ncols=ncols, wscale=wscale, p=p: sc.activation(
                    outs_hi[p][:, :, 0:ncols], stgs_f[p][:, :, 0:ncols], AF.Copy,
                    scale=wscale), sem="sc", inc=1)
                em.wait("vector", "sc", t_hi)
                t_lo = em.run("vector", lambda v, ncols=ncols, wscale=wscale, p=p: v.scalar_tensor_tensor(
                    outs_lo[p][:, :, 0:ncols], stgs_f[p][:, :, 0:ncols], wscale,
                    outs_hi[p][:, :, 0:ncols], ALU.mult, ALU.subtract), sem="ev", inc=1)
                em.wait("gpsimd", "sc", t_hi)
                em.wait("gpsimd", "ev", t_lo)
                em.run("gpsimd", lambda g, wname=wname, blk=blk, ncols=ncols, p=p: g.dma_start(
                    out=c.wsp[f"{wname}_hi"][blk], in_=outs_hi[p][:, :, 0:ncols]),
                    sem="st", inc=16)
                st_tok[p] = em.run("gpsimd", lambda g, wname=wname, blk=blk, ncols=ncols, p=p: g.dma_start(
                    out=c.wsp[f"{wname}_lo"][blk], in_=outs_lo[p][:, :, 0:ncols]),
                    sem="st", inc=16)

    def ln_stats(self, em, xsrc):
        c = self
        t_cons = 0
        t_statmm = 0
        t_stat_ev = 0
        for g in range(NSG):
            sl = slice(g * SG, (g + 1) * SG)
            em.wait("sync", "sc", t_cons)
            em.wait("sync", "mm", t_statmm)
            t_ld = em.run("sync", lambda s, sl=sl: s.dma_start(
                out=c.xstg, in_=xsrc.rearrange("(a p) t -> p a t", p=128)[:, :, sl]),
                sem="ld", inc=16)
            em.wait("scalar", "ld", t_ld)
            em.wait("scalar", "mm", t_statmm)
            t_cons = em.run("scalar", lambda sc: sc.activation(
                c.scr, c.xstg, AF.Square), sem="sc", inc=1)
            em.wait("tensor", "sc", t_cons)
            em.wait("tensor", "ld", t_ld)
            em.wait("tensor", "ev", t_stat_ev)
            def mm_stats(t):
                ins = None
                for cc in range(CC):
                    ins = t.matmul(c.pC[0:1, 0:SG], c.onesf[:, 0:1], c.xstg[:, cc, :],
                                   start=(cc == 0), stop=(cc == CC - 1))
                for cc in range(CC):
                    ins = t.matmul(c.pD[0:1, 0:SG], c.onesf[:, 0:1], c.scr[:, cc, :],
                                   start=(cc == 0), stop=(cc == CC - 1))
                return ins
            t_mm = em.run("tensor", mm_stats, sem="mm", inc=1)
            t_statmm = t_mm
            em.wait("vector", "mm", t_mm)
            t_stat_ev = em.run("vector", lambda v, sl=sl: (
                v.tensor_copy(c.vrow(self.ROW_M, sl), c.pC[0:1, 0:SG]),
                v.tensor_copy(c.vrow(self.ROW_V, sl), c.pD[0:1, 0:SG]))[-1],
                sem="ev", inc=1)
        em.state["ln_xstg_free"] = t_cons
        em.state["ln_stats_mm"] = t_statmm
        return t_stat_ev

    def bcast_mr(self, em, g, t_mr, t_free, width=SG):
        c = self
        sl = slice(g * width, g * width + width)
        em.wait("tensor", "ev", max(t_mr, em.state.get("bc_ev", 0),
                                    em.state.get("rb_ev", 0)))
        def mm_bc(t, sl=sl, width=width):
            t.matmul(c.pD[:, 0:width], c.sels[0:33, 0:128], c.vecs[0:33, sl],
                     start=True, stop=False)
            return t.matmul(c.pD[:, 256:256 + width], c.sels[0:33, 128:256],
                            c.vecs[0:33, sl], start=False, stop=True)
        t_bm = em.run("tensor", mm_bc, sem="mm", inc=1)
        em.wait("vector", "mm", t_bm)
        em.wait("vector", "ev", t_free)
        t_ev = em.run("vector", lambda v, width=width: (
            v.tensor_copy(c.bcast[:, 0, 0:width], c.pD[:, 0:width]),
            v.tensor_copy(c.bcast[:, 1, 0:width], c.pD[:, 256:256 + width]))[-1],
            sem="ev", inc=1)
        em.state["bc_ev"] = t_ev
        return t_ev

    def h_slice(self, em, dst_hi, dst_lo, width=SG):
        c = self
        em.wait("vector", "mm", em.state.get("ln_stats_mm", 0))
        def mk1(v):
            b0 = c.bcast[:, 0, 0:width]
            b1 = c.bcast[:, 1, 0:width]
            t = c.scr[:, 7, 0:width]
            v.tensor_scalar(b0, b0, 1.0 / C, None, ALU.mult)
            v.tensor_tensor(t, b0, b0, ALU.mult)
            v.scalar_tensor_tensor(b1, b1, 1.0 / C, t, ALU.mult, ALU.subtract)
            v.tensor_scalar(b1, b1, EPS, None, ALU.add)
            return v.reciprocal(b1, b1)
        t1 = em.run("vector", mk1, sem="ev", inc=1)
        em.wait("scalar", "ev", t1)
        t2 = em.run("scalar", lambda sc, width=width: sc.activation(
            c.bcast[:, 1, 0:width], c.bcast[:, 1, 0:width], AF.Sqrt),
            sem="sc", inc=1)
        em.wait("vector", "sc", t2)
        def mk2(v):
            v.tensor_tensor(c.scr[:, :, 0:width], c.xstg[:, :, 0:width],
                            c.bcast[:, 0:1, 0:width].broadcast_to((128, CC, width)),
                            ALU.subtract)
            v.tensor_tensor(c.scr[:, :, 0:width], c.scr[:, :, 0:width],
                            c.bcast[:, 1:2, 0:width].broadcast_to((128, CC, width)),
                            ALU.mult)
            v.tensor_copy(dst_hi, c.scr[:, :, 0:width])
            return v.tensor_sub(dst_lo, c.scr[:, :, 0:width], dst_hi)
        return em.run("vector", mk2, sem="ev", inc=1)

    def ffn_phase(self, em, xsrc, xdst, dbg_idx=None):
        c = self
        t_mr = self.ln_stats(em, xsrc)
        if self.debug and dbg_idx is not None:
            em.wait("gpsimd", "st", t_mr)
            em.run("gpsimd", lambda g, dbg_idx=dbg_idx: g.dma_start(
                out=self.dbg_vec[dbg_idx], in_=c.vecd[0:2, :]), sem="st", inc=16)
        t_h = 0
        for g in range(NSG):
            sl = slice(g * SG, (g + 1) * SG)
            em.wait("sync", "ev", t_h)
            em.wait("sync", "sc", em.state["ln_xstg_free"])
            em.wait("sync", "mm", em.state["ln_stats_mm"])
            t_ld = em.run("sync", lambda s, sl=sl: s.dma_start(
                out=c.xstg, in_=xsrc.rearrange("(a p) t -> p a t", p=128)[:, :, sl]),
                sem="ld", inc=16)
            t_bc = self.bcast_mr(em, g, t_mr, t_h)
            em.wait("vector", "ld", t_ld)
            t_h = self.h_slice(em, c.h_hi[:, :, sl], c.h_lo[:, :, sl])
            if self.paranoid & 16:
                em.barrier()
        if self.debug and dbg_idx is not None:
            em.wait("gpsimd", "ev", t_h)
            em.run("gpsimd", lambda g, dbg_idx=dbg_idx: g.dma_start(
                out=self.dbg_bc[dbg_idx], in_=c.bcast), sem="st", inc=16)

        slot_free = {}
        t_wl = {}

        def load_slot(slot, hc):
            em.wait("sync", "mm", slot_free.get(slot, 0))
            em.run("sync", lambda s: s.dma_start(
                out=c.rg[:, slot], in_=c.wg16[hc]), sem="wl", inc=16)
            em.run("sync", lambda s: s.dma_start(
                out=c.ru[:, slot], in_=c.wu16[hc]), sem="wl", inc=16)
            return em.run("sync", lambda s: s.dma_start(
                out=c.rd[:, slot], in_=c.wd16[hc]), sem="wl", inc=16)

        for i in range(BHC):
            t_wl[(0, i)] = load_slot(i % NRS, i)

        em.wait("tensor", "ev", t_h)
        t_act_prev = 0
        t_down_mm_prev = 0
        t_down_ev_prev = 0
        for b in range(NB):
            for tg in range(NFG):
                sl = slice(tg * FG, (tg + 1) * FG)
                t_act = {}
                for i in range(BHC):
                    slot = (b * BHC + i) % NRS
                    em.wait("tensor", "wl", t_wl[(b, i)])
                    pg, pu = (c.pA, c.pB) if i % 2 == 0 else (c.pC, c.pD)
                    em.wait("tensor", "ev", t_act.get(i - 2, t_act_prev))
                    def mm_gu(t, slot=slot, sl=sl, pg=pg, pu=pu):
                        ins = None
                        for src, ps_ in ((c.rg, pg), (c.ru, pu)):
                            for p, hsp in enumerate((c.h_hi, c.h_lo)):
                                for cc in range(CC):
                                    ins = t.matmul(ps_[:, 0:FG], src[:, slot, cc, :],
                                                   hsp[:, cc, sl],
                                                   start=(p == 0 and cc == 0),
                                                   stop=(p == 1 and cc == CC - 1))
                        return ins
                    t_mm = em.run("tensor", mm_gu, sem="mm", inc=1)
                    em.wait("scalar", "mm", t_mm)
                    t_si = em.run("scalar", lambda sc, pg=pg: sc.activation(
                        c.scr[:, 0:2, :].rearrange("p a b -> p (a b)"), pg[:, 0:FG],
                        AF.Silu, scale=QSC), sem="sc", inc=1)
                    em.wait("vector", "sc", t_si)
                    em.wait("vector", "mm", t_down_mm_prev)
                    def mk_act(v, i=i, pu=pu):
                        z = c.scr[:, 2:4, :].rearrange("p a b -> p (a b)")
                        silu = c.scr[:, 0:2, :].rearrange("p a b -> p (a b)")
                        v.scalar_tensor_tensor(z, pu[:, 0:FG], QSC, silu, ALU.mult, ALU.mult)
                        v.tensor_copy(c.act_hi[:, i, :], z)
                        return v.tensor_sub(c.act_lo[:, i, :], z, c.act_hi[:, i, :])
                    t_act[i] = em.run("vector", mk_act, sem="ev", inc=1)
                t_act_prev = t_act[BHC - 1]
                for dh in range(2):
                    em.wait("tensor", "ev", t_down_ev_prev)
                    def mm_down_i(t, b=b, dh=dh, i=0, sub=False):
                        ins = None
                        rng = (i,) if sub else range(BHC)
                        for ii in rng:
                            slot = (b * BHC + ii) % NRS
                            for p, asp in enumerate((c.act_hi, c.act_lo)):
                                for dc in range(4):
                                    dsl = slice(dh * 512 + dc * 128,
                                                dh * 512 + (dc + 1) * 128)
                                    ins = t.matmul(c.big4_d[:, dc, 0:FG],
                                                   c.rd[:, slot, dsl], asp[:, ii, :],
                                                   start=(ii == 0 and p == 0),
                                                   stop=(ii == BHC - 1 and p == 1))
                        return ins
                    if dh == 0:
                        for i in range(BHC):
                            em.wait("tensor", "ev", t_act[i])
                            t_down_mm_prev = em.run(
                                "tensor",
                                (lambda ii, bb=b, dd=dh: (
                                    lambda t: mm_down_i(t, b=bb, dh=dd, i=ii, sub=True)))(i),
                                sem="mm", inc=1)
                    else:
                        t_down_mm_prev = em.run("tensor", mm_down_i, sem="mm", inc=1)
                    em.wait("vector", "mm", t_down_mm_prev)
                    def ev_down(v, b=b, dh=dh, sl=sl):
                        dst = c.dacc[:, dh * 4:(dh + 1) * 4, sl]
                        src = c.big4_d[:, :, 0:FG]
                        if b == 0:
                            return v.tensor_copy(dst, src)
                        return v.tensor_tensor(dst, dst, src, ALU.add)
                    t_down_ev_prev = em.run("vector", ev_down, sem="ev", inc=1)
                if tg == NFG - 1:
                    for i in range(BHC):
                        slot_free[(b * BHC + i) % NRS] = t_down_mm_prev
                if self.paranoid & 8:
                    em.barrier()
            if self.paranoid & 4:
                em.barrier()
            if b + 1 < NB:
                for i in range(BHC):
                    hc = (b + 1) * BHC + i
                    t_wl[(b + 1, i)] = load_slot(hc % NRS, hc)

        if self.debug and dbg_idx is not None:
            em.wait("gpsimd", "ev", t_down_ev_prev)
            em.run("gpsimd", lambda g, dbg_idx=dbg_idx: g.dma_start(
                out=self.dbg_dacc[dbg_idx].rearrange("(a p) t -> p a t", p=128),
                in_=c.dacc), sem="st", inc=16)
            em.run("gpsimd", lambda g, dbg_idx=dbg_idx: g.dma_start(
                out=self.dbg_h[dbg_idx].rearrange("(a p) t -> p a t", p=128),
                in_=c.h_hi), sem="st", inc=16)
        t_f = t_down_ev_prev
        t_fin_st = em.tok("st")
        for g in range(NSG):
            sl = slice(g * SG, (g + 1) * SG)
            em.wait("sync", "ev", t_f)
            t_ld = em.run("sync", lambda s, sl=sl: s.dma_start(
                out=c.xstg, in_=xsrc.rearrange("(a p) t -> p a t", p=128)[:, :, sl]),
                sem="ld", inc=16)
            em.wait("vector", "ld", t_ld)
            em.wait("vector", "ev", t_down_ev_prev)
            em.wait("vector", "st", t_fin_st)
            t_f = em.run("vector", lambda v, sl=sl: v.scalar_tensor_tensor(
                c.scr, c.dacc[:, :, sl], 0.5 * QSC, c.xstg, ALU.mult, ALU.add),
                sem="ev", inc=1)
            em.wait("gpsimd", "ev", t_f)
            t_fin_st = em.run("gpsimd", lambda g_, sl=sl: g_.dma_start(
                out=xdst.rearrange("(a p) t -> p a t", p=128)[:, :, sl], in_=c.scr),
                sem="st", inc=16)

    def attn_phase(self, em, xsrc, xdst, upto=99):
        c = self
        t_mr = self.ln_stats(em, xsrc)

        rw_free = {0: 0, 1: 0}
        wv_free = {0: 0, 1: 0}

        def load_wv(slot, dq):
            em.wait("sync", "mm", wv_free[slot])
            em.run("sync", lambda s: s.dma_start(
                out=c.wvr[:, slot, 0], in_=c.wsp["wv_hi"][dq]), sem="wl", inc=16)
            return em.run("sync", lambda s: s.dma_start(
                out=c.wvr[:, slot, 1], in_=c.wsp["wv_lo"][dq]), sem="wl", inc=16)

        def load_rw(slot, wname, blk):
            em.wait("sync", "mm", rw_free[slot])
            em.run("sync", lambda s: s.dma_start(
                out=c.rw_hi[:, slot], in_=c.wsp[f"{wname}_hi"][blk]), sem="wl", inc=16)
            return em.run("sync", lambda s: s.dma_start(
                out=c.rw_lo[:, slot], in_=c.wsp[f"{wname}_lo"][blk]), sem="wl", inc=16)

        first_vb = True
        t_h = 0
        t_th_cons = 0
        for g in range(NSG):
            sl = slice(g * SG, (g + 1) * SG)
            em.wait("sync", "ev", t_h)
            em.wait("sync", "sc", em.state["ln_xstg_free"])
            em.wait("sync", "mm", em.state["ln_stats_mm"])
            t_ld = em.run("sync", lambda s, sl=sl: s.dma_start(
                out=c.xstg, in_=xsrc.rearrange("(a p) t -> p a t", p=128)[:, :, sl]),
                sem="ld", inc=16)
            t_bc = self.bcast_mr(em, g, t_mr, t_h)
            em.wait("vector", "ld", t_ld)
            em.wait("vector", "mm", t_th_cons)
            t_h = self.h_slice(em, c.th_hi, c.th_lo)
            em.wait("tensor", "ev", t_h)
            rw_tok = {0: load_rw(0, "wk", 0), 1: load_rw(1, "wk", 1)}
            t_evk = {}
            for dkc in range(CC):
                slot = dkc % 2
                em.wait("tensor", "wl", rw_tok[dkc])
                ps_ = c.pA if dkc % 2 == 0 else c.pB
                em.wait("tensor", "ev", t_evk.get(dkc - 2, 0))
                def mm_k(t, slot=slot, ps_=ps_):
                    ins = None
                    for p, (whalf, hsp) in enumerate((
                            (c.rw_hi, c.th_hi), (c.rw_hi, c.th_lo), (c.rw_lo, c.th_hi))):
                        for cc in range(CC):
                            ins = t.matmul(ps_[:, 0:SG], whalf[:, slot, cc, :],
                                           hsp[:, cc, :],
                                           start=(p == 0 and cc == 0),
                                           stop=(p == 2 and cc == CC - 1))
                    return ins
                t_mm = em.run("tensor", mm_k, sem="mm", inc=1)
                rw_free[slot] = t_mm
                if dkc + 2 < CC:
                    rw_tok[dkc + 2] = load_rw(slot, "wk", dkc + 2)
                em.wait("vector", "mm", t_mm)
                def ev_k(v, dkc=dkc, sl=sl, ps_=ps_):
                    v.tensor_scalar(c.k_hi[:, dkc, sl], ps_[:, 0:SG], 0.1, None, ALU.mult)
                    return v.scalar_tensor_tensor(c.k_lo[:, dkc, sl], ps_[:, 0:SG], 0.1,
                                                  c.k_hi[:, dkc, sl], ALU.mult, ALU.subtract)
                t_evk[dkc] = em.run("vector", ev_k, sem="ev", inc=1)
            t_th_cons = em.tok("mm")
            t_evv = {}
            wv_tok = {0: load_wv(0, 0), 1: load_wv(1, 1)}
            for dq in range(4):
                wslot = dq % 2
                for ti in range(2):
                    tt = g * 2 + ti
                    idx = dq * 2 + ti
                    ps_ = c.pA if idx % 2 == 0 else c.pB
                    em.wait("tensor", "wl", wv_tok[dq])
                    em.wait("tensor", "ev", t_evv.get(idx - 2, t_evk[CC - 1]))
                    def mm_v(t, ti=ti, wslot=wslot, ps_=ps_):
                        ins = None
                        for p, (hsp, wh) in enumerate(((c.th_hi, 0), (c.th_lo, 0),
                                                       (c.th_hi, 1))):
                            for cc in range(CC):
                                ins = t.matmul(ps_[:, 0:256],
                                               hsp[:, cc, ti * 128:(ti + 1) * 128],
                                               c.wvr[:, wslot, wh, cc, :],
                                               start=(p == 0 and cc == 0),
                                               stop=(p == 2 and cc == CC - 1))
                        return ins
                    t_mm = em.run("tensor", mm_v, sem="mm", inc=1)
                    wv_free[wslot] = t_mm
                    if ti == 1 and dq + 2 < 4:
                        wv_tok[dq + 2] = load_wv(wslot, dq + 2)
                    em.wait("vector", "mm", t_mm)
                    def ev_v(v, tt=tt, dq=dq, ps_=ps_):
                        dsl = slice(dq * 256, (dq + 1) * 256)
                        v.tensor_copy(c.v_hi[:, tt, dsl], ps_[:, 0:256])
                        return v.tensor_sub(c.v_lo[:, tt, dsl], ps_[:, 0:256],
                                            c.v_hi[:, tt, dsl])
                    t_evv[idx] = em.run("vector", ev_v, sem="ev", inc=1)
            t_th_cons = em.tok("mm")
            if self.paranoid & 2:
                em.barrier()
        t_vrem = em.tok("ev")
        if upto == 1:
            em.wait("gpsimd", "ev", t_vrem)
            flat = xdst.rearrange("(a p) t -> p a t", p=128)
            for i in range(4):
                src = self.arena[0:128, 8192 * i:8192 * i + 4096].rearrange(
                    "p (a t) -> p a t", a=2)
                em.run("gpsimd", lambda g, i=i, src=src: g.dma_start(
                    out=flat[:, 2 * i:2 * i + 2, :], in_=src), sem="st", inc=16)
            return
        if self.debug:
            em.wait("gpsimd", "ev", t_vrem)
            em.run("gpsimd", lambda g: g.dma_start(out=self.dbg_k, in_=c.k_hi),
                   sem="st", inc=16)
            em.run("gpsimd", lambda g: g.dma_start(out=self.dbg_v, in_=c.v_hi),
                   sem="st", inc=16)

        t_q_cons = 0
        t_y_cons = 0
        t_stg_cons = 0
        t_att = {}
        t_av = {}
        t_den_prev = 0
        t_y_prev = 0
        t_o_st = em.tok("st")
        for qg in range(NQG):
            sl = slice(qg * QG, (qg + 1) * QG)
            kb_max = 2 * qg + 1
            em.wait("sync", "ev", max(t_stg_cons, t_h))
            t_ld = em.run("sync", lambda s, sl=sl: s.dma_start(
                out=c.xstg, in_=xsrc.rearrange("(a p) t -> p a t", p=128)[:, :, sl]),
                sem="ld", inc=16)
            t_bc = self.bcast_mr(em, qg, t_mr, max(t_stg_cons, t_h))
            em.wait("vector", "ld", t_ld)
            em.wait("vector", "mm", t_th_cons)
            em.wait("vector", "st", t_o_st)
            t_h = self.h_slice(em, c.th_hi, c.th_lo)
            em.wait("tensor", "ev", t_h)
            rwq = {0: load_rw(0, "wq", 0), 1: load_rw(1, "wq", 1)}
            t_evq = {}
            for dqc in range(CC):
                slot = dqc % 2
                em.wait("tensor", "wl", rwq[dqc])
                ps_ = c.pA if dqc % 2 == 0 else c.pB
                em.wait("tensor", "ev", t_evq.get(dqc - 2, 0))
                def mm_q(t, slot=slot, ps_=ps_):
                    ins = None
                    for p, (whalf, hsp) in enumerate((
                            (c.rw_hi, c.th_hi), (c.rw_hi, c.th_lo), (c.rw_lo, c.th_hi))):
                        for cc in range(CC):
                            ins = t.matmul(ps_[:, 0:QG], whalf[:, slot, cc, :],
                                           hsp[:, cc, :],
                                           start=(p == 0 and cc == 0),
                                           stop=(p == 2 and cc == CC - 1))
                    return ins
                t_mm = em.run("tensor", mm_q, sem="mm", inc=1)
                rw_free[slot] = t_mm
                if dqc + 2 < CC:
                    rwq[dqc + 2] = load_rw(slot, "wq", dqc + 2)
                em.wait("vector", "mm", t_mm)
                em.wait("vector", "mm", t_q_cons)
                def ev_q(v, dqc=dqc, ps_=ps_):
                    v.tensor_scalar(c.q_hi[:, dqc, :], ps_[:, 0:QG], 0.1, None, ALU.mult)
                    return v.scalar_tensor_tensor(c.q_lo[:, dqc, :], ps_[:, 0:QG], 0.1,
                                                  c.q_hi[:, dqc, :], ALU.mult, ALU.subtract)
                t_evq[dqc] = em.run("vector", ev_q, sem="ev", inc=1)
            t_th_cons = em.tok("mm")
            if self.vmode & 32:
                em.barrier()
            em.wait("tensor", "ev", max(t_evq[CC - 1], t_y_prev, t_den_prev, t_vrem))
            av_fns = {}
            prev_kb = None
            for kb in range(kb_max + 1):
                ps_ = c.pA if kb % 2 == 0 else c.pB
                abuf = kb % 2
                em.wait("tensor", "ev", t_att.get(kb - 2, 0))
                def mm_s(t, kb=kb, ps_=ps_):
                    ins = None
                    for p, (ksp, qsp) in enumerate((
                            (c.k_hi, c.q_hi), (c.k_hi, c.q_lo), (c.k_lo, c.q_hi))):
                        for dkc in range(CC):
                            ins = t.matmul(ps_[:, 0:QG],
                                           ksp[:, dkc, kb * 128:(kb + 1) * 128],
                                           qsp[:, dkc, :],
                                           start=(p == 0 and dkc == 0),
                                           stop=(p == 2 and dkc == CC - 1))
                    return ins
                t_mm = em.run("tensor", mm_s, sem="mm", inc=1)
                if prev_kb is not None:
                    em.wait("tensor", "ev", t_att[prev_kb])
                    t_av[prev_kb] = em.run("tensor", av_fns[prev_kb], sem="mm", inc=1)
                em.wait("vector", "mm", max(t_mm, t_av.get(kb - 2, 0)))
                if kb >= 2 * qg:
                    var = kb - 2 * qg
                    def ev_att(v, ps_=ps_, abuf=abuf, var=var):
                        z = c.scr[:, 0, 0:QG]
                        v.tensor_scalar(z, ps_[:, 0:QG], 0.01, 1.0, ALU.mult, ALU.add)
                        v.tensor_tensor(z, z, c.masks[:, var, :], ALU.mult)
                        v.tensor_scalar(c.att_hi[:, abuf, :], z, -1.0, None, ALU.add)
                        return v.scalar_tensor_tensor(c.att_lo[:, abuf, :], z, -1.0,
                                                      c.att_hi[:, abuf, :],
                                                      ALU.add, ALU.subtract)
                else:
                    def ev_att(v, ps_=ps_, abuf=abuf):
                        v.tensor_scalar(c.att_hi[:, abuf, :], ps_[:, 0:QG], 0.01,
                                        None, ALU.mult)
                        return v.scalar_tensor_tensor(c.att_lo[:, abuf, :], ps_[:, 0:QG],
                                                      0.01, c.att_hi[:, abuf, :],
                                                      ALU.mult, ALU.subtract)
                t_att[kb] = em.run("vector", ev_att, sem="ev", inc=1)
                last = (kb == kb_max)
                final = (kb == TT - 1)
                def mm_av(t, kb=kb, abuf=abuf, first=(kb == 0), last=last, final=final):
                    for k_, asp in enumerate((c.att_hi, c.att_lo)):
                        t.matmul(c.pC[0:1, 0:QG], c.ones[:, 0:1], asp[:, abuf, :],
                                 start=(first and k_ == 0), stop=(last and k_ == 1))
                    ins = None
                    for p, (vsp, asp) in enumerate((
                            (c.v_hi, c.att_hi), (c.v_hi, c.att_lo), (c.v_lo, c.att_hi))):
                        for dc in range(CC):
                            ins = t.matmul(c.big4_n[:, dc, :],
                                           vsp[:, kb, dc * 128:(dc + 1) * 128],
                                           asp[:, abuf, :],
                                           start=(first and p == 0 and dc % 2 == 0),
                                           stop=(final and p == 2 and dc % 2 == 1))
                    return ins
                av_fns[kb] = mm_av
                prev_kb = kb
            em.wait("tensor", "ev", t_att[kb_max])
            t_av[kb_max] = em.run("tensor", av_fns[kb_max], sem="mm", inc=1)
            t_corr = t_av[kb_max]
            if kb_max < TT - 1:
                em.wait("vector", "mm", t_av[kb_max - 1])
                t_no = em.run("vector", lambda v: v.memset(c.att_hi[:, 0, :], -1.0),
                              sem="ev", inc=1)
                em.wait("tensor", "ev", t_no)
                def mm_corr(t, kb_max=kb_max):
                    ins = None
                    for ckb in range(kb_max + 1, TT):
                        for p, vsp in enumerate((c.v_hi, c.v_lo)):
                            for dc in range(CC):
                                ins = t.matmul(c.big4_n[:, dc, :],
                                               vsp[:, ckb, dc * 128:(dc + 1) * 128],
                                               c.att_hi[:, 0, :],
                                               start=False,
                                               stop=(ckb == TT - 1 and p == 1 and dc % 2 == 1))
                    return ins
                t_corr = em.run("tensor", mm_corr, sem="mm", inc=1)
            t_q_cons = em.tok("mm")
            em.wait("vector", "mm", t_corr)
            def fin_den(v, kb_max=kb_max):
                const = -(T - 128 * (kb_max + 1)) + EPS
                v.tensor_scalar(c.vecs[64:65, 0:QG], c.pC[0:1, 0:QG],
                                const, None, ALU.add)
                v.reciprocal(c.vecs[64:65, 0:QG], c.vecs[64:65, 0:QG])
                return v.tensor_scalar(c.vecs[64:65, 0:QG], c.vecs[64:65, 0:QG],
                                       1.0 / 64.0, None, ALU.mult)
            t_den = em.run("vector", fin_den, sem="ev", inc=1)
            t_den_prev = t_den
            em.wait("tensor", "ev", max(t_den, em.state.get("rb_ev", 0)))
            def mm_rb(t):
                return t.matmul(c.pD[:, 0:QG], c.sels[0:65, 256:384],
                                c.vecs[0:65, 0:QG], start=True, stop=True)
            t_rbm = em.run("tensor", mm_rb, sem="mm", inc=1)
            em.wait("vector", "mm", t_rbm)
            em.wait("vector", "ev", t_y_prev)
            t_rb = em.run("vector", lambda v: v.tensor_copy(c.recipb, c.pD[:, 0:QG]),
                          sem="ev", inc=1)
            em.state["rb_ev"] = t_rb
            em.wait("vector", "mm", t_y_cons)
            def ev_y(v, kb_max=kb_max):
                ins = None
                for dc in range(CC):
                    z = c.scr[:, 1, 0:QG]
                    v.tensor_tensor(z, c.big4_n[:, dc, :], c.recipb, ALU.mult)
                    v.tensor_copy(c.y_hi[:, dc, :], z)
                    ins = v.tensor_sub(c.y_lo[:, dc, :], z, c.y_hi[:, dc, :])
                return ins
            t_y = em.run("vector", ev_y, sem="ev", inc=1)
            t_y_prev = t_y
            if self.vmode & 16:
                em.wait("gpsimd", "ev", t_y)
                yw = self.arena[0:128, self._yhi_off:self._yhi_off + 2048].rearrange(
                    "p (a t) -> p a t", a=1)
                em.run("gpsimd", lambda g_, qg=qg, yw=yw: g_.dma_start(
                    out=xdst.rearrange("(a p) t -> p a t", p=128)[:, qg:qg + 1, :],
                    in_=yw), sem="st", inc=16)
                em.barrier()
                t_y_cons = em.tok("mm")
                t_stg_cons = t_y
                t_o_st = em.tok("st")
                continue
            if self.debug:
                em.wait("gpsimd", "ev", t_y)
                em.run("gpsimd", lambda g, qg=qg: g.dma_start(
                    out=self.dbg_y[qg], in_=c.y_hi), sem="st", inc=16)
            em.wait("tensor", "ev", t_y)
            rwo = {0: load_rw(0, "wo", 0), 1: load_rw(1, "wo", 1)}
            t_evo = {}
            for doc in range(CC):
                slot = doc % 2
                em.wait("tensor", "wl", rwo[doc])
                ps_ = c.pA if doc % 2 == 0 else c.pB
                em.wait("tensor", "ev", t_evo.get(doc - 2, 0))
                def mm_o(t, slot=slot, ps_=ps_):
                    ins = None
                    for p, (whalf, ysp) in enumerate((
                            (c.rw_hi, c.y_hi), (c.rw_hi, c.y_lo), (c.rw_lo, c.y_hi))):
                        for cc in range(CC):
                            ins = t.matmul(ps_[:, 0:QG], whalf[:, slot, cc, :],
                                           ysp[:, cc, :],
                                           start=(p == 0 and cc == 0),
                                           stop=(p == 2 and cc == CC - 1))
                    return ins
                t_mm = em.run("tensor", mm_o, sem="mm", inc=1)
                rw_free[slot] = t_mm
                if doc + 2 < CC:
                    rwo[doc + 2] = load_rw(slot, "wo", doc + 2)
                em.wait("vector", "mm", t_mm)
                em.wait("vector", "st", t_o_st)
                def ev_o(v, doc=doc, ps_=ps_):
                    return v.scalar_tensor_tensor(c.scr[:, doc, :], ps_[:, 0:QG],
                                                  c.bo_sb[:, doc:doc + 1],
                                                  c.xstg[:, doc, :], ALU.add, ALU.add)
                t_evo[doc] = em.run("vector", ev_o, sem="ev", inc=1)
            t_y_cons = em.tok("mm")
            t_stg_cons = t_evo[CC - 1]
            em.wait("gpsimd", "ev", t_evo[CC - 1])
            t_o_st = em.run("gpsimd", lambda g_, sl=sl: g_.dma_start(
                out=xdst.rearrange("(a p) t -> p a t", p=128)[:, :, sl], in_=c.scr),
                sem="st", inc=16)
            if self.paranoid & 1:
                em.barrier()


def build_nc(n_seq=4, debug=False):
    nc = bass.Bass("TRN2", target_bir_lowering=False)
    core = Core(nc, n_seq=n_seq, debug=debug)
    return core.build()




def kernel(L0_attn_x, L0_attn_Wq, L0_attn_Wk, L0_attn_Wv, L0_attn_Wo, L0_attn_bo,
           L0_ffn_w_gate, L0_ffn_w_up, L0_ffn_w_down):
    n_cores = 8
    n_seq_per_core = 4
    x = np.asarray(L0_attn_x)
    B = x.shape[0]
    assert B == n_cores * n_seq_per_core

    xT = np.ascontiguousarray(x.transpose(0, 2, 1))
    wqt = np.ascontiguousarray(np.asarray(L0_attn_Wq).T)
    wkt = np.ascontiguousarray(np.asarray(L0_attn_Wk).T)
    wvt = np.ascontiguousarray(np.asarray(L0_attn_Wv).T)
    wot = np.ascontiguousarray(np.asarray(L0_attn_Wo).T)
    bo = np.ascontiguousarray(np.asarray(L0_attn_bo))
    wgt = np.ascontiguousarray(np.asarray(L0_ffn_w_gate).T)
    wut = np.ascontiguousarray(np.asarray(L0_ffn_w_up).T)
    wdt = np.ascontiguousarray(np.asarray(L0_ffn_w_down).T)

    nc = bass.Bass("TRN2", target_bir_lowering=False)
    core = Core(nc, n_seq=n_seq_per_core)
    nc = core.build()

    in_maps = []
    for ci in range(n_cores):
        in_maps.append({
            "xT": xT[ci * n_seq_per_core:(ci + 1) * n_seq_per_core],
            "wqt": wqt, "wkt": wkt, "wvt": wvt, "wot": wot, "bo": bo,
            "wgt": wgt, "wut": wut, "wdt": wdt,
        })
    res = run_bass_kernel_spmd(nc, in_maps, core_ids=list(range(n_cores)))
    outs = [r["outT"] for r in res.results]
    out = np.concatenate(outs, axis=0).transpose(0, 2, 1)
    return np.ascontiguousarray(out.astype(np.float32))


# version 10
# speedup vs baseline: 1.0731x; 1.0047x over previous
from contextlib import ExitStack

import numpy as np
import concourse.bass as bass
import concourse.mybir as mybir
from concourse.bass_utils import run_bass_kernel_spmd

F32 = mybir.dt.float32
F16 = mybir.dt.float16
I32 = mybir.dt.int32
AF = mybir.ActivationFunctionType
ALU = mybir.AluOpType

C = 1024
H = 3072
T = 2048
CC = C // 128
HC = H // 128
TT = T // 128
QG = 256
NQG = T // QG
FG = 512
NFG = T // FG
SG = 256
NSG = T // SG
BHC = 3
NB = HC // BHC
NRS = 4
EPS = 1e-5
QSC = 1.0 / 127.0


class Emitter:

    def __init__(self, core, active, eng):
        self.core = core
        self.active = active
        self.eng = eng
        self.cnt = dict.fromkeys(core.semnames, 0)
        self.state = {}

    def run(self, engine, fn, sem=None, inc=0):
        if sem is not None:
            self.cnt[sem] += inc
        if self.active == engine:
            ins = fn(self.eng)
            if sem is not None:
                assert ins is not None, f"fn on {engine} must return last instruction"
                ins.then_inc(self.core.sem[sem], inc)
        return self.cnt[sem] if sem is not None else None

    def wait(self, engine, sem, val):
        if val is None or val <= 0:
            return
        if self.active == engine:
            self.eng.wait_ge(self.core.sem[sem], val)

    def tok(self, sem):
        return self.cnt[sem]

    def barrier(self):
        for sem in self.core.semnames:
            v = self.cnt[sem]
            if v > 0:
                for e in ("sync", "scalar", "vector", "tensor", "gpsimd"):
                    self.wait(e, sem, v)


class Core:
    semnames = ["ld", "wl", "bc", "mm", "ev", "sc", "gp", "st"]

    def __init__(self, nc, n_seq=4, arena_words=53180, debug=False, paranoid=0, phases='faf', vmode=0):
        self.nc = nc
        self.n_seq = n_seq
        self.paranoid = paranoid
        self.phases = phases
        self.vmode = vmode
        es = ExitStack()
        self.es = es
        d = nc.dram_tensor

        self.xin = d("xT", [n_seq, C, T], F32, kind="ExternalInput").ap()
        self.wqt = d("wqt", [C, C], F32, kind="ExternalInput").ap()
        self.wkt = d("wkt", [C, C], F32, kind="ExternalInput").ap()
        self.wvt = d("wvt", [C, C], F32, kind="ExternalInput").ap()
        self.wot = d("wot", [C, C], F32, kind="ExternalInput").ap()
        self.bo = d("bo", [C], F32, kind="ExternalInput").ap()
        self.wgt = d("wgt", [C, H], I32, kind="ExternalInput").ap()
        self.wut = d("wut", [C, H], I32, kind="ExternalInput").ap()
        self.wdt = d("wdt", [H, C], I32, kind="ExternalInput").ap()
        self.out = d("outT", [n_seq, C, T], F32, kind="ExternalOutput").ap()

        dbg = "ExternalOutput" if debug else "Internal"
        self.debug = debug
        self.x1d = d("x1_dram", [C, T], F32, kind=dbg).ap()
        self.x2d = d("x2_dram", [C, T], F32, kind=dbg).ap()
        if debug:
            self.dbg_dacc = d("dbg_dacc", [2, C, T], F32, kind=dbg).ap()
            self.dbg_h = d("dbg_h", [2, C, T], F16, kind=dbg).ap()
            self.dbg_vec = d("dbg_vec", [2, 2, T], F32, kind=dbg).ap()
            self.dbg_bc = d("dbg_bc", [2, 128, 2, SG], F32, kind=dbg).ap()
            self.dbg_k = d("dbg_k", [128, CC, T], F16, kind=dbg).ap()
            self.dbg_v = d("dbg_v", [128, TT, C], F16, kind=dbg).ap()
            self.dbg_den = d("dbg_den", [NQG, QG], F32, kind=dbg).ap()
            self.dbg_y = d("dbg_y", [NQG, 128, CC, QG], F16, kind=dbg).ap()
        self.vecd = d("vec_dram", [8, T], F32).ap()
        self.wg16 = d("wg16_dram", [HC, 128, CC, 128], F16).ap()
        self.wu16 = d("wu16_dram", [HC, 128, CC, 128], F16).ap()
        self.wd16 = d("wd16_dram", [HC, 128, C], F16).ap()
        self.wsp = {}
        for w in ("wq", "wk", "wo"):
            for h in ("hi", "lo"):
                self.wsp[f"{w}_{h}"] = d(f"{w}_{h}_dram", [CC, 128, CC, 128], F16).ap()
        for h in ("hi", "lo"):
            self.wsp[f"wv_{h}"] = d(f"wv_{h}_dram", [4, 128, CC, 256], F16).ap()

        self.arena = es.enter_context(nc.sbuf_tensor("arena", [128, arena_words], F32))
        self._off = 0

        def V(shape, dt=F32, parts=128, at=None):
            n = 1
            for s in shape:
                n *= s
            words = (n + 1) // 2 if dt == F16 else n
            words = (words + 1) & ~1
            off = self._off if at is None else at
            if at is None:
                self._off += words
            ap = self.arena[0:parts, off:off + words]
            ap = ap.bitcast(F16)[:, 0:n] if dt == F16 else ap[:, 0:n]
            if len(shape) > 1:
                letters = list("abcdef")[: len(shape)]
                ap = ap.rearrange(
                    f"p ({' '.join(letters)}) -> p {' '.join(letters)}",
                    **{letters[i]: shape[i] for i in range(len(shape))})
            return ap, off

        self.k_hi, o0 = V([CC, T], F16)
        self.k_lo, o1 = V([CC, T], F16)
        self.v_hi, o2 = V([TT, C], F16)
        self.v_lo, o3 = V([TT, C], F16)
        self.h_hi, _ = V([CC, T], F16, at=o0)
        self.h_lo, _ = V([CC, T], F16, at=o1)
        self.dacc, _ = V([CC, T], F32, at=o2)

        ring0 = self._off
        self.rg, _ = V([NRS, CC, 128], F16)
        self.ru, _ = V([NRS, CC, 128], F16)
        self.rd, _ = V([NRS, C], F16)
        ring_end = self._off
        self.wvr, _ = V([2, 2, CC, 256], F16, at=ring0)
        self.rw_hi, _ = V([2, CC, 128], F16, at=ring0 + 4096)
        self.rw_lo, _ = V([2, CC, 128], F16, at=ring0 + 4096 + 1024)
        self._off = max(ring_end, ring0 + 4096 + 2048)

        work0 = self._off
        self.q_hi, _ = V([CC, QG], F16)
        self.q_lo, _ = V([CC, QG], F16)
        self.y_hi, yo_ = V([CC, QG], F16)
        self.y_lo, _ = V([CC, QG], F16)
        self._yhi_off = yo_
        self.th_hi, _ = V([CC, SG], F16)
        self.th_lo, _ = V([CC, SG], F16)
        self.att_hi, _ = V([2, QG], F16)
        self.att_lo, _ = V([2, QG], F16)
        work_end = self._off
        self.act_hi, _ = V([BHC, FG], F16, at=work0)
        self.act_lo, _ = V([BHC, FG], F16, at=work0 + 1024)
        self._off = work_end

        self.xstg, _ = V([CC, SG], F32)
        self.scr, _ = V([CC, SG], F32)

        self.vecs, _ = V([T], F32, parts=128)
        self.bcast, _ = V([2, SG], F32)
        self.recipb, _ = V([QG], F32)
        self.sels, _ = V([384], F32)
        self.masks, _ = V([2, QG], F16)
        self.ones, _ = V([2], F16)
        self.onesf, _ = V([2], F32)
        self.bo_sb, _ = V([CC], F32)
        print(f"[arena] {self._off} words = {self._off * 4 / 1024:.1f} KiB/partition")
        assert self._off <= arena_words, self._off

        ps = lambda name, shape: es.enter_context(nc.psum_tensor(name, shape, F32))
        self.big4 = ps("big4", [128, 2048])
        self.big4_d = self.big4.ap().rearrange("p (a b) -> p a b", a=4)
        self.big4_n = self.big4.ap().rearrange("p (a b) -> p a b", a=8)
        self.pA = ps("pA", [128, 512])
        self.pB = ps("pB", [128, 512])
        self.pC = ps("pC", [128, 512])
        self.pD = ps("pD", [128, 512])
        self.sem = {s: es.enter_context(nc.semaphore(s)) for s in self.semnames}

    ROW_M, ROW_V, ROW_R, ROW_DEN = 0, 32, 64, 96

    def vrow(self, r, sl=slice(None)):
        return self.vecs[r:r + 1, sl]

    def build(self):
        nc = self.nc
        with self.es, nc.Block() as block:
            for name in ("sync", "scalar", "vector", "tensor", "gpsimd"):
                def mk(engname):
                    def body(eng):
                        em = Emitter(self, engname, eng)
                        self.schedule(em)
                    return body
                getattr(block, name)(mk(name))
        return nc

    def schedule(self, em):
        self.setup(em)
        em.barrier()
        for s in range(self.n_seq):
            if self.phases == 'faf':
                self.ffn_phase(em, self.xin[s], self.x1d, dbg_idx=0 if s == self.n_seq - 1 else None)
                em.barrier()
                self.attn_phase(em, self.x1d, self.x2d)
                em.barrier()
                self.ffn_phase(em, self.x2d, self.out[s], dbg_idx=1 if s == self.n_seq - 1 else None)
            elif self.phases == 'f':
                self.ffn_phase(em, self.xin[s], self.out[s])
            elif self.phases == 'a':
                self.attn_phase(em, self.xin[s], self.out[s])
            elif self.phases == 'a1':
                self.attn_phase(em, self.xin[s], self.out[s], upto=1)
            elif self.phases == 'fa':
                self.ffn_phase(em, self.xin[s], self.x1d)
                em.barrier()
                self.attn_phase(em, self.x1d, self.out[s])
            em.barrier()
        em.run("gpsimd", lambda g: g.wait_ge(self.sem["st"], em.tok("st")) or g.nop())

    def setup(self, em):
        c = self
        em.run("vector", lambda v: v.memset(c.ones, 1.0), sem="ev", inc=1)
        em.run("vector", lambda v: v.memset(c.onesf, 1.0), sem="ev", inc=1)
        em.run("gpsimd", lambda g: g.iota(c.sels, [[0, 384]], channel_multiplier=1,
               allow_small_or_imprecise_dtypes=True), sem="gp", inc=1)
        em.wait("vector", "gp", em.tok("gp"))
        def mk_sels(v):
            v.tensor_scalar(c.sels[:, 0:128], c.sels[:, 0:128], 0.0, None, ALU.is_equal)
            v.tensor_scalar(c.sels[:, 128:256], c.sels[:, 128:256], 32.0, None, ALU.is_equal)
            v.tensor_scalar(c.sels[:, 256:384], c.sels[:, 256:384], 64.0, None, ALU.is_equal)
            return v.memset(c.vecs[0:96, :], 0.0)
        em.run("vector", mk_sels, sem="ev", inc=1)
        em.run("sync", lambda s: s.dma_start(
            out=c.bo_sb, in_=c.bo.rearrange("(d p) -> p d", p=128),
            allow_slow_non_contiguous=True), sem="ld", inc=16)

        def mk_masks(g):
            ins = None
            for var in range(2):
                ins = g.iota(c.masks[:, var, :], [[1, QG]], base=-128 * var,
                             channel_multiplier=-1,
                             allow_small_or_imprecise_dtypes=True)
            return ins
        em.run("gpsimd", mk_masks, sem="gp", inc=1)
        em.wait("vector", "gp", em.tok("gp"))

        def fin_masks(v):
            return v.tensor_scalar(c.masks[:], c.masks[:], -0.5, None, ALU.is_ge)
        em.run("vector", fin_masks, sem="ev", inc=1)

        stgs_i = (c.xstg.bitcast(I32)[:, :, 0:128], c.scr.bitcast(I32)[:, :, 0:128])
        outs_f = (c.th_hi, c.th_lo)
        st_tok = [0, 0]
        idx = 0
        for wsrc, wdst in ((c.wgt, c.wg16), (c.wut, c.wu16)):
            for hc in range(HC):
                p = idx % 2
                idx += 1
                hsl = slice(hc * 128, hc * 128 + 128)
                em.wait("sync", "st", st_tok[p])
                t_ld = em.run("sync", lambda s, wsrc=wsrc, hsl=hsl, p=p: s.dma_start(
                    out=stgs_i[p],
                    in_=wsrc.rearrange("(a p) n -> p a n", p=128)[:, :, hsl]),
                    sem="ld", inc=16)
                em.wait("vector", "ld", t_ld)
                em.wait("vector", "st", st_tok[p])
                t_cv = em.run("vector", lambda v, p=p: v.tensor_copy(
                    outs_f[p][:, :, 0:128], stgs_i[p]), sem="ev", inc=1)
                em.wait("gpsimd", "ev", t_cv)
                st_tok[p] = em.run("gpsimd", lambda g, wdst=wdst, hc=hc, p=p: g.dma_start(
                    out=wdst[hc], in_=outs_f[p][:, :, 0:128]), sem="st", inc=16)
        stgs_i2 = (c.xstg.bitcast(I32)[:, 0:4, :], c.scr.bitcast(I32)[:, 0:4, :])
        for hc in range(HC):
            p = idx % 2
            idx += 1
            em.wait("sync", "st", st_tok[p])
            t_ld = em.run("sync", lambda s, hc=hc, p=p: s.dma_start(
                out=stgs_i2[p],
                in_=c.wdt.rearrange("(a p) n -> p a n", p=128)[:, hc, :]
                    .rearrange("p (a b) -> p a b", a=4)),
                sem="ld", inc=16)
            em.wait("vector", "ld", t_ld)
            em.wait("vector", "st", st_tok[p])
            t_cv = em.run("vector", lambda v, p=p: v.tensor_copy(
                outs_f[p][:, 0:4, :], stgs_i2[p]), sem="ev", inc=1)
            em.wait("gpsimd", "ev", t_cv)
            st_tok[p] = em.run("gpsimd", lambda g, hc=hc, p=p: g.dma_start(
                out=c.wd16[hc],
                in_=outs_f[p][:, 0:4, :].rearrange("p a b -> p (a b)")),
                sem="st", inc=16)

        stgs_f = (c.xstg, c.scr)
        outs_hi = (c.th_hi, c.q_hi)
        outs_lo = (c.th_lo, c.q_lo)
        for wname, wsrc in (("wq", c.wqt), ("wk", c.wkt), ("wo", c.wot), ("wv", c.wvt)):
            ncols = 128 if wname != "wv" else 256
            for blk in range(C // ncols):
                p = idx % 2
                idx += 1
                csl = slice(blk * ncols, (blk + 1) * ncols)
                em.wait("sync", "st", st_tok[p])
                t_ld = em.run("sync", lambda s, wsrc=wsrc, csl=csl, ncols=ncols, p=p: s.dma_start(
                    out=stgs_f[p][:, :, 0:ncols],
                    in_=wsrc.rearrange("(a p) n -> p a n", p=128)[:, :, csl]),
                    sem="ld", inc=16)
                em.wait("scalar", "ld", t_ld)
                em.wait("scalar", "st", st_tok[p])
                wscale = 64.0 if wname == "wo" else 1.0
                t_hi = em.run("scalar", lambda sc, ncols=ncols, wscale=wscale, p=p: sc.activation(
                    outs_hi[p][:, :, 0:ncols], stgs_f[p][:, :, 0:ncols], AF.Copy,
                    scale=wscale), sem="sc", inc=1)
                em.wait("vector", "sc", t_hi)
                t_lo = em.run("vector", lambda v, ncols=ncols, wscale=wscale, p=p: v.scalar_tensor_tensor(
                    outs_lo[p][:, :, 0:ncols], stgs_f[p][:, :, 0:ncols], wscale,
                    outs_hi[p][:, :, 0:ncols], ALU.mult, ALU.subtract), sem="ev", inc=1)
                em.wait("gpsimd", "sc", t_hi)
                em.wait("gpsimd", "ev", t_lo)
                em.run("gpsimd", lambda g, wname=wname, blk=blk, ncols=ncols, p=p: g.dma_start(
                    out=c.wsp[f"{wname}_hi"][blk], in_=outs_hi[p][:, :, 0:ncols]),
                    sem="st", inc=16)
                st_tok[p] = em.run("gpsimd", lambda g, wname=wname, blk=blk, ncols=ncols, p=p: g.dma_start(
                    out=c.wsp[f"{wname}_lo"][blk], in_=outs_lo[p][:, :, 0:ncols]),
                    sem="st", inc=16)

    def ln_stats(self, em, xsrc):
        c = self
        t_cons = 0
        t_statmm = 0
        t_stat_ev = 0
        for g in range(NSG):
            sl = slice(g * SG, (g + 1) * SG)
            em.wait("sync", "sc", t_cons)
            em.wait("sync", "mm", t_statmm)
            t_ld = em.run("sync", lambda s, sl=sl: s.dma_start(
                out=c.xstg, in_=xsrc.rearrange("(a p) t -> p a t", p=128)[:, :, sl]),
                sem="ld", inc=16)
            em.wait("scalar", "ld", t_ld)
            em.wait("scalar", "mm", t_statmm)
            t_cons = em.run("scalar", lambda sc: sc.activation(
                c.scr, c.xstg, AF.Square), sem="sc", inc=1)
            em.wait("tensor", "sc", t_cons)
            em.wait("tensor", "ld", t_ld)
            em.wait("tensor", "ev", t_stat_ev)
            def mm_stats(t):
                ins = None
                for cc in range(CC):
                    ins = t.matmul(c.pC[0:1, 0:SG], c.onesf[:, 0:1], c.xstg[:, cc, :],
                                   start=(cc == 0), stop=(cc == CC - 1))
                for cc in range(CC):
                    ins = t.matmul(c.pD[0:1, 0:SG], c.onesf[:, 0:1], c.scr[:, cc, :],
                                   start=(cc == 0), stop=(cc == CC - 1))
                return ins
            t_mm = em.run("tensor", mm_stats, sem="mm", inc=1)
            t_statmm = t_mm
            em.wait("vector", "mm", t_mm)
            t_stat_ev = em.run("vector", lambda v, sl=sl: (
                v.tensor_copy(c.vrow(self.ROW_M, sl), c.pC[0:1, 0:SG]),
                v.tensor_copy(c.vrow(self.ROW_V, sl), c.pD[0:1, 0:SG]))[-1],
                sem="ev", inc=1)
        em.state["ln_xstg_free"] = t_cons
        em.state["ln_stats_mm"] = t_statmm
        return t_stat_ev

    def bcast_mr(self, em, g, t_mr, t_free, width=SG):
        c = self
        sl = slice(g * width, g * width + width)
        em.wait("tensor", "ev", max(t_mr, em.state.get("bc_ev", 0),
                                    em.state.get("rb_ev", 0)))
        def mm_bc(t, sl=sl, width=width):
            t.matmul(c.pD[:, 0:width], c.sels[0:33, 0:128], c.vecs[0:33, sl],
                     start=True, stop=False)
            return t.matmul(c.pD[:, 256:256 + width], c.sels[0:33, 128:256],
                            c.vecs[0:33, sl], start=False, stop=True)
        t_bm = em.run("tensor", mm_bc, sem="mm", inc=1)
        em.wait("vector", "mm", t_bm)
        em.wait("vector", "ev", t_free)
        t_ev = em.run("vector", lambda v, width=width: (
            v.tensor_copy(c.bcast[:, 0, 0:width], c.pD[:, 0:width]),
            v.tensor_copy(c.bcast[:, 1, 0:width], c.pD[:, 256:256 + width]))[-1],
            sem="ev", inc=1)
        em.state["bc_ev"] = t_ev
        return t_ev

    def h_slice(self, em, dst_hi, dst_lo, width=SG):
        c = self
        em.wait("vector", "mm", em.state.get("ln_stats_mm", 0))
        def mk1(v):
            b0 = c.bcast[:, 0, 0:width]
            b1 = c.bcast[:, 1, 0:width]
            t = c.scr[:, 7, 0:width]
            v.tensor_scalar(b0, b0, 1.0 / C, None, ALU.mult)
            v.tensor_tensor(t, b0, b0, ALU.mult)
            v.scalar_tensor_tensor(b1, b1, 1.0 / C, t, ALU.mult, ALU.subtract)
            v.tensor_scalar(b1, b1, EPS, None, ALU.add)
            return v.reciprocal(b1, b1)
        t1 = em.run("vector", mk1, sem="ev", inc=1)
        em.wait("scalar", "ev", t1)
        t2 = em.run("scalar", lambda sc, width=width: sc.activation(
            c.bcast[:, 1, 0:width], c.bcast[:, 1, 0:width], AF.Sqrt),
            sem="sc", inc=1)
        em.wait("vector", "sc", t2)
        def mk2(v):
            v.tensor_tensor(c.scr[:, :, 0:width], c.xstg[:, :, 0:width],
                            c.bcast[:, 0:1, 0:width].broadcast_to((128, CC, width)),
                            ALU.subtract)
            v.tensor_tensor(c.scr[:, :, 0:width], c.scr[:, :, 0:width],
                            c.bcast[:, 1:2, 0:width].broadcast_to((128, CC, width)),
                            ALU.mult)
            v.tensor_copy(dst_hi, c.scr[:, :, 0:width])
            return v.tensor_sub(dst_lo, c.scr[:, :, 0:width], dst_hi)
        return em.run("vector", mk2, sem="ev", inc=1)

    def ffn_phase(self, em, xsrc, xdst, dbg_idx=None):
        c = self
        t_mr = self.ln_stats(em, xsrc)
        if self.debug and dbg_idx is not None:
            em.wait("gpsimd", "st", t_mr)
            em.run("gpsimd", lambda g, dbg_idx=dbg_idx: g.dma_start(
                out=self.dbg_vec[dbg_idx], in_=c.vecd[0:2, :]), sem="st", inc=16)
        t_h = 0
        for g in range(NSG):
            sl = slice(g * SG, (g + 1) * SG)
            em.wait("sync", "ev", t_h)
            em.wait("sync", "sc", em.state["ln_xstg_free"])
            em.wait("sync", "mm", em.state["ln_stats_mm"])
            t_ld = em.run("sync", lambda s, sl=sl: s.dma_start(
                out=c.xstg, in_=xsrc.rearrange("(a p) t -> p a t", p=128)[:, :, sl]),
                sem="ld", inc=16)
            t_bc = self.bcast_mr(em, g, t_mr, t_h)
            em.wait("vector", "ld", t_ld)
            t_h = self.h_slice(em, c.h_hi[:, :, sl], c.h_lo[:, :, sl])
            if self.paranoid & 16:
                em.barrier()
        if self.debug and dbg_idx is not None:
            em.wait("gpsimd", "ev", t_h)
            em.run("gpsimd", lambda g, dbg_idx=dbg_idx: g.dma_start(
                out=self.dbg_bc[dbg_idx], in_=c.bcast), sem="st", inc=16)

        slot_free = {}
        t_wl = {}

        def load_slot(slot, hc):
            em.wait("sync", "mm", slot_free.get(slot, 0))
            em.run("sync", lambda s: s.dma_start(
                out=c.rg[:, slot], in_=c.wg16[hc]), sem="wl", inc=16)
            em.run("sync", lambda s: s.dma_start(
                out=c.ru[:, slot], in_=c.wu16[hc]), sem="wl", inc=16)
            return em.run("sync", lambda s: s.dma_start(
                out=c.rd[:, slot], in_=c.wd16[hc]), sem="wl", inc=16)

        for i in range(BHC):
            t_wl[(0, i)] = load_slot(i % NRS, i)

        em.wait("tensor", "ev", t_h)
        t_act_prev = 0
        t_down_mm_prev = 0
        t_down_ev_prev = 0
        for b in range(NB):
            for tg in range(NFG):
                sl = slice(tg * FG, (tg + 1) * FG)
                t_act = {}
                for i in range(BHC):
                    slot = (b * BHC + i) % NRS
                    em.wait("tensor", "wl", t_wl[(b, i)])
                    pg, pu = (c.pA, c.pB) if i % 2 == 0 else (c.pC, c.pD)
                    em.wait("tensor", "ev", t_act.get(i - 2, t_act_prev))
                    def mm_gu(t, slot=slot, sl=sl, pg=pg, pu=pu):
                        ins = None
                        for src, ps_ in ((c.rg, pg), (c.ru, pu)):
                            for p, hsp in enumerate((c.h_hi, c.h_lo)):
                                for cc in range(CC):
                                    ins = t.matmul(ps_[:, 0:FG], src[:, slot, cc, :],
                                                   hsp[:, cc, sl],
                                                   start=(p == 0 and cc == 0),
                                                   stop=(p == 1 and cc == CC - 1))
                        return ins
                    t_mm = em.run("tensor", mm_gu, sem="mm", inc=1)
                    em.wait("scalar", "mm", t_mm)
                    t_si = em.run("scalar", lambda sc, pg=pg: sc.activation(
                        c.scr[:, 0:2, :].rearrange("p a b -> p (a b)"), pg[:, 0:FG],
                        AF.Silu, scale=QSC), sem="sc", inc=1)
                    em.wait("vector", "sc", t_si)
                    em.wait("vector", "mm", t_down_mm_prev)
                    def mk_act(v, i=i, pu=pu):
                        z = c.scr[:, 2:4, :].rearrange("p a b -> p (a b)")
                        silu = c.scr[:, 0:2, :].rearrange("p a b -> p (a b)")
                        v.scalar_tensor_tensor(z, pu[:, 0:FG], QSC, silu, ALU.mult, ALU.mult)
                        v.tensor_copy(c.act_hi[:, i, :], z)
                        return v.tensor_sub(c.act_lo[:, i, :], z, c.act_hi[:, i, :])
                    t_act[i] = em.run("vector", mk_act, sem="ev", inc=1)
                t_act_prev = t_act[BHC - 1]
                for dh in range(2):
                    em.wait("tensor", "ev", t_down_ev_prev)
                    def mm_down_i(t, b=b, dh=dh, i=0, sub=False):
                        ins = None
                        rng = (i,) if sub else range(BHC)
                        for ii in rng:
                            slot = (b * BHC + ii) % NRS
                            for p, asp in enumerate((c.act_hi, c.act_lo)):
                                for dc in range(4):
                                    dsl = slice(dh * 512 + dc * 128,
                                                dh * 512 + (dc + 1) * 128)
                                    ins = t.matmul(c.big4_d[:, dc, 0:FG],
                                                   c.rd[:, slot, dsl], asp[:, ii, :],
                                                   start=(ii == 0 and p == 0),
                                                   stop=(ii == BHC - 1 and p == 1))
                        return ins
                    if dh == 0:
                        for i in range(BHC):
                            em.wait("tensor", "ev", t_act[i])
                            t_down_mm_prev = em.run(
                                "tensor",
                                (lambda ii, bb=b, dd=dh: (
                                    lambda t: mm_down_i(t, b=bb, dh=dd, i=ii, sub=True)))(i),
                                sem="mm", inc=1)
                    else:
                        t_down_mm_prev = em.run("tensor", mm_down_i, sem="mm", inc=1)
                    em.wait("vector", "mm", t_down_mm_prev)
                    def ev_down(v, b=b, dh=dh, sl=sl):
                        dst = c.dacc[:, dh * 4:(dh + 1) * 4, sl]
                        src = c.big4_d[:, :, 0:FG]
                        if b == 0:
                            return v.tensor_copy(dst, src)
                        return v.tensor_tensor(dst, dst, src, ALU.add)
                    t_down_ev_prev = em.run("vector", ev_down, sem="ev", inc=1)
                if tg == NFG - 1:
                    for i in range(BHC):
                        slot_free[(b * BHC + i) % NRS] = t_down_mm_prev
                if self.paranoid & 8:
                    em.barrier()
            if self.paranoid & 4:
                em.barrier()
            if b + 1 < NB:
                for i in range(BHC):
                    hc = (b + 1) * BHC + i
                    t_wl[(b + 1, i)] = load_slot(hc % NRS, hc)

        if self.debug and dbg_idx is not None:
            em.wait("gpsimd", "ev", t_down_ev_prev)
            em.run("gpsimd", lambda g, dbg_idx=dbg_idx: g.dma_start(
                out=self.dbg_dacc[dbg_idx].rearrange("(a p) t -> p a t", p=128),
                in_=c.dacc), sem="st", inc=16)
            em.run("gpsimd", lambda g, dbg_idx=dbg_idx: g.dma_start(
                out=self.dbg_h[dbg_idx].rearrange("(a p) t -> p a t", p=128),
                in_=c.h_hi), sem="st", inc=16)
        t_f = t_down_ev_prev
        t_fin_st = em.tok("st")
        for g in range(NSG):
            sl = slice(g * SG, (g + 1) * SG)
            em.wait("sync", "ev", t_f)
            t_ld = em.run("sync", lambda s, sl=sl: s.dma_start(
                out=c.xstg, in_=xsrc.rearrange("(a p) t -> p a t", p=128)[:, :, sl]),
                sem="ld", inc=16)
            em.wait("vector", "ld", t_ld)
            em.wait("vector", "ev", t_down_ev_prev)
            em.wait("vector", "st", t_fin_st)
            t_f = em.run("vector", lambda v, sl=sl: v.scalar_tensor_tensor(
                c.scr, c.dacc[:, :, sl], 0.5 * QSC, c.xstg, ALU.mult, ALU.add),
                sem="ev", inc=1)
            em.wait("gpsimd", "ev", t_f)
            t_fin_st = em.run("gpsimd", lambda g_, sl=sl: g_.dma_start(
                out=xdst.rearrange("(a p) t -> p a t", p=128)[:, :, sl], in_=c.scr),
                sem="st", inc=16)

    def attn_phase(self, em, xsrc, xdst, upto=99):
        c = self
        t_mr = self.ln_stats(em, xsrc)

        rw_free = {0: 0, 1: 0}
        wv_free = {0: 0, 1: 0}

        def load_wv(slot, dq):
            em.wait("scalar", "mm", wv_free[slot])
            em.run("scalar", lambda s: s.dma_start(
                out=c.wvr[:, slot, 0], in_=c.wsp["wv_hi"][dq]), sem="wl", inc=16)
            return em.run("scalar", lambda s: s.dma_start(
                out=c.wvr[:, slot, 1], in_=c.wsp["wv_lo"][dq]), sem="wl", inc=16)

        def load_rw(slot, wname, blk):
            em.wait("scalar", "mm", rw_free[slot])
            em.run("scalar", lambda s: s.dma_start(
                out=c.rw_hi[:, slot], in_=c.wsp[f"{wname}_hi"][blk]), sem="wl", inc=16)
            return em.run("scalar", lambda s: s.dma_start(
                out=c.rw_lo[:, slot], in_=c.wsp[f"{wname}_lo"][blk]), sem="wl", inc=16)

        first_vb = True
        t_h = 0
        t_th_cons = 0
        for g in range(NSG):
            sl = slice(g * SG, (g + 1) * SG)
            em.wait("sync", "ev", t_h)
            em.wait("sync", "sc", em.state["ln_xstg_free"])
            em.wait("sync", "mm", em.state["ln_stats_mm"])
            t_ld = em.run("sync", lambda s, sl=sl: s.dma_start(
                out=c.xstg, in_=xsrc.rearrange("(a p) t -> p a t", p=128)[:, :, sl]),
                sem="ld", inc=16)
            t_bc = self.bcast_mr(em, g, t_mr, t_h)
            em.wait("vector", "ld", t_ld)
            em.wait("vector", "mm", t_th_cons)
            t_h = self.h_slice(em, c.th_hi, c.th_lo)
            em.wait("tensor", "ev", t_h)
            rw_tok = {0: load_rw(0, "wk", 0), 1: load_rw(1, "wk", 1)}
            t_evk = {}
            for dkc in range(CC):
                slot = dkc % 2
                em.wait("tensor", "wl", rw_tok[dkc])
                ps_ = c.pA if dkc % 2 == 0 else c.pB
                em.wait("tensor", "ev", t_evk.get(dkc - 2, 0))
                def mm_k(t, slot=slot, ps_=ps_):
                    ins = None
                    for p, (whalf, hsp) in enumerate((
                            (c.rw_hi, c.th_hi), (c.rw_hi, c.th_lo), (c.rw_lo, c.th_hi))):
                        for cc in range(CC):
                            ins = t.matmul(ps_[:, 0:SG], whalf[:, slot, cc, :],
                                           hsp[:, cc, :],
                                           start=(p == 0 and cc == 0),
                                           stop=(p == 2 and cc == CC - 1))
                    return ins
                t_mm = em.run("tensor", mm_k, sem="mm", inc=1)
                rw_free[slot] = t_mm
                if dkc + 2 < CC:
                    rw_tok[dkc + 2] = load_rw(slot, "wk", dkc + 2)
                em.wait("vector", "mm", t_mm)
                def ev_k(v, dkc=dkc, sl=sl, ps_=ps_):
                    v.tensor_scalar(c.k_hi[:, dkc, sl], ps_[:, 0:SG], 0.1, None, ALU.mult)
                    return v.scalar_tensor_tensor(c.k_lo[:, dkc, sl], ps_[:, 0:SG], 0.1,
                                                  c.k_hi[:, dkc, sl], ALU.mult, ALU.subtract)
                t_evk[dkc] = em.run("vector", ev_k, sem="ev", inc=1)
            t_th_cons = em.tok("mm")
            t_evv = {}
            wv_tok = {0: load_wv(0, 0), 1: load_wv(1, 1)}
            for dq in range(4):
                wslot = dq % 2
                for ti in range(2):
                    tt = g * 2 + ti
                    idx = dq * 2 + ti
                    ps_ = c.pA if idx % 2 == 0 else c.pB
                    em.wait("tensor", "wl", wv_tok[dq])
                    em.wait("tensor", "ev", t_evv.get(idx - 2, t_evk[CC - 1]))
                    def mm_v(t, ti=ti, wslot=wslot, ps_=ps_):
                        ins = None
                        for p, (hsp, wh) in enumerate(((c.th_hi, 0), (c.th_lo, 0),
                                                       (c.th_hi, 1))):
                            for cc in range(CC):
                                ins = t.matmul(ps_[:, 0:256],
                                               hsp[:, cc, ti * 128:(ti + 1) * 128],
                                               c.wvr[:, wslot, wh, cc, :],
                                               start=(p == 0 and cc == 0),
                                               stop=(p == 2 and cc == CC - 1))
                        return ins
                    t_mm = em.run("tensor", mm_v, sem="mm", inc=1)
                    wv_free[wslot] = t_mm
                    if ti == 1 and dq + 2 < 4:
                        wv_tok[dq + 2] = load_wv(wslot, dq + 2)
                    em.wait("vector", "mm", t_mm)
                    def ev_v(v, tt=tt, dq=dq, ps_=ps_):
                        dsl = slice(dq * 256, (dq + 1) * 256)
                        v.tensor_copy(c.v_hi[:, tt, dsl], ps_[:, 0:256])
                        return v.tensor_sub(c.v_lo[:, tt, dsl], ps_[:, 0:256],
                                            c.v_hi[:, tt, dsl])
                    t_evv[idx] = em.run("vector", ev_v, sem="ev", inc=1)
            t_th_cons = em.tok("mm")
            if self.paranoid & 2:
                em.barrier()
        t_vrem = em.tok("ev")
        if upto == 1:
            em.wait("gpsimd", "ev", t_vrem)
            flat = xdst.rearrange("(a p) t -> p a t", p=128)
            for i in range(4):
                src = self.arena[0:128, 8192 * i:8192 * i + 4096].rearrange(
                    "p (a t) -> p a t", a=2)
                em.run("gpsimd", lambda g, i=i, src=src: g.dma_start(
                    out=flat[:, 2 * i:2 * i + 2, :], in_=src), sem="st", inc=16)
            return
        if self.debug:
            em.wait("gpsimd", "ev", t_vrem)
            em.run("gpsimd", lambda g: g.dma_start(out=self.dbg_k, in_=c.k_hi),
                   sem="st", inc=16)
            em.run("gpsimd", lambda g: g.dma_start(out=self.dbg_v, in_=c.v_hi),
                   sem="st", inc=16)

        t_q_cons = 0
        t_y_cons = 0
        t_stg_cons = 0
        t_att = {}
        t_av = {}
        t_den_prev = 0
        t_y_prev = 0
        t_o_st = em.tok("st")
        for qg in range(NQG):
            sl = slice(qg * QG, (qg + 1) * QG)
            kb_max = 2 * qg + 1
            em.wait("sync", "ev", max(t_stg_cons, t_h))
            t_ld = em.run("sync", lambda s, sl=sl: s.dma_start(
                out=c.xstg, in_=xsrc.rearrange("(a p) t -> p a t", p=128)[:, :, sl]),
                sem="ld", inc=16)
            t_bc = self.bcast_mr(em, qg, t_mr, max(t_stg_cons, t_h))
            em.wait("vector", "ld", t_ld)
            em.wait("vector", "mm", t_th_cons)
            em.wait("vector", "st", t_o_st)
            t_h = self.h_slice(em, c.th_hi, c.th_lo)
            em.wait("tensor", "ev", t_h)
            rwq = {0: load_rw(0, "wq", 0), 1: load_rw(1, "wq", 1)}
            t_evq = {}
            for dqc in range(CC):
                slot = dqc % 2
                em.wait("tensor", "wl", rwq[dqc])
                ps_ = c.pA if dqc % 2 == 0 else c.pB
                em.wait("tensor", "ev", t_evq.get(dqc - 2, 0))
                def mm_q(t, slot=slot, ps_=ps_):
                    ins = None
                    for p, (whalf, hsp) in enumerate((
                            (c.rw_hi, c.th_hi), (c.rw_hi, c.th_lo), (c.rw_lo, c.th_hi))):
                        for cc in range(CC):
                            ins = t.matmul(ps_[:, 0:QG], whalf[:, slot, cc, :],
                                           hsp[:, cc, :],
                                           start=(p == 0 and cc == 0),
                                           stop=(p == 2 and cc == CC - 1))
                    return ins
                t_mm = em.run("tensor", mm_q, sem="mm", inc=1)
                rw_free[slot] = t_mm
                if dqc + 2 < CC:
                    rwq[dqc + 2] = load_rw(slot, "wq", dqc + 2)
                em.wait("vector", "mm", t_mm)
                em.wait("vector", "mm", t_q_cons)
                def ev_q(v, dqc=dqc, ps_=ps_):
                    v.tensor_scalar(c.q_hi[:, dqc, :], ps_[:, 0:QG], 0.1, None, ALU.mult)
                    return v.scalar_tensor_tensor(c.q_lo[:, dqc, :], ps_[:, 0:QG], 0.1,
                                                  c.q_hi[:, dqc, :], ALU.mult, ALU.subtract)
                t_evq[dqc] = em.run("vector", ev_q, sem="ev", inc=1)
            t_th_cons = em.tok("mm")
            if self.vmode & 32:
                em.barrier()
            em.wait("tensor", "ev", max(t_evq[CC - 1], t_y_prev, t_den_prev, t_vrem))
            av_fns = {}
            prev_kb = None
            for kb in range(kb_max + 1):
                ps_ = c.pA if kb % 2 == 0 else c.pB
                abuf = kb % 2
                em.wait("tensor", "ev", t_att.get(kb - 2, 0))
                def mm_s(t, kb=kb, ps_=ps_):
                    ins = None
                    for p, (ksp, qsp) in enumerate((
                            (c.k_hi, c.q_hi), (c.k_hi, c.q_lo), (c.k_lo, c.q_hi))):
                        for dkc in range(CC):
                            ins = t.matmul(ps_[:, 0:QG],
                                           ksp[:, dkc, kb * 128:(kb + 1) * 128],
                                           qsp[:, dkc, :],
                                           start=(p == 0 and dkc == 0),
                                           stop=(p == 2 and dkc == CC - 1))
                    return ins
                t_mm = em.run("tensor", mm_s, sem="mm", inc=1)
                if prev_kb is not None:
                    em.wait("tensor", "ev", t_att[prev_kb])
                    t_av[prev_kb] = em.run("tensor", av_fns[prev_kb], sem="mm", inc=1)
                em.wait("vector", "mm", max(t_mm, t_av.get(kb - 2, 0)))
                if kb >= 2 * qg:
                    var = kb - 2 * qg
                    def ev_att(v, ps_=ps_, abuf=abuf, var=var):
                        z = c.scr[:, 0, 0:QG]
                        v.tensor_scalar(z, ps_[:, 0:QG], 0.01, 1.0, ALU.mult, ALU.add)
                        v.tensor_tensor(z, z, c.masks[:, var, :], ALU.mult)
                        v.tensor_scalar(c.att_hi[:, abuf, :], z, -1.0, None, ALU.add)
                        return v.scalar_tensor_tensor(c.att_lo[:, abuf, :], z, -1.0,
                                                      c.att_hi[:, abuf, :],
                                                      ALU.add, ALU.subtract)
                else:
                    def ev_att(v, ps_=ps_, abuf=abuf):
                        v.tensor_scalar(c.att_hi[:, abuf, :], ps_[:, 0:QG], 0.01,
                                        None, ALU.mult)
                        return v.scalar_tensor_tensor(c.att_lo[:, abuf, :], ps_[:, 0:QG],
                                                      0.01, c.att_hi[:, abuf, :],
                                                      ALU.mult, ALU.subtract)
                t_att[kb] = em.run("vector", ev_att, sem="ev", inc=1)
                last = (kb == kb_max)
                final = (kb == TT - 1)
                def mm_av(t, kb=kb, abuf=abuf, first=(kb == 0), last=last, final=final):
                    for k_, asp in enumerate((c.att_hi, c.att_lo)):
                        t.matmul(c.pC[0:1, 0:QG], c.ones[:, 0:1], asp[:, abuf, :],
                                 start=(first and k_ == 0), stop=(last and k_ == 1))
                    ins = None
                    for p, (vsp, asp) in enumerate((
                            (c.v_hi, c.att_hi), (c.v_hi, c.att_lo), (c.v_lo, c.att_hi))):
                        for dc in range(CC):
                            ins = t.matmul(c.big4_n[:, dc, :],
                                           vsp[:, kb, dc * 128:(dc + 1) * 128],
                                           asp[:, abuf, :],
                                           start=(first and p == 0 and dc % 2 == 0),
                                           stop=(final and p == 2 and dc % 2 == 1))
                    return ins
                av_fns[kb] = mm_av
                prev_kb = kb
            em.wait("tensor", "ev", t_att[kb_max])
            t_av[kb_max] = em.run("tensor", av_fns[kb_max], sem="mm", inc=1)
            t_corr = t_av[kb_max]
            if kb_max < TT - 1:
                em.wait("vector", "mm", t_av[kb_max - 1])
                t_no = em.run("vector", lambda v: v.memset(c.att_hi[:, 0, :], -1.0),
                              sem="ev", inc=1)
                em.wait("tensor", "ev", t_no)
                def mm_corr(t, kb_max=kb_max):
                    ins = None
                    for ckb in range(kb_max + 1, TT):
                        for p, vsp in enumerate((c.v_hi, c.v_lo)):
                            for dc in range(CC):
                                ins = t.matmul(c.big4_n[:, dc, :],
                                               vsp[:, ckb, dc * 128:(dc + 1) * 128],
                                               c.att_hi[:, 0, :],
                                               start=False,
                                               stop=(ckb == TT - 1 and p == 1 and dc % 2 == 1))
                    return ins
                t_corr = em.run("tensor", mm_corr, sem="mm", inc=1)
            t_q_cons = em.tok("mm")
            em.wait("vector", "mm", t_corr)
            def fin_den(v, kb_max=kb_max):
                const = -(T - 128 * (kb_max + 1)) + EPS
                v.tensor_scalar(c.vecs[64:65, 0:QG], c.pC[0:1, 0:QG],
                                const, None, ALU.add)
                v.reciprocal(c.vecs[64:65, 0:QG], c.vecs[64:65, 0:QG])
                return v.tensor_scalar(c.vecs[64:65, 0:QG], c.vecs[64:65, 0:QG],
                                       1.0 / 64.0, None, ALU.mult)
            t_den = em.run("vector", fin_den, sem="ev", inc=1)
            t_den_prev = t_den
            em.wait("tensor", "ev", max(t_den, em.state.get("rb_ev", 0)))
            def mm_rb(t):
                return t.matmul(c.pD[:, 0:QG], c.sels[0:65, 256:384],
                                c.vecs[0:65, 0:QG], start=True, stop=True)
            t_rbm = em.run("tensor", mm_rb, sem="mm", inc=1)
            em.wait("vector", "mm", t_rbm)
            em.wait("vector", "ev", t_y_prev)
            t_rb = em.run("vector", lambda v: v.tensor_copy(c.recipb, c.pD[:, 0:QG]),
                          sem="ev", inc=1)
            em.state["rb_ev"] = t_rb
            em.wait("vector", "mm", t_y_cons)
            def ev_y(v, kb_max=kb_max):
                ins = None
                for dc in range(CC):
                    z = c.scr[:, 1, 0:QG]
                    v.tensor_tensor(z, c.big4_n[:, dc, :], c.recipb, ALU.mult)
                    v.tensor_copy(c.y_hi[:, dc, :], z)
                    ins = v.tensor_sub(c.y_lo[:, dc, :], z, c.y_hi[:, dc, :])
                return ins
            t_y = em.run("vector", ev_y, sem="ev", inc=1)
            t_y_prev = t_y
            if self.vmode & 16:
                em.wait("gpsimd", "ev", t_y)
                yw = self.arena[0:128, self._yhi_off:self._yhi_off + 2048].rearrange(
                    "p (a t) -> p a t", a=1)
                em.run("gpsimd", lambda g_, qg=qg, yw=yw: g_.dma_start(
                    out=xdst.rearrange("(a p) t -> p a t", p=128)[:, qg:qg + 1, :],
                    in_=yw), sem="st", inc=16)
                em.barrier()
                t_y_cons = em.tok("mm")
                t_stg_cons = t_y
                t_o_st = em.tok("st")
                continue
            if self.debug:
                em.wait("gpsimd", "ev", t_y)
                em.run("gpsimd", lambda g, qg=qg: g.dma_start(
                    out=self.dbg_y[qg], in_=c.y_hi), sem="st", inc=16)
            em.wait("tensor", "ev", t_y)
            rwo = {0: load_rw(0, "wo", 0), 1: load_rw(1, "wo", 1)}
            t_evo = {}
            for doc in range(CC):
                slot = doc % 2
                em.wait("tensor", "wl", rwo[doc])
                ps_ = c.pA if doc % 2 == 0 else c.pB
                em.wait("tensor", "ev", t_evo.get(doc - 2, 0))
                def mm_o(t, slot=slot, ps_=ps_):
                    ins = None
                    for p, (whalf, ysp) in enumerate((
                            (c.rw_hi, c.y_hi), (c.rw_hi, c.y_lo), (c.rw_lo, c.y_hi))):
                        for cc in range(CC):
                            ins = t.matmul(ps_[:, 0:QG], whalf[:, slot, cc, :],
                                           ysp[:, cc, :],
                                           start=(p == 0 and cc == 0),
                                           stop=(p == 2 and cc == CC - 1))
                    return ins
                t_mm = em.run("tensor", mm_o, sem="mm", inc=1)
                rw_free[slot] = t_mm
                if doc + 2 < CC:
                    rwo[doc + 2] = load_rw(slot, "wo", doc + 2)
                em.wait("vector", "mm", t_mm)
                em.wait("vector", "st", t_o_st)
                def ev_o(v, doc=doc, ps_=ps_):
                    return v.scalar_tensor_tensor(c.scr[:, doc, :], ps_[:, 0:QG],
                                                  c.bo_sb[:, doc:doc + 1],
                                                  c.xstg[:, doc, :], ALU.add, ALU.add)
                t_evo[doc] = em.run("vector", ev_o, sem="ev", inc=1)
            t_y_cons = em.tok("mm")
            t_stg_cons = t_evo[CC - 1]
            em.wait("gpsimd", "ev", t_evo[CC - 1])
            t_o_st = em.run("gpsimd", lambda g_, sl=sl: g_.dma_start(
                out=xdst.rearrange("(a p) t -> p a t", p=128)[:, :, sl], in_=c.scr),
                sem="st", inc=16)
            if self.paranoid & 1:
                em.barrier()


def build_nc(n_seq=4, debug=False):
    nc = bass.Bass("TRN2", target_bir_lowering=False)
    core = Core(nc, n_seq=n_seq, debug=debug)
    return core.build()




def kernel(L0_attn_x, L0_attn_Wq, L0_attn_Wk, L0_attn_Wv, L0_attn_Wo, L0_attn_bo,
           L0_ffn_w_gate, L0_ffn_w_up, L0_ffn_w_down):
    n_cores = 8
    n_seq_per_core = 4
    x = np.asarray(L0_attn_x)
    B = x.shape[0]
    assert B == n_cores * n_seq_per_core

    xT = np.ascontiguousarray(x.transpose(0, 2, 1))
    wqt = np.ascontiguousarray(np.asarray(L0_attn_Wq).T)
    wkt = np.ascontiguousarray(np.asarray(L0_attn_Wk).T)
    wvt = np.ascontiguousarray(np.asarray(L0_attn_Wv).T)
    wot = np.ascontiguousarray(np.asarray(L0_attn_Wo).T)
    bo = np.ascontiguousarray(np.asarray(L0_attn_bo))
    wgt = np.ascontiguousarray(np.asarray(L0_ffn_w_gate).T)
    wut = np.ascontiguousarray(np.asarray(L0_ffn_w_up).T)
    wdt = np.ascontiguousarray(np.asarray(L0_ffn_w_down).T)

    nc = bass.Bass("TRN2", target_bir_lowering=False)
    core = Core(nc, n_seq=n_seq_per_core)
    nc = core.build()

    in_maps = []
    for ci in range(n_cores):
        in_maps.append({
            "xT": xT[ci * n_seq_per_core:(ci + 1) * n_seq_per_core],
            "wqt": wqt, "wkt": wkt, "wvt": wvt, "wot": wot, "bo": bo,
            "wgt": wgt, "wut": wut, "wdt": wdt,
        })
    res = run_bass_kernel_spmd(nc, in_maps, core_ids=list(range(n_cores)))
    outs = [r["outT"] for r in res.results]
    out = np.concatenate(outs, axis=0).transpose(0, 2, 1)
    return np.ascontiguousarray(out.astype(np.float32))


# version 11 (final)
# speedup vs baseline: 1.0739x; 1.0007x over previous
from contextlib import ExitStack

import numpy as np
import concourse.bass as bass
import concourse.mybir as mybir
from concourse.bass_utils import run_bass_kernel_spmd

F32 = mybir.dt.float32
F16 = mybir.dt.float16
I32 = mybir.dt.int32
AF = mybir.ActivationFunctionType
ALU = mybir.AluOpType

C = 1024
H = 3072
T = 2048
CC = C // 128
HC = H // 128
TT = T // 128
QG = 256
NQG = T // QG
FG = 512
NFG = T // FG
SG = 256
NSG = T // SG
BHC = 3
NB = HC // BHC
NRS = 4
EPS = 1e-5
QSC = 1.0 / 127.0


class Emitter:

    def __init__(self, core, active, eng):
        self.core = core
        self.active = active
        self.eng = eng
        self.cnt = dict.fromkeys(core.semnames, 0)
        self.state = {}

    def run(self, engine, fn, sem=None, inc=0):
        if sem is not None:
            self.cnt[sem] += inc
        if self.active == engine:
            ins = fn(self.eng)
            if sem is not None:
                assert ins is not None, f"fn on {engine} must return last instruction"
                ins.then_inc(self.core.sem[sem], inc)
        return self.cnt[sem] if sem is not None else None

    def wait(self, engine, sem, val):
        if val is None or val <= 0:
            return
        if self.active == engine:
            self.eng.wait_ge(self.core.sem[sem], val)

    def tok(self, sem):
        return self.cnt[sem]

    def barrier(self):
        for sem in self.core.semnames:
            v = self.cnt[sem]
            if v > 0:
                for e in ("sync", "scalar", "vector", "tensor", "gpsimd"):
                    self.wait(e, sem, v)


class Core:
    semnames = ["ld", "wl", "bc", "mm", "ev", "sc", "gp", "st"]

    def __init__(self, nc, n_seq=4, arena_words=53180, debug=False, paranoid=0, phases='faf', vmode=0):
        self.nc = nc
        self.n_seq = n_seq
        self.paranoid = paranoid
        self.phases = phases
        self.vmode = vmode
        es = ExitStack()
        self.es = es
        d = nc.dram_tensor

        self.xin = d("xT", [n_seq, C, T], F32, kind="ExternalInput").ap()
        self.wqt = d("wqt", [C, C], F32, kind="ExternalInput").ap()
        self.wkt = d("wkt", [C, C], F32, kind="ExternalInput").ap()
        self.wvt = d("wvt", [C, C], F32, kind="ExternalInput").ap()
        self.wot = d("wot", [C, C], F32, kind="ExternalInput").ap()
        self.bo = d("bo", [C], F32, kind="ExternalInput").ap()
        self.wgt = d("wgt", [C, H], I32, kind="ExternalInput").ap()
        self.wut = d("wut", [C, H], I32, kind="ExternalInput").ap()
        self.wdt = d("wdt", [H, C], I32, kind="ExternalInput").ap()
        self.out = d("outT", [n_seq, C, T], F32, kind="ExternalOutput").ap()

        dbg = "ExternalOutput" if debug else "Internal"
        self.debug = debug
        self.x1d = d("x1_dram", [C, T], F32, kind=dbg).ap()
        self.x2d = d("x2_dram", [C, T], F32, kind=dbg).ap()
        if debug:
            self.dbg_dacc = d("dbg_dacc", [2, C, T], F32, kind=dbg).ap()
            self.dbg_h = d("dbg_h", [2, C, T], F16, kind=dbg).ap()
            self.dbg_vec = d("dbg_vec", [2, 2, T], F32, kind=dbg).ap()
            self.dbg_bc = d("dbg_bc", [2, 128, 2, SG], F32, kind=dbg).ap()
            self.dbg_k = d("dbg_k", [128, CC, T], F16, kind=dbg).ap()
            self.dbg_v = d("dbg_v", [128, TT, C], F16, kind=dbg).ap()
            self.dbg_den = d("dbg_den", [NQG, QG], F32, kind=dbg).ap()
            self.dbg_y = d("dbg_y", [NQG, 128, CC, QG], F16, kind=dbg).ap()
        self.vecd = d("vec_dram", [8, T], F32).ap()
        self.wg16 = d("wg16_dram", [HC, 128, CC, 128], F16).ap()
        self.wu16 = d("wu16_dram", [HC, 128, CC, 128], F16).ap()
        self.wd16 = d("wd16_dram", [HC, 128, C], F16).ap()
        self.wsp = {}
        for w in ("wq", "wk", "wo"):
            for h in ("hi", "lo"):
                self.wsp[f"{w}_{h}"] = d(f"{w}_{h}_dram", [CC, 128, CC, 128], F16).ap()
        for h in ("hi", "lo"):
            self.wsp[f"wv_{h}"] = d(f"wv_{h}_dram", [4, 128, CC, 256], F16).ap()

        self.arena = es.enter_context(nc.sbuf_tensor("arena", [128, arena_words], F32))
        self._off = 0

        def V(shape, dt=F32, parts=128, at=None):
            n = 1
            for s in shape:
                n *= s
            words = (n + 1) // 2 if dt == F16 else n
            words = (words + 1) & ~1
            off = self._off if at is None else at
            if at is None:
                self._off += words
            ap = self.arena[0:parts, off:off + words]
            ap = ap.bitcast(F16)[:, 0:n] if dt == F16 else ap[:, 0:n]
            if len(shape) > 1:
                letters = list("abcdef")[: len(shape)]
                ap = ap.rearrange(
                    f"p ({' '.join(letters)}) -> p {' '.join(letters)}",
                    **{letters[i]: shape[i] for i in range(len(shape))})
            return ap, off

        self.k_hi, o0 = V([CC, T], F16)
        self.k_lo, o1 = V([CC, T], F16)
        self.v_hi, o2 = V([TT, C], F16)
        self.v_lo, o3 = V([TT, C], F16)
        self.h_hi, _ = V([CC, T], F16, at=o0)
        self.h_lo, _ = V([CC, T], F16, at=o1)
        self.dacc, _ = V([CC, T], F32, at=o2)

        ring0 = self._off
        self.rg, _ = V([NRS, CC, 128], F16)
        self.ru, _ = V([NRS, CC, 128], F16)
        self.rd, _ = V([NRS, C], F16)
        ring_end = self._off
        self.wvr, _ = V([2, 2, CC, 256], F16, at=ring0)
        self.rw_hi, _ = V([2, CC, 128], F16, at=ring0 + 4096)
        self.rw_lo, _ = V([2, CC, 128], F16, at=ring0 + 4096 + 1024)
        self._off = max(ring_end, ring0 + 4096 + 2048)

        work0 = self._off
        self.q_hi, _ = V([CC, QG], F16)
        self.q_lo, _ = V([CC, QG], F16)
        self.y_hi, yo_ = V([CC, QG], F16)
        self.y_lo, _ = V([CC, QG], F16)
        self._yhi_off = yo_
        self.th_hi, _ = V([CC, SG], F16)
        self.th_lo, _ = V([CC, SG], F16)
        self.att_hi, _ = V([2, QG], F16)
        self.att_lo, _ = V([2, QG], F16)
        work_end = self._off
        self.act_hi, _ = V([BHC, FG], F16, at=work0)
        self.act_lo, _ = V([BHC, FG], F16, at=work0 + 1024)
        self._off = work_end

        self.xstg, _ = V([CC, SG], F32)
        self.scr, _ = V([CC, SG], F32)

        self.vecs, _ = V([T], F32, parts=128)
        self.bcast, _ = V([2, SG], F32)
        self.recipb, _ = V([QG], F32)
        self.sels, _ = V([384], F32)
        self.masks, _ = V([2, QG], F16)
        self.ones, _ = V([2], F16)
        self.onesf, _ = V([2], F32)
        self.bo_sb, _ = V([CC], F32)
        print(f"[arena] {self._off} words = {self._off * 4 / 1024:.1f} KiB/partition")
        assert self._off <= arena_words, self._off

        ps = lambda name, shape: es.enter_context(nc.psum_tensor(name, shape, F32))
        self.big4 = ps("big4", [128, 2048])
        self.big4_d = self.big4.ap().rearrange("p (a b) -> p a b", a=4)
        self.big4_n = self.big4.ap().rearrange("p (a b) -> p a b", a=8)
        self.pA = ps("pA", [128, 512])
        self.pB = ps("pB", [128, 512])
        self.pC = ps("pC", [128, 512])
        self.pD = ps("pD", [128, 512])
        self.sem = {s: es.enter_context(nc.semaphore(s)) for s in self.semnames}

    ROW_M, ROW_V, ROW_R, ROW_DEN = 0, 32, 64, 96

    def vrow(self, r, sl=slice(None)):
        return self.vecs[r:r + 1, sl]

    def build(self):
        nc = self.nc
        with self.es, nc.Block() as block:
            for name in ("sync", "scalar", "vector", "tensor", "gpsimd"):
                def mk(engname):
                    def body(eng):
                        em = Emitter(self, engname, eng)
                        self.schedule(em)
                    return body
                getattr(block, name)(mk(name))
        return nc

    def schedule(self, em):
        self.setup(em)
        em.barrier()
        for s in range(self.n_seq):
            if self.phases == 'faf':
                self.ffn_phase(em, self.xin[s], self.x1d, dbg_idx=0 if s == self.n_seq - 1 else None)
                em.barrier()
                self.attn_phase(em, self.x1d, self.x2d)
                em.barrier()
                self.ffn_phase(em, self.x2d, self.out[s], dbg_idx=1 if s == self.n_seq - 1 else None)
            elif self.phases == 'f':
                self.ffn_phase(em, self.xin[s], self.out[s])
            elif self.phases == 'a':
                self.attn_phase(em, self.xin[s], self.out[s])
            elif self.phases == 'a1':
                self.attn_phase(em, self.xin[s], self.out[s], upto=1)
            elif self.phases == 'fa':
                self.ffn_phase(em, self.xin[s], self.x1d)
                em.barrier()
                self.attn_phase(em, self.x1d, self.out[s])
            em.barrier()
        em.run("gpsimd", lambda g: g.wait_ge(self.sem["st"], em.tok("st")) or g.nop())

    def setup(self, em):
        c = self
        em.run("vector", lambda v: v.memset(c.ones, 1.0), sem="ev", inc=1)
        em.run("vector", lambda v: v.memset(c.onesf, 1.0), sem="ev", inc=1)
        em.run("gpsimd", lambda g: g.iota(c.sels, [[0, 384]], channel_multiplier=1,
               allow_small_or_imprecise_dtypes=True), sem="gp", inc=1)
        em.wait("vector", "gp", em.tok("gp"))
        def mk_sels(v):
            v.tensor_scalar(c.sels[:, 0:128], c.sels[:, 0:128], 0.0, None, ALU.is_equal)
            v.tensor_scalar(c.sels[:, 128:256], c.sels[:, 128:256], 32.0, None, ALU.is_equal)
            v.tensor_scalar(c.sels[:, 256:384], c.sels[:, 256:384], 64.0, None, ALU.is_equal)
            return v.memset(c.vecs[0:96, :], 0.0)
        em.run("vector", mk_sels, sem="ev", inc=1)
        em.run("sync", lambda s: s.dma_start(
            out=c.bo_sb, in_=c.bo.rearrange("(d p) -> p d", p=128),
            allow_slow_non_contiguous=True), sem="ld", inc=16)

        def mk_masks(g):
            ins = None
            for var in range(2):
                ins = g.iota(c.masks[:, var, :], [[1, QG]], base=-128 * var,
                             channel_multiplier=-1,
                             allow_small_or_imprecise_dtypes=True)
            return ins
        em.run("gpsimd", mk_masks, sem="gp", inc=1)
        em.wait("vector", "gp", em.tok("gp"))

        def fin_masks(v):
            return v.tensor_scalar(c.masks[:], c.masks[:], -0.5, None, ALU.is_ge)
        em.run("vector", fin_masks, sem="ev", inc=1)

        stgs_i = (c.xstg.bitcast(I32)[:, :, 0:128], c.scr.bitcast(I32)[:, :, 0:128])
        outs_f = (c.th_hi, c.th_lo)
        st_tok = [0, 0]
        idx = 0
        for wsrc, wdst in ((c.wgt, c.wg16), (c.wut, c.wu16)):
            for hc in range(HC):
                p = idx % 2
                idx += 1
                hsl = slice(hc * 128, hc * 128 + 128)
                em.wait("sync", "st", st_tok[p])
                t_ld = em.run("sync", lambda s, wsrc=wsrc, hsl=hsl, p=p: s.dma_start(
                    out=stgs_i[p],
                    in_=wsrc.rearrange("(a p) n -> p a n", p=128)[:, :, hsl]),
                    sem="ld", inc=16)
                em.wait("vector", "ld", t_ld)
                em.wait("vector", "st", st_tok[p])
                t_cv = em.run("vector", lambda v, p=p: v.tensor_copy(
                    outs_f[p][:, :, 0:128], stgs_i[p]), sem="ev", inc=1)
                em.wait("gpsimd", "ev", t_cv)
                st_tok[p] = em.run("gpsimd", lambda g, wdst=wdst, hc=hc, p=p: g.dma_start(
                    out=wdst[hc], in_=outs_f[p][:, :, 0:128]), sem="st", inc=16)
        stgs_i2 = (c.xstg.bitcast(I32)[:, 0:4, :], c.scr.bitcast(I32)[:, 0:4, :])
        for hc in range(HC):
            p = idx % 2
            idx += 1
            em.wait("sync", "st", st_tok[p])
            t_ld = em.run("sync", lambda s, hc=hc, p=p: s.dma_start(
                out=stgs_i2[p],
                in_=c.wdt.rearrange("(a p) n -> p a n", p=128)[:, hc, :]
                    .rearrange("p (a b) -> p a b", a=4)),
                sem="ld", inc=16)
            em.wait("vector", "ld", t_ld)
            em.wait("vector", "st", st_tok[p])
            t_cv = em.run("vector", lambda v, p=p: v.tensor_copy(
                outs_f[p][:, 0:4, :], stgs_i2[p]), sem="ev", inc=1)
            em.wait("gpsimd", "ev", t_cv)
            st_tok[p] = em.run("gpsimd", lambda g, hc=hc, p=p: g.dma_start(
                out=c.wd16[hc],
                in_=outs_f[p][:, 0:4, :].rearrange("p a b -> p (a b)")),
                sem="st", inc=16)

        stgs_f = (c.xstg, c.scr)
        outs_hi = (c.th_hi, c.q_hi)
        outs_lo = (c.th_lo, c.q_lo)
        for wname, wsrc in (("wq", c.wqt), ("wk", c.wkt), ("wo", c.wot), ("wv", c.wvt)):
            ncols = 128 if wname != "wv" else 256
            for blk in range(C // ncols):
                p = idx % 2
                idx += 1
                csl = slice(blk * ncols, (blk + 1) * ncols)
                em.wait("sync", "st", st_tok[p])
                t_ld = em.run("sync", lambda s, wsrc=wsrc, csl=csl, ncols=ncols, p=p: s.dma_start(
                    out=stgs_f[p][:, :, 0:ncols],
                    in_=wsrc.rearrange("(a p) n -> p a n", p=128)[:, :, csl]),
                    sem="ld", inc=16)
                em.wait("scalar", "ld", t_ld)
                em.wait("scalar", "st", st_tok[p])
                wscale = 64.0 if wname == "wo" else 1.0
                t_hi = em.run("scalar", lambda sc, ncols=ncols, wscale=wscale, p=p: sc.activation(
                    outs_hi[p][:, :, 0:ncols], stgs_f[p][:, :, 0:ncols], AF.Copy,
                    scale=wscale), sem="sc", inc=1)
                em.wait("vector", "sc", t_hi)
                t_lo = em.run("vector", lambda v, ncols=ncols, wscale=wscale, p=p: v.scalar_tensor_tensor(
                    outs_lo[p][:, :, 0:ncols], stgs_f[p][:, :, 0:ncols], wscale,
                    outs_hi[p][:, :, 0:ncols], ALU.mult, ALU.subtract), sem="ev", inc=1)
                em.wait("gpsimd", "sc", t_hi)
                em.wait("gpsimd", "ev", t_lo)
                em.run("gpsimd", lambda g, wname=wname, blk=blk, ncols=ncols, p=p: g.dma_start(
                    out=c.wsp[f"{wname}_hi"][blk], in_=outs_hi[p][:, :, 0:ncols]),
                    sem="st", inc=16)
                st_tok[p] = em.run("gpsimd", lambda g, wname=wname, blk=blk, ncols=ncols, p=p: g.dma_start(
                    out=c.wsp[f"{wname}_lo"][blk], in_=outs_lo[p][:, :, 0:ncols]),
                    sem="st", inc=16)

    def ln_stats(self, em, xsrc):
        c = self
        t_cons = 0
        t_statmm = 0
        t_stat_ev = 0
        for g in range(NSG):
            sl = slice(g * SG, (g + 1) * SG)
            em.wait("sync", "sc", t_cons)
            em.wait("sync", "mm", t_statmm)
            t_ld = em.run("sync", lambda s, sl=sl: s.dma_start(
                out=c.xstg, in_=xsrc.rearrange("(a p) t -> p a t", p=128)[:, :, sl]),
                sem="ld", inc=16)
            em.wait("scalar", "ld", t_ld)
            em.wait("scalar", "mm", t_statmm)
            t_cons = em.run("scalar", lambda sc: sc.activation(
                c.scr, c.xstg, AF.Square), sem="sc", inc=1)
            em.wait("tensor", "sc", t_cons)
            em.wait("tensor", "ld", t_ld)
            em.wait("tensor", "ev", t_stat_ev)
            def mm_stats(t):
                ins = None
                for cc in range(CC):
                    ins = t.matmul(c.pC[0:1, 0:SG], c.onesf[:, 0:1], c.xstg[:, cc, :],
                                   start=(cc == 0), stop=(cc == CC - 1))
                for cc in range(CC):
                    ins = t.matmul(c.pD[0:1, 0:SG], c.onesf[:, 0:1], c.scr[:, cc, :],
                                   start=(cc == 0), stop=(cc == CC - 1))
                return ins
            t_mm = em.run("tensor", mm_stats, sem="mm", inc=1)
            t_statmm = t_mm
            em.wait("vector", "mm", t_mm)
            t_stat_ev = em.run("vector", lambda v, sl=sl: (
                v.tensor_copy(c.vrow(self.ROW_M, sl), c.pC[0:1, 0:SG]),
                v.tensor_copy(c.vrow(self.ROW_V, sl), c.pD[0:1, 0:SG]))[-1],
                sem="ev", inc=1)
        em.state["ln_xstg_free"] = t_cons
        em.state["ln_stats_mm"] = t_statmm
        return t_stat_ev

    def bcast_mr(self, em, g, t_mr, t_free, width=SG):
        c = self
        sl = slice(g * width, g * width + width)
        em.wait("tensor", "ev", max(t_mr, em.state.get("bc_ev", 0),
                                    em.state.get("rb_ev", 0)))
        def mm_bc(t, sl=sl, width=width):
            t.matmul(c.pD[:, 0:width], c.sels[0:33, 0:128], c.vecs[0:33, sl],
                     start=True, stop=False)
            return t.matmul(c.pD[:, 256:256 + width], c.sels[0:33, 128:256],
                            c.vecs[0:33, sl], start=False, stop=True)
        t_bm = em.run("tensor", mm_bc, sem="mm", inc=1)
        em.wait("vector", "mm", t_bm)
        em.wait("vector", "ev", t_free)
        t_ev = em.run("vector", lambda v, width=width: (
            v.tensor_copy(c.bcast[:, 0, 0:width], c.pD[:, 0:width]),
            v.tensor_copy(c.bcast[:, 1, 0:width], c.pD[:, 256:256 + width]))[-1],
            sem="ev", inc=1)
        em.state["bc_ev"] = t_ev
        return t_ev

    def h_slice(self, em, dst_hi, dst_lo, width=SG):
        c = self
        em.wait("vector", "mm", em.state.get("ln_stats_mm", 0))
        def mk1(v):
            b0 = c.bcast[:, 0, 0:width]
            b1 = c.bcast[:, 1, 0:width]
            t = c.scr[:, 7, 0:width]
            v.tensor_scalar(b0, b0, 1.0 / C, None, ALU.mult)
            v.tensor_tensor(t, b0, b0, ALU.mult)
            v.scalar_tensor_tensor(b1, b1, 1.0 / C, t, ALU.mult, ALU.subtract)
            v.tensor_scalar(b1, b1, EPS, None, ALU.add)
            return v.reciprocal(b1, b1)
        t1 = em.run("vector", mk1, sem="ev", inc=1)
        em.wait("scalar", "ev", t1)
        t2 = em.run("scalar", lambda sc, width=width: sc.activation(
            c.bcast[:, 1, 0:width], c.bcast[:, 1, 0:width], AF.Sqrt),
            sem="sc", inc=1)
        em.wait("vector", "sc", t2)
        def mk2(v):
            v.tensor_tensor(c.scr[:, :, 0:width], c.xstg[:, :, 0:width],
                            c.bcast[:, 0:1, 0:width].broadcast_to((128, CC, width)),
                            ALU.subtract)
            v.tensor_tensor(c.scr[:, :, 0:width], c.scr[:, :, 0:width],
                            c.bcast[:, 1:2, 0:width].broadcast_to((128, CC, width)),
                            ALU.mult)
            v.tensor_copy(dst_hi, c.scr[:, :, 0:width])
            return v.tensor_sub(dst_lo, c.scr[:, :, 0:width], dst_hi)
        return em.run("vector", mk2, sem="ev", inc=1)

    def ffn_phase(self, em, xsrc, xdst, dbg_idx=None):
        c = self
        t_mr = self.ln_stats(em, xsrc)
        if self.debug and dbg_idx is not None:
            em.wait("gpsimd", "st", t_mr)
            em.run("gpsimd", lambda g, dbg_idx=dbg_idx: g.dma_start(
                out=self.dbg_vec[dbg_idx], in_=c.vecd[0:2, :]), sem="st", inc=16)
        t_h = 0
        for g in range(NSG):
            sl = slice(g * SG, (g + 1) * SG)
            em.wait("sync", "ev", t_h)
            em.wait("sync", "sc", em.state["ln_xstg_free"])
            em.wait("sync", "mm", em.state["ln_stats_mm"])
            t_ld = em.run("sync", lambda s, sl=sl: s.dma_start(
                out=c.xstg, in_=xsrc.rearrange("(a p) t -> p a t", p=128)[:, :, sl]),
                sem="ld", inc=16)
            t_bc = self.bcast_mr(em, g, t_mr, t_h)
            em.wait("vector", "ld", t_ld)
            t_h = self.h_slice(em, c.h_hi[:, :, sl], c.h_lo[:, :, sl])
            if self.paranoid & 16:
                em.barrier()
        if self.debug and dbg_idx is not None:
            em.wait("gpsimd", "ev", t_h)
            em.run("gpsimd", lambda g, dbg_idx=dbg_idx: g.dma_start(
                out=self.dbg_bc[dbg_idx], in_=c.bcast), sem="st", inc=16)

        slot_free = {}
        t_wl = {}

        def load_slot(slot, hc):
            em.wait("sync", "mm", slot_free.get(slot, 0))
            em.run("sync", lambda s: s.dma_start(
                out=c.rg[:, slot], in_=c.wg16[hc]), sem="wl", inc=16)
            em.run("sync", lambda s: s.dma_start(
                out=c.ru[:, slot], in_=c.wu16[hc]), sem="wl", inc=16)
            return em.run("sync", lambda s: s.dma_start(
                out=c.rd[:, slot], in_=c.wd16[hc]), sem="wl", inc=16)

        for i in range(BHC):
            t_wl[(0, i)] = load_slot(i % NRS, i)

        em.wait("tensor", "ev", t_h)
        t_act_prev = 0
        t_down_mm_prev = 0
        t_down_ev_prev = 0
        for b in range(NB):
            for tg in range(NFG):
                sl = slice(tg * FG, (tg + 1) * FG)
                t_act = {}
                for i in range(BHC):
                    slot = (b * BHC + i) % NRS
                    em.wait("tensor", "wl", t_wl[(b, i)])
                    pg, pu = (c.pA, c.pB) if i % 2 == 0 else (c.pC, c.pD)
                    em.wait("tensor", "ev", t_act.get(i - 2, t_act_prev))
                    def mm_gu(t, slot=slot, sl=sl, pg=pg, pu=pu):
                        ins = None
                        for src, ps_ in ((c.rg, pg), (c.ru, pu)):
                            for p, hsp in enumerate((c.h_hi, c.h_lo)):
                                for cc in range(CC):
                                    ins = t.matmul(ps_[:, 0:FG], src[:, slot, cc, :],
                                                   hsp[:, cc, sl],
                                                   start=(p == 0 and cc == 0),
                                                   stop=(p == 1 and cc == CC - 1))
                        return ins
                    t_mm = em.run("tensor", mm_gu, sem="mm", inc=1)
                    em.wait("scalar", "mm", t_mm)
                    t_si = em.run("scalar", lambda sc, pg=pg: sc.activation(
                        c.scr[:, 0:2, :].rearrange("p a b -> p (a b)"), pg[:, 0:FG],
                        AF.Silu, scale=QSC), sem="sc", inc=1)
                    em.wait("vector", "sc", t_si)
                    em.wait("vector", "mm", t_down_mm_prev)
                    def mk_act(v, i=i, pu=pu):
                        z = c.scr[:, 2:4, :].rearrange("p a b -> p (a b)")
                        silu = c.scr[:, 0:2, :].rearrange("p a b -> p (a b)")
                        v.scalar_tensor_tensor(z, pu[:, 0:FG], QSC, silu, ALU.mult, ALU.mult)
                        v.tensor_copy(c.act_hi[:, i, :], z)
                        return v.tensor_sub(c.act_lo[:, i, :], z, c.act_hi[:, i, :])
                    t_act[i] = em.run("vector", mk_act, sem="ev", inc=1)
                t_act_prev = t_act[BHC - 1]
                for dh in range(2):
                    em.wait("tensor", "ev", t_down_ev_prev)
                    def mm_down_i(t, b=b, dh=dh, i=0, sub=False):
                        ins = None
                        rng = (i,) if sub else range(BHC)
                        for ii in rng:
                            slot = (b * BHC + ii) % NRS
                            for p, asp in enumerate((c.act_hi, c.act_lo)):
                                for dc in range(4):
                                    dsl = slice(dh * 512 + dc * 128,
                                                dh * 512 + (dc + 1) * 128)
                                    ins = t.matmul(c.big4_d[:, dc, 0:FG],
                                                   c.rd[:, slot, dsl], asp[:, ii, :],
                                                   start=(ii == 0 and p == 0),
                                                   stop=(ii == BHC - 1 and p == 1))
                        return ins
                    if dh == 0:
                        for i in range(BHC):
                            em.wait("tensor", "ev", t_act[i])
                            t_down_mm_prev = em.run(
                                "tensor",
                                (lambda ii, bb=b, dd=dh: (
                                    lambda t: mm_down_i(t, b=bb, dh=dd, i=ii, sub=True)))(i),
                                sem="mm", inc=1)
                    else:
                        t_down_mm_prev = em.run("tensor", mm_down_i, sem="mm", inc=1)
                    em.wait("vector", "mm", t_down_mm_prev)
                    def ev_down(v, b=b, dh=dh, sl=sl):
                        dst = c.dacc[:, dh * 4:(dh + 1) * 4, sl]
                        src = c.big4_d[:, :, 0:FG]
                        if b == 0:
                            return v.tensor_copy(dst, src)
                        return v.tensor_tensor(dst, dst, src, ALU.add)
                    t_down_ev_prev = em.run("vector", ev_down, sem="ev", inc=1)
                if tg == NFG - 1:
                    for i in range(BHC):
                        slot_free[(b * BHC + i) % NRS] = t_down_mm_prev
                if self.paranoid & 8:
                    em.barrier()
            if self.paranoid & 4:
                em.barrier()
            if b + 1 < NB:
                for i in range(BHC):
                    hc = (b + 1) * BHC + i
                    t_wl[(b + 1, i)] = load_slot(hc % NRS, hc)

        if self.debug and dbg_idx is not None:
            em.wait("gpsimd", "ev", t_down_ev_prev)
            em.run("gpsimd", lambda g, dbg_idx=dbg_idx: g.dma_start(
                out=self.dbg_dacc[dbg_idx].rearrange("(a p) t -> p a t", p=128),
                in_=c.dacc), sem="st", inc=16)
            em.run("gpsimd", lambda g, dbg_idx=dbg_idx: g.dma_start(
                out=self.dbg_h[dbg_idx].rearrange("(a p) t -> p a t", p=128),
                in_=c.h_hi), sem="st", inc=16)
        t_f = t_down_ev_prev
        t_fin_st = em.tok("st")
        for g in range(NSG):
            sl = slice(g * SG, (g + 1) * SG)
            em.wait("sync", "ev", t_f)
            t_ld = em.run("sync", lambda s, sl=sl: s.dma_start(
                out=c.xstg, in_=xsrc.rearrange("(a p) t -> p a t", p=128)[:, :, sl]),
                sem="ld", inc=16)
            em.wait("vector", "ld", t_ld)
            em.wait("vector", "ev", t_down_ev_prev)
            em.wait("vector", "st", t_fin_st)
            t_f = em.run("vector", lambda v, sl=sl: v.scalar_tensor_tensor(
                c.scr, c.dacc[:, :, sl], 0.5 * QSC, c.xstg, ALU.mult, ALU.add),
                sem="ev", inc=1)
            em.wait("gpsimd", "ev", t_f)
            t_fin_st = em.run("gpsimd", lambda g_, sl=sl: g_.dma_start(
                out=xdst.rearrange("(a p) t -> p a t", p=128)[:, :, sl], in_=c.scr),
                sem="st", inc=16)

    def attn_phase(self, em, xsrc, xdst, upto=99):
        c = self
        t_mr = self.ln_stats(em, xsrc)

        rw_free = {0: 0, 1: 0}
        wv_free = {0: 0, 1: 0}

        def load_wv(slot, dq):
            em.wait("sync", "mm", wv_free[slot])
            em.run("sync", lambda s: s.dma_start(
                out=c.wvr[:, slot, 0], in_=c.wsp["wv_hi"][dq]), sem="wl", inc=16)
            return em.run("sync", lambda s: s.dma_start(
                out=c.wvr[:, slot, 1], in_=c.wsp["wv_lo"][dq]), sem="wl", inc=16)

        def load_rw(slot, wname, blk):
            em.wait("sync", "mm", rw_free[slot])
            em.run("sync", lambda s: s.dma_start(
                out=c.rw_hi[:, slot], in_=c.wsp[f"{wname}_hi"][blk]), sem="wl", inc=16)
            return em.run("sync", lambda s: s.dma_start(
                out=c.rw_lo[:, slot], in_=c.wsp[f"{wname}_lo"][blk]), sem="wl", inc=16)

        first_vb = True
        t_h = 0
        t_th_cons = 0
        for g in range(NSG):
            sl = slice(g * SG, (g + 1) * SG)
            em.wait("scalar", "ev", t_h)
            em.wait("scalar", "sc", em.state["ln_xstg_free"])
            em.wait("scalar", "mm", em.state["ln_stats_mm"])
            t_ld = em.run("scalar", lambda s, sl=sl: s.dma_start(
                out=c.xstg, in_=xsrc.rearrange("(a p) t -> p a t", p=128)[:, :, sl]),
                sem="ld", inc=16)
            t_bc = self.bcast_mr(em, g, t_mr, t_h)
            em.wait("vector", "ld", t_ld)
            em.wait("vector", "mm", t_th_cons)
            t_h = self.h_slice(em, c.th_hi, c.th_lo)
            em.wait("tensor", "ev", t_h)
            rw_tok = {0: load_rw(0, "wk", 0), 1: load_rw(1, "wk", 1)}
            t_evk = {}
            for dkc in range(CC):
                slot = dkc % 2
                em.wait("tensor", "wl", rw_tok[dkc])
                ps_ = c.pA if dkc % 2 == 0 else c.pB
                em.wait("tensor", "ev", t_evk.get(dkc - 2, 0))
                def mm_k(t, slot=slot, ps_=ps_):
                    ins = None
                    for p, (whalf, hsp) in enumerate((
                            (c.rw_hi, c.th_hi), (c.rw_hi, c.th_lo), (c.rw_lo, c.th_hi))):
                        for cc in range(CC):
                            ins = t.matmul(ps_[:, 0:SG], whalf[:, slot, cc, :],
                                           hsp[:, cc, :],
                                           start=(p == 0 and cc == 0),
                                           stop=(p == 2 and cc == CC - 1))
                    return ins
                t_mm = em.run("tensor", mm_k, sem="mm", inc=1)
                rw_free[slot] = t_mm
                if dkc + 2 < CC:
                    rw_tok[dkc + 2] = load_rw(slot, "wk", dkc + 2)
                em.wait("vector", "mm", t_mm)
                def ev_k(v, dkc=dkc, sl=sl, ps_=ps_):
                    v.tensor_scalar(c.k_hi[:, dkc, sl], ps_[:, 0:SG], 0.1, None, ALU.mult)
                    return v.scalar_tensor_tensor(c.k_lo[:, dkc, sl], ps_[:, 0:SG], 0.1,
                                                  c.k_hi[:, dkc, sl], ALU.mult, ALU.subtract)
                t_evk[dkc] = em.run("vector", ev_k, sem="ev", inc=1)
            t_th_cons = em.tok("mm")
            t_evv = {}
            wv_tok = {0: load_wv(0, 0), 1: load_wv(1, 1)}
            for dq in range(4):
                wslot = dq % 2
                for ti in range(2):
                    tt = g * 2 + ti
                    idx = dq * 2 + ti
                    ps_ = c.pA if idx % 2 == 0 else c.pB
                    em.wait("tensor", "wl", wv_tok[dq])
                    em.wait("tensor", "ev", t_evv.get(idx - 2, t_evk[CC - 1]))
                    def mm_v(t, ti=ti, wslot=wslot, ps_=ps_):
                        ins = None
                        for p, (hsp, wh) in enumerate(((c.th_hi, 0), (c.th_lo, 0),
                                                       (c.th_hi, 1))):
                            for cc in range(CC):
                                ins = t.matmul(ps_[:, 0:256],
                                               hsp[:, cc, ti * 128:(ti + 1) * 128],
                                               c.wvr[:, wslot, wh, cc, :],
                                               start=(p == 0 and cc == 0),
                                               stop=(p == 2 and cc == CC - 1))
                        return ins
                    t_mm = em.run("tensor", mm_v, sem="mm", inc=1)
                    wv_free[wslot] = t_mm
                    if ti == 1 and dq + 2 < 4:
                        wv_tok[dq + 2] = load_wv(wslot, dq + 2)
                    em.wait("vector", "mm", t_mm)
                    def ev_v(v, tt=tt, dq=dq, ps_=ps_):
                        dsl = slice(dq * 256, (dq + 1) * 256)
                        v.tensor_copy(c.v_hi[:, tt, dsl], ps_[:, 0:256])
                        return v.tensor_sub(c.v_lo[:, tt, dsl], ps_[:, 0:256],
                                            c.v_hi[:, tt, dsl])
                    t_evv[idx] = em.run("vector", ev_v, sem="ev", inc=1)
            t_th_cons = em.tok("mm")
            if self.paranoid & 2:
                em.barrier()
        t_vrem = em.tok("ev")
        if upto == 1:
            em.wait("gpsimd", "ev", t_vrem)
            flat = xdst.rearrange("(a p) t -> p a t", p=128)
            for i in range(4):
                src = self.arena[0:128, 8192 * i:8192 * i + 4096].rearrange(
                    "p (a t) -> p a t", a=2)
                em.run("gpsimd", lambda g, i=i, src=src: g.dma_start(
                    out=flat[:, 2 * i:2 * i + 2, :], in_=src), sem="st", inc=16)
            return
        if self.debug:
            em.wait("gpsimd", "ev", t_vrem)
            em.run("gpsimd", lambda g: g.dma_start(out=self.dbg_k, in_=c.k_hi),
                   sem="st", inc=16)
            em.run("gpsimd", lambda g: g.dma_start(out=self.dbg_v, in_=c.v_hi),
                   sem="st", inc=16)

        t_q_cons = 0
        t_y_cons = 0
        t_stg_cons = 0
        t_att = {}
        t_av = {}
        t_den_prev = 0
        t_y_prev = 0
        t_o_st = em.tok("st")
        for qg in range(NQG):
            sl = slice(qg * QG, (qg + 1) * QG)
            kb_max = 2 * qg + 1
            em.wait("scalar", "ev", max(t_stg_cons, t_h))
            t_ld = em.run("scalar", lambda s, sl=sl: s.dma_start(
                out=c.xstg, in_=xsrc.rearrange("(a p) t -> p a t", p=128)[:, :, sl]),
                sem="ld", inc=16)
            t_bc = self.bcast_mr(em, qg, t_mr, max(t_stg_cons, t_h))
            em.wait("vector", "ld", t_ld)
            em.wait("vector", "mm", t_th_cons)
            em.wait("vector", "st", t_o_st)
            t_h = self.h_slice(em, c.th_hi, c.th_lo)
            em.wait("tensor", "ev", t_h)
            rwq = {0: load_rw(0, "wq", 0), 1: load_rw(1, "wq", 1)}
            t_evq = {}
            for dqc in range(CC):
                slot = dqc % 2
                em.wait("tensor", "wl", rwq[dqc])
                ps_ = c.pA if dqc % 2 == 0 else c.pB
                em.wait("tensor", "ev", t_evq.get(dqc - 2, 0))
                def mm_q(t, slot=slot, ps_=ps_):
                    ins = None
                    for p, (whalf, hsp) in enumerate((
                            (c.rw_hi, c.th_hi), (c.rw_hi, c.th_lo), (c.rw_lo, c.th_hi))):
                        for cc in range(CC):
                            ins = t.matmul(ps_[:, 0:QG], whalf[:, slot, cc, :],
                                           hsp[:, cc, :],
                                           start=(p == 0 and cc == 0),
                                           stop=(p == 2 and cc == CC - 1))
                    return ins
                t_mm = em.run("tensor", mm_q, sem="mm", inc=1)
                rw_free[slot] = t_mm
                if dqc + 2 < CC:
                    rwq[dqc + 2] = load_rw(slot, "wq", dqc + 2)
                em.wait("vector", "mm", t_mm)
                em.wait("vector", "mm", t_q_cons)
                def ev_q(v, dqc=dqc, ps_=ps_):
                    v.tensor_scalar(c.q_hi[:, dqc, :], ps_[:, 0:QG], 0.1, None, ALU.mult)
                    return v.scalar_tensor_tensor(c.q_lo[:, dqc, :], ps_[:, 0:QG], 0.1,
                                                  c.q_hi[:, dqc, :], ALU.mult, ALU.subtract)
                t_evq[dqc] = em.run("vector", ev_q, sem="ev", inc=1)
            t_th_cons = em.tok("mm")
            if self.vmode & 32:
                em.barrier()
            em.wait("tensor", "ev", max(t_evq[CC - 1], t_y_prev, t_den_prev, t_vrem))
            av_fns = {}
            prev_kb = None
            for kb in range(kb_max + 1):
                ps_ = c.pA if kb % 2 == 0 else c.pB
                abuf = kb % 2
                em.wait("tensor", "ev", t_att.get(kb - 2, 0))
                def mm_s(t, kb=kb, ps_=ps_):
                    ins = None
                    for p, (ksp, qsp) in enumerate((
                            (c.k_hi, c.q_hi), (c.k_hi, c.q_lo), (c.k_lo, c.q_hi))):
                        for dkc in range(CC):
                            ins = t.matmul(ps_[:, 0:QG],
                                           ksp[:, dkc, kb * 128:(kb + 1) * 128],
                                           qsp[:, dkc, :],
                                           start=(p == 0 and dkc == 0),
                                           stop=(p == 2 and dkc == CC - 1))
                    return ins
                t_mm = em.run("tensor", mm_s, sem="mm", inc=1)
                if prev_kb is not None:
                    em.wait("tensor", "ev", t_att[prev_kb])
                    t_av[prev_kb] = em.run("tensor", av_fns[prev_kb], sem="mm", inc=1)
                em.wait("vector", "mm", max(t_mm, t_av.get(kb - 2, 0)))
                if kb >= 2 * qg:
                    var = kb - 2 * qg
                    def ev_att(v, ps_=ps_, abuf=abuf, var=var):
                        z = c.scr[:, 0, 0:QG]
                        v.tensor_scalar(z, ps_[:, 0:QG], 0.01, 1.0, ALU.mult, ALU.add)
                        v.tensor_tensor(z, z, c.masks[:, var, :], ALU.mult)
                        v.tensor_scalar(c.att_hi[:, abuf, :], z, -1.0, None, ALU.add)
                        return v.scalar_tensor_tensor(c.att_lo[:, abuf, :], z, -1.0,
                                                      c.att_hi[:, abuf, :],
                                                      ALU.add, ALU.subtract)
                else:
                    def ev_att(v, ps_=ps_, abuf=abuf):
                        v.tensor_scalar(c.att_hi[:, abuf, :], ps_[:, 0:QG], 0.01,
                                        None, ALU.mult)
                        return v.scalar_tensor_tensor(c.att_lo[:, abuf, :], ps_[:, 0:QG],
                                                      0.01, c.att_hi[:, abuf, :],
                                                      ALU.mult, ALU.subtract)
                t_att[kb] = em.run("vector", ev_att, sem="ev", inc=1)
                last = (kb == kb_max)
                final = (kb == TT - 1)
                def mm_av(t, kb=kb, abuf=abuf, first=(kb == 0), last=last, final=final):
                    for k_, asp in enumerate((c.att_hi, c.att_lo)):
                        t.matmul(c.pC[0:1, 0:QG], c.ones[:, 0:1], asp[:, abuf, :],
                                 start=(first and k_ == 0), stop=(last and k_ == 1))
                    ins = None
                    for p, (vsp, asp) in enumerate((
                            (c.v_hi, c.att_hi), (c.v_hi, c.att_lo), (c.v_lo, c.att_hi))):
                        for dc in range(CC):
                            ins = t.matmul(c.big4_n[:, dc, :],
                                           vsp[:, kb, dc * 128:(dc + 1) * 128],
                                           asp[:, abuf, :],
                                           start=(first and p == 0 and dc % 2 == 0),
                                           stop=(final and p == 2 and dc % 2 == 1))
                    return ins
                av_fns[kb] = mm_av
                prev_kb = kb
            em.wait("tensor", "ev", t_att[kb_max])
            t_av[kb_max] = em.run("tensor", av_fns[kb_max], sem="mm", inc=1)
            t_corr = t_av[kb_max]
            if kb_max < TT - 1:
                em.wait("vector", "mm", t_av[kb_max - 1])
                t_no = em.run("vector", lambda v: v.memset(c.att_hi[:, 0, :], -1.0),
                              sem="ev", inc=1)
                em.wait("tensor", "ev", t_no)
                def mm_corr(t, kb_max=kb_max):
                    ins = None
                    for ckb in range(kb_max + 1, TT):
                        for p, vsp in enumerate((c.v_hi, c.v_lo)):
                            for dc in range(CC):
                                ins = t.matmul(c.big4_n[:, dc, :],
                                               vsp[:, ckb, dc * 128:(dc + 1) * 128],
                                               c.att_hi[:, 0, :],
                                               start=False,
                                               stop=(ckb == TT - 1 and p == 1 and dc % 2 == 1))
                    return ins
                t_corr = em.run("tensor", mm_corr, sem="mm", inc=1)
            t_q_cons = em.tok("mm")
            em.wait("vector", "mm", t_corr)
            def fin_den(v, kb_max=kb_max):
                const = -(T - 128 * (kb_max + 1)) + EPS
                v.tensor_scalar(c.vecs[64:65, 0:QG], c.pC[0:1, 0:QG],
                                const, None, ALU.add)
                v.reciprocal(c.vecs[64:65, 0:QG], c.vecs[64:65, 0:QG])
                return v.tensor_scalar(c.vecs[64:65, 0:QG], c.vecs[64:65, 0:QG],
                                       1.0 / 64.0, None, ALU.mult)
            t_den = em.run("vector", fin_den, sem="ev", inc=1)
            t_den_prev = t_den
            em.wait("tensor", "ev", max(t_den, em.state.get("rb_ev", 0)))
            def mm_rb(t):
                return t.matmul(c.pD[:, 0:QG], c.sels[0:65, 256:384],
                                c.vecs[0:65, 0:QG], start=True, stop=True)
            t_rbm = em.run("tensor", mm_rb, sem="mm", inc=1)
            em.wait("vector", "mm", t_rbm)
            em.wait("vector", "ev", t_y_prev)
            t_rb = em.run("vector", lambda v: v.tensor_copy(c.recipb, c.pD[:, 0:QG]),
                          sem="ev", inc=1)
            em.state["rb_ev"] = t_rb
            em.wait("vector", "mm", t_y_cons)
            def ev_y(v, kb_max=kb_max):
                ins = None
                for dc in range(CC):
                    z = c.scr[:, 1, 0:QG]
                    v.tensor_tensor(z, c.big4_n[:, dc, :], c.recipb, ALU.mult)
                    v.tensor_copy(c.y_hi[:, dc, :], z)
                    ins = v.tensor_sub(c.y_lo[:, dc, :], z, c.y_hi[:, dc, :])
                return ins
            t_y = em.run("vector", ev_y, sem="ev", inc=1)
            t_y_prev = t_y
            if self.vmode & 16:
                em.wait("gpsimd", "ev", t_y)
                yw = self.arena[0:128, self._yhi_off:self._yhi_off + 2048].rearrange(
                    "p (a t) -> p a t", a=1)
                em.run("gpsimd", lambda g_, qg=qg, yw=yw: g_.dma_start(
                    out=xdst.rearrange("(a p) t -> p a t", p=128)[:, qg:qg + 1, :],
                    in_=yw), sem="st", inc=16)
                em.barrier()
                t_y_cons = em.tok("mm")
                t_stg_cons = t_y
                t_o_st = em.tok("st")
                continue
            if self.debug:
                em.wait("gpsimd", "ev", t_y)
                em.run("gpsimd", lambda g, qg=qg: g.dma_start(
                    out=self.dbg_y[qg], in_=c.y_hi), sem="st", inc=16)
            em.wait("tensor", "ev", t_y)
            rwo = {0: load_rw(0, "wo", 0), 1: load_rw(1, "wo", 1)}
            t_evo = {}
            for doc in range(CC):
                slot = doc % 2
                em.wait("tensor", "wl", rwo[doc])
                ps_ = c.pA if doc % 2 == 0 else c.pB
                em.wait("tensor", "ev", t_evo.get(doc - 2, 0))
                def mm_o(t, slot=slot, ps_=ps_):
                    ins = None
                    for p, (whalf, ysp) in enumerate((
                            (c.rw_hi, c.y_hi), (c.rw_hi, c.y_lo), (c.rw_lo, c.y_hi))):
                        for cc in range(CC):
                            ins = t.matmul(ps_[:, 0:QG], whalf[:, slot, cc, :],
                                           ysp[:, cc, :],
                                           start=(p == 0 and cc == 0),
                                           stop=(p == 2 and cc == CC - 1))
                    return ins
                t_mm = em.run("tensor", mm_o, sem="mm", inc=1)
                rw_free[slot] = t_mm
                if doc + 2 < CC:
                    rwo[doc + 2] = load_rw(slot, "wo", doc + 2)
                em.wait("vector", "mm", t_mm)
                em.wait("vector", "st", t_o_st)
                def ev_o(v, doc=doc, ps_=ps_):
                    return v.scalar_tensor_tensor(c.scr[:, doc, :], ps_[:, 0:QG],
                                                  c.bo_sb[:, doc:doc + 1],
                                                  c.xstg[:, doc, :], ALU.add, ALU.add)
                t_evo[doc] = em.run("vector", ev_o, sem="ev", inc=1)
            t_y_cons = em.tok("mm")
            t_stg_cons = t_evo[CC - 1]
            em.wait("gpsimd", "ev", t_evo[CC - 1])
            t_o_st = em.run("gpsimd", lambda g_, sl=sl: g_.dma_start(
                out=xdst.rearrange("(a p) t -> p a t", p=128)[:, :, sl], in_=c.scr),
                sem="st", inc=16)
            if self.paranoid & 1:
                em.barrier()


def build_nc(n_seq=4, debug=False):
    nc = bass.Bass("TRN2", target_bir_lowering=False)
    core = Core(nc, n_seq=n_seq, debug=debug)
    return core.build()




def kernel(L0_attn_x, L0_attn_Wq, L0_attn_Wk, L0_attn_Wv, L0_attn_Wo, L0_attn_bo,
           L0_ffn_w_gate, L0_ffn_w_up, L0_ffn_w_down):
    n_cores = 8
    n_seq_per_core = 4
    x = np.asarray(L0_attn_x)
    B = x.shape[0]
    assert B == n_cores * n_seq_per_core

    xT = np.ascontiguousarray(x.transpose(0, 2, 1))
    wqt = np.ascontiguousarray(np.asarray(L0_attn_Wq).T)
    wkt = np.ascontiguousarray(np.asarray(L0_attn_Wk).T)
    wvt = np.ascontiguousarray(np.asarray(L0_attn_Wv).T)
    wot = np.ascontiguousarray(np.asarray(L0_attn_Wo).T)
    bo = np.ascontiguousarray(np.asarray(L0_attn_bo))
    wgt = np.ascontiguousarray(np.asarray(L0_ffn_w_gate).T)
    wut = np.ascontiguousarray(np.asarray(L0_ffn_w_up).T)
    wdt = np.ascontiguousarray(np.asarray(L0_ffn_w_down).T)

    nc = bass.Bass("TRN2", target_bir_lowering=False)
    core = Core(nc, n_seq=n_seq_per_core)
    nc = core.build()

    in_maps = []
    for ci in range(n_cores):
        in_maps.append({
            "xT": xT[ci * n_seq_per_core:(ci + 1) * n_seq_per_core],
            "wqt": wqt, "wkt": wkt, "wvt": wvt, "wot": wot, "bo": bo,
            "wgt": wgt, "wut": wut, "wdt": wdt,
        })
    res = run_bass_kernel_spmd(nc, in_maps, core_ids=list(range(n_cores)))
    outs = [r["outT"] for r in res.results]
    out = np.concatenate(outs, axis=0).transpose(0, 2, 1)
    return np.ascontiguousarray(out.astype(np.float32))


# version 12
# speedup vs baseline: 1.0920x; 1.0169x over previous
from contextlib import ExitStack

import numpy as np
import concourse.bass as bass
import concourse.mybir as mybir
from concourse.bass_utils import run_bass_kernel_spmd

F32 = mybir.dt.float32
F16 = mybir.dt.float16
I32 = mybir.dt.int32
AF = mybir.ActivationFunctionType
ALU = mybir.AluOpType

C = 1024
H = 3072
T = 2048
CC = C // 128
HC = H // 128
TT = T // 128
QG = 256
NQG = T // QG
FG = 512
NFG = T // FG
SG = 256
NSG = T // SG
BHC = 3
NB = HC // BHC
NRS = 4
EPS = 1e-5
QSC = 1.0 / 127.0


class Emitter:

    def __init__(self, core, active, eng):
        self.core = core
        self.active = active
        self.eng = eng
        self.cnt = dict.fromkeys(core.semnames, 0)
        self.state = {}

    def run(self, engine, fn, sem=None, inc=0):
        if sem is not None:
            self.cnt[sem] += inc
        if self.active == engine:
            ins = fn(self.eng)
            if sem is not None:
                assert ins is not None, f"fn on {engine} must return last instruction"
                ins.then_inc(self.core.sem[sem], inc)
        return self.cnt[sem] if sem is not None else None

    def wait(self, engine, sem, val):
        if val is None or val <= 0:
            return
        if self.active == engine:
            self.eng.wait_ge(self.core.sem[sem], val)

    def tok(self, sem):
        return self.cnt[sem]

    def barrier(self):
        for sem in self.core.semnames:
            v = self.cnt[sem]
            if v > 0:
                for e in ("sync", "scalar", "vector", "tensor", "gpsimd"):
                    self.wait(e, sem, v)


class Core:
    semnames = ["ld", "wl", "bc", "mm", "ev", "sc", "gp", "st"]

    def __init__(self, nc, n_seq=4, arena_words=53180, debug=False, paranoid=0, phases='faf', vmode=0):
        self.nc = nc
        self.n_seq = n_seq
        self.paranoid = paranoid
        self.phases = phases
        self.vmode = vmode
        es = ExitStack()
        self.es = es
        d = nc.dram_tensor

        self.xin = d("xT", [n_seq, C, T], F32, kind="ExternalInput").ap()
        self.wqt = d("wqt", [C, C], F32, kind="ExternalInput").ap()
        self.wkt = d("wkt", [C, C], F32, kind="ExternalInput").ap()
        self.wvt = d("wvt", [C, C], F32, kind="ExternalInput").ap()
        self.wot = d("wot", [C, C], F32, kind="ExternalInput").ap()
        self.bo = d("bo", [C], F32, kind="ExternalInput").ap()
        self.wgt = d("wgt", [C, H], I32, kind="ExternalInput").ap()
        self.wut = d("wut", [C, H], I32, kind="ExternalInput").ap()
        self.wdt = d("wdt", [H, C], I32, kind="ExternalInput").ap()
        self.out = d("outT", [n_seq, C, T], F32, kind="ExternalOutput").ap()

        dbg = "ExternalOutput" if debug else "Internal"
        self.debug = debug
        self.x1d = d("x1_dram", [C, T], F32, kind=dbg).ap()
        self.x2d = d("x2_dram", [C, T], F32, kind=dbg).ap()
        if debug:
            self.dbg_dacc = d("dbg_dacc", [2, C, T], F32, kind=dbg).ap()
            self.dbg_h = d("dbg_h", [2, C, T], F16, kind=dbg).ap()
            self.dbg_vec = d("dbg_vec", [2, 2, T], F32, kind=dbg).ap()
            self.dbg_bc = d("dbg_bc", [2, 128, 2, SG], F32, kind=dbg).ap()
            self.dbg_k = d("dbg_k", [128, CC, T], F16, kind=dbg).ap()
            self.dbg_v = d("dbg_v", [128, TT, C], F16, kind=dbg).ap()
            self.dbg_den = d("dbg_den", [NQG, QG], F32, kind=dbg).ap()
            self.dbg_y = d("dbg_y", [NQG, 128, CC, QG], F16, kind=dbg).ap()
        self.vecd = d("vec_dram", [8, T], F32).ap()
        self.wg16 = d("wg16_dram", [HC, 128, CC, 128], F16).ap()
        self.wu16 = d("wu16_dram", [HC, 128, CC, 128], F16).ap()
        self.wd16 = d("wd16_dram", [HC, 128, C], F16).ap()
        self.wsp = {}
        for w in ("wq", "wk", "wo"):
            for h in ("hi", "lo"):
                self.wsp[f"{w}_{h}"] = d(f"{w}_{h}_dram", [CC, 128, CC, 128], F16).ap()
        for h in ("hi", "lo"):
            self.wsp[f"wv_{h}"] = d(f"wv_{h}_dram", [4, 128, CC, 256], F16).ap()

        self.arena = es.enter_context(nc.sbuf_tensor("arena", [128, arena_words], F32))
        self._off = 0

        def V(shape, dt=F32, parts=128, at=None):
            n = 1
            for s in shape:
                n *= s
            words = (n + 1) // 2 if dt == F16 else n
            words = (words + 1) & ~1
            off = self._off if at is None else at
            if at is None:
                self._off += words
            ap = self.arena[0:parts, off:off + words]
            ap = ap.bitcast(F16)[:, 0:n] if dt == F16 else ap[:, 0:n]
            if len(shape) > 1:
                letters = list("abcdef")[: len(shape)]
                ap = ap.rearrange(
                    f"p ({' '.join(letters)}) -> p {' '.join(letters)}",
                    **{letters[i]: shape[i] for i in range(len(shape))})
            return ap, off

        self.k_hi, o0 = V([CC, T], F16)
        self.k_lo, o1 = V([CC, T], F16)
        self.v_hi, o2 = V([TT, C], F16)
        self.v_lo, o3 = V([TT, C], F16)
        self.h_hi, _ = V([CC, T], F16, at=o0)
        self.h_lo, _ = V([CC, T], F16, at=o1)
        self.dacc, _ = V([CC, T], F32, at=o2)

        ring0 = self._off
        self.rg, _ = V([NRS, CC, 128], F16)
        self.ru, _ = V([NRS, CC, 128], F16)
        self.rd, _ = V([NRS, C], F16)
        ring_end = self._off
        self.wvr, _ = V([2, 2, CC, 256], F16, at=ring0)
        self.rw_hi, _ = V([2, CC, 128], F16, at=ring0 + 4096)
        self.rw_lo, _ = V([2, CC, 128], F16, at=ring0 + 4096 + 1024)
        self._off = max(ring_end, ring0 + 4096 + 2048)

        work0 = self._off
        self.q_hi, _ = V([CC, QG], F16)
        self.q_lo, _ = V([CC, QG], F16)
        self.y_hi, yo_ = V([CC, QG], F16)
        self.y_lo, _ = V([CC, QG], F16)
        self._yhi_off = yo_
        self.th_hi, _ = V([CC, SG], F16)
        self.th_lo, _ = V([CC, SG], F16)
        self.att_hi, _ = V([2, QG], F16)
        self.att_lo, _ = V([2, QG], F16)
        work_end = self._off
        self.act_hi, _ = V([BHC, FG], F16, at=work0)
        self.act_lo, _ = V([BHC, FG], F16, at=work0 + 1024)
        self._off = work_end

        self.xstg, _ = V([CC, SG], F32)
        self.scr, _ = V([CC, SG], F32)

        self.vecs, _ = V([T], F32, parts=128)
        self.bcast, _ = V([2, SG], F32)
        self.recipb, _ = V([QG], F32)
        self.sels, _ = V([384], F32)
        self.masks, _ = V([2, QG], F16)
        self.ones, _ = V([2], F16)
        self.onesf, _ = V([2], F32)
        self.bo_sb, _ = V([CC], F32)
        print(f"[arena] {self._off} words = {self._off * 4 / 1024:.1f} KiB/partition")
        assert self._off <= arena_words, self._off

        ps = lambda name, shape: es.enter_context(nc.psum_tensor(name, shape, F32))
        self.big4 = ps("big4", [128, 2048])
        self.big4_d = self.big4.ap().rearrange("p (a b) -> p a b", a=4)
        self.big4_n = self.big4.ap().rearrange("p (a b) -> p a b", a=8)
        self.pA = ps("pA", [128, 512])
        self.pB = ps("pB", [128, 512])
        self.pC = ps("pC", [128, 512])
        self.pD = ps("pD", [128, 512])
        self.sem = {s: es.enter_context(nc.semaphore(s)) for s in self.semnames}

    ROW_M, ROW_V, ROW_R, ROW_DEN = 0, 32, 64, 96

    def vrow(self, r, sl=slice(None)):
        return self.vecs[r:r + 1, sl]

    def build(self):
        nc = self.nc
        with self.es, nc.Block() as block:
            for name in ("sync", "scalar", "vector", "tensor", "gpsimd"):
                def mk(engname):
                    def body(eng):
                        em = Emitter(self, engname, eng)
                        self.schedule(em)
                    return body
                getattr(block, name)(mk(name))
        return nc

    def schedule(self, em):
        self.setup(em)
        em.barrier()
        for s in range(self.n_seq):
            if self.phases == 'faf':
                self.ffn_phase(em, self.xin[s], self.x1d, dbg_idx=0 if s == self.n_seq - 1 else None)
                em.barrier()
                self.attn_phase(em, self.x1d, self.x2d)
                em.barrier()
                self.ffn_phase(em, self.x2d, self.out[s], dbg_idx=1 if s == self.n_seq - 1 else None)
            elif self.phases == 'f':
                self.ffn_phase(em, self.xin[s], self.out[s])
            elif self.phases == 'a':
                self.attn_phase(em, self.xin[s], self.out[s])
            elif self.phases == 'a1':
                self.attn_phase(em, self.xin[s], self.out[s], upto=1)
            elif self.phases == 'fa':
                self.ffn_phase(em, self.xin[s], self.x1d)
                em.barrier()
                self.attn_phase(em, self.x1d, self.out[s])
            em.barrier()
        em.run("gpsimd", lambda g: g.wait_ge(self.sem["st"], em.tok("st")) or g.nop())

    def setup(self, em):
        c = self
        em.run("vector", lambda v: v.memset(c.ones, 1.0), sem="ev", inc=1)
        em.run("vector", lambda v: v.memset(c.onesf, 1.0), sem="ev", inc=1)
        em.run("gpsimd", lambda g: g.iota(c.sels, [[0, 384]], channel_multiplier=1,
               allow_small_or_imprecise_dtypes=True), sem="gp", inc=1)
        em.wait("vector", "gp", em.tok("gp"))
        def mk_sels(v):
            v.tensor_scalar(c.sels[:, 0:128], c.sels[:, 0:128], 0.0, None, ALU.is_equal)
            v.tensor_scalar(c.sels[:, 128:256], c.sels[:, 128:256], 32.0, None, ALU.is_equal)
            v.tensor_scalar(c.sels[:, 256:384], c.sels[:, 256:384], 64.0, None, ALU.is_equal)
            return v.memset(c.vecs[0:96, :], 0.0)
        em.run("vector", mk_sels, sem="ev", inc=1)
        em.run("sync", lambda s: s.dma_start(
            out=c.bo_sb, in_=c.bo.rearrange("(d p) -> p d", p=128),
            allow_slow_non_contiguous=True), sem="ld", inc=16)

        def mk_masks(g):
            ins = None
            for var in range(2):
                ins = g.iota(c.masks[:, var, :], [[1, QG]], base=-128 * var,
                             channel_multiplier=-1,
                             allow_small_or_imprecise_dtypes=True)
            return ins
        em.run("gpsimd", mk_masks, sem="gp", inc=1)
        em.wait("vector", "gp", em.tok("gp"))

        def fin_masks(v):
            return v.tensor_scalar(c.masks[:], c.masks[:], -0.5, None, ALU.is_ge)
        em.run("vector", fin_masks, sem="ev", inc=1)

        stgs_i = (c.xstg.bitcast(I32)[:, :, 0:128], c.scr.bitcast(I32)[:, :, 0:128])
        outs_f = (c.th_hi, c.th_lo)
        st_tok = [0, 0]
        idx = 0
        for wsrc, wdst in ((c.wgt, c.wg16), (c.wut, c.wu16)):
            for hc in range(HC):
                p = idx % 2
                idx += 1
                hsl = slice(hc * 128, hc * 128 + 128)
                em.wait("sync", "st", st_tok[p])
                t_ld = em.run("sync", lambda s, wsrc=wsrc, hsl=hsl, p=p: s.dma_start(
                    out=stgs_i[p],
                    in_=wsrc.rearrange("(a p) n -> p a n", p=128)[:, :, hsl]),
                    sem="ld", inc=16)
                em.wait("vector", "ld", t_ld)
                em.wait("vector", "st", st_tok[p])
                t_cv = em.run("vector", lambda v, p=p: v.tensor_copy(
                    outs_f[p][:, :, 0:128], stgs_i[p]), sem="ev", inc=1)
                em.wait("gpsimd", "ev", t_cv)
                st_tok[p] = em.run("gpsimd", lambda g, wdst=wdst, hc=hc, p=p: g.dma_start(
                    out=wdst[hc], in_=outs_f[p][:, :, 0:128]), sem="st", inc=16)
        stgs_i2 = (c.xstg.bitcast(I32)[:, 0:4, :], c.scr.bitcast(I32)[:, 0:4, :])
        for hc in range(HC):
            p = idx % 2
            idx += 1
            em.wait("sync", "st", st_tok[p])
            t_ld = em.run("sync", lambda s, hc=hc, p=p: s.dma_start(
                out=stgs_i2[p],
                in_=c.wdt.rearrange("(a p) n -> p a n", p=128)[:, hc, :]
                    .rearrange("p (a b) -> p a b", a=4)),
                sem="ld", inc=16)
            em.wait("vector", "ld", t_ld)
            em.wait("vector", "st", st_tok[p])
            t_cv = em.run("vector", lambda v, p=p: v.tensor_copy(
                outs_f[p][:, 0:4, :], stgs_i2[p]), sem="ev", inc=1)
            em.wait("gpsimd", "ev", t_cv)
            st_tok[p] = em.run("gpsimd", lambda g, hc=hc, p=p: g.dma_start(
                out=c.wd16[hc],
                in_=outs_f[p][:, 0:4, :].rearrange("p a b -> p (a b)")),
                sem="st", inc=16)

        stgs_f = (c.xstg, c.scr)
        outs_hi = (c.th_hi, c.q_hi)
        outs_lo = (c.th_lo, c.q_lo)
        for wname, wsrc in (("wq", c.wqt), ("wk", c.wkt), ("wo", c.wot), ("wv", c.wvt)):
            ncols = 128 if wname != "wv" else 256
            for blk in range(C // ncols):
                p = idx % 2
                idx += 1
                csl = slice(blk * ncols, (blk + 1) * ncols)
                em.wait("sync", "st", st_tok[p])
                t_ld = em.run("sync", lambda s, wsrc=wsrc, csl=csl, ncols=ncols, p=p: s.dma_start(
                    out=stgs_f[p][:, :, 0:ncols],
                    in_=wsrc.rearrange("(a p) n -> p a n", p=128)[:, :, csl]),
                    sem="ld", inc=16)
                em.wait("scalar", "ld", t_ld)
                em.wait("scalar", "st", st_tok[p])
                wscale = 64.0 if wname == "wo" else 1.0
                t_hi = em.run("scalar", lambda sc, ncols=ncols, wscale=wscale, p=p: sc.activation(
                    outs_hi[p][:, :, 0:ncols], stgs_f[p][:, :, 0:ncols], AF.Copy,
                    scale=wscale), sem="sc", inc=1)
                em.wait("vector", "sc", t_hi)
                t_lo = em.run("vector", lambda v, ncols=ncols, wscale=wscale, p=p: v.scalar_tensor_tensor(
                    outs_lo[p][:, :, 0:ncols], stgs_f[p][:, :, 0:ncols], wscale,
                    outs_hi[p][:, :, 0:ncols], ALU.mult, ALU.subtract), sem="ev", inc=1)
                em.wait("gpsimd", "sc", t_hi)
                em.wait("gpsimd", "ev", t_lo)
                em.run("gpsimd", lambda g, wname=wname, blk=blk, ncols=ncols, p=p: g.dma_start(
                    out=c.wsp[f"{wname}_hi"][blk], in_=outs_hi[p][:, :, 0:ncols]),
                    sem="st", inc=16)
                st_tok[p] = em.run("gpsimd", lambda g, wname=wname, blk=blk, ncols=ncols, p=p: g.dma_start(
                    out=c.wsp[f"{wname}_lo"][blk], in_=outs_lo[p][:, :, 0:ncols]),
                    sem="st", inc=16)

    def ln_stats(self, em, xsrc):
        c = self
        t_cons = 0
        t_statmm = 0
        t_stat_ev = 0
        for g in range(NSG):
            sl = slice(g * SG, (g + 1) * SG)
            em.wait("sync", "sc", t_cons)
            em.wait("sync", "mm", t_statmm)
            t_ld = em.run("sync", lambda s, sl=sl: s.dma_start(
                out=c.xstg, in_=xsrc.rearrange("(a p) t -> p a t", p=128)[:, :, sl]),
                sem="ld", inc=16)
            em.wait("scalar", "ld", t_ld)
            em.wait("scalar", "mm", t_statmm)
            t_cons = em.run("scalar", lambda sc: sc.activation(
                c.scr, c.xstg, AF.Square), sem="sc", inc=1)
            em.wait("tensor", "sc", t_cons)
            em.wait("tensor", "ld", t_ld)
            em.wait("tensor", "ev", t_stat_ev)
            def mm_stats(t):
                ins = None
                for cc in range(CC):
                    ins = t.matmul(c.pC[0:1, 0:SG], c.onesf[:, 0:1], c.xstg[:, cc, :],
                                   start=(cc == 0), stop=(cc == CC - 1))
                for cc in range(CC):
                    ins = t.matmul(c.pD[0:1, 0:SG], c.onesf[:, 0:1], c.scr[:, cc, :],
                                   start=(cc == 0), stop=(cc == CC - 1))
                return ins
            t_mm = em.run("tensor", mm_stats, sem="mm", inc=1)
            t_statmm = t_mm
            em.wait("vector", "mm", t_mm)
            t_stat_ev = em.run("vector", lambda v, sl=sl: (
                v.tensor_copy(c.vrow(self.ROW_M, sl), c.pC[0:1, 0:SG]),
                v.tensor_copy(c.vrow(self.ROW_V, sl), c.pD[0:1, 0:SG]))[-1],
                sem="ev", inc=1)
        em.state["ln_xstg_free"] = t_cons
        em.state["ln_stats_mm"] = t_statmm
        return t_stat_ev

    def bcast_mr(self, em, g, t_mr, t_free, width=SG):
        c = self
        sl = slice(g * width, g * width + width)
        em.wait("tensor", "ev", max(t_mr, em.state.get("bc_ev", 0),
                                    em.state.get("rb_ev", 0)))
        def mm_bc(t, sl=sl, width=width):
            t.matmul(c.pD[:, 0:width], c.sels[0:33, 0:128], c.vecs[0:33, sl],
                     start=True, stop=False)
            return t.matmul(c.pD[:, 256:256 + width], c.sels[0:33, 128:256],
                            c.vecs[0:33, sl], start=False, stop=True)
        t_bm = em.run("tensor", mm_bc, sem="mm", inc=1)
        em.wait("vector", "mm", t_bm)
        em.wait("vector", "ev", t_free)
        t_ev = em.run("vector", lambda v, width=width: (
            v.tensor_copy(c.bcast[:, 0, 0:width], c.pD[:, 0:width]),
            v.tensor_copy(c.bcast[:, 1, 0:width], c.pD[:, 256:256 + width]))[-1],
            sem="ev", inc=1)
        em.state["bc_ev"] = t_ev
        return t_ev

    def h_slice(self, em, dst_hi, dst_lo, width=SG):
        c = self
        em.wait("vector", "mm", em.state.get("ln_stats_mm", 0))
        def mk1(v):
            b0 = c.bcast[:, 0, 0:width]
            b1 = c.bcast[:, 1, 0:width]
            t = c.scr[:, 7, 0:width]
            v.tensor_scalar(b0, b0, 1.0 / C, None, ALU.mult)
            v.tensor_tensor(t, b0, b0, ALU.mult)
            v.scalar_tensor_tensor(b1, b1, 1.0 / C, t, ALU.mult, ALU.subtract)
            v.tensor_scalar(b1, b1, EPS, None, ALU.add)
            return v.reciprocal(b1, b1)
        t1 = em.run("vector", mk1, sem="ev", inc=1)
        em.wait("scalar", "ev", t1)
        t2 = em.run("scalar", lambda sc, width=width: sc.activation(
            c.bcast[:, 1, 0:width], c.bcast[:, 1, 0:width], AF.Sqrt),
            sem="sc", inc=1)
        em.wait("vector", "sc", t2)
        def mk2(v):
            v.tensor_tensor(c.scr[:, :, 0:width], c.xstg[:, :, 0:width],
                            c.bcast[:, 0:1, 0:width].broadcast_to((128, CC, width)),
                            ALU.subtract)
            v.tensor_tensor(c.scr[:, :, 0:width], c.scr[:, :, 0:width],
                            c.bcast[:, 1:2, 0:width].broadcast_to((128, CC, width)),
                            ALU.mult)
            v.tensor_copy(dst_hi, c.scr[:, :, 0:width])
            return v.tensor_sub(dst_lo, c.scr[:, :, 0:width], dst_hi)
        return em.run("vector", mk2, sem="ev", inc=1)

    def ffn_phase(self, em, xsrc, xdst, dbg_idx=None):
        c = self
        t_mr = self.ln_stats(em, xsrc)
        if self.debug and dbg_idx is not None:
            em.wait("gpsimd", "st", t_mr)
            em.run("gpsimd", lambda g, dbg_idx=dbg_idx: g.dma_start(
                out=self.dbg_vec[dbg_idx], in_=c.vecd[0:2, :]), sem="st", inc=16)
        t_h = 0
        for g in range(NSG):
            sl = slice(g * SG, (g + 1) * SG)
            em.wait("gpsimd", "ev", t_h)
            em.wait("gpsimd", "sc", em.state["ln_xstg_free"])
            em.wait("gpsimd", "mm", em.state["ln_stats_mm"])
            t_ld = em.run("gpsimd", lambda s, sl=sl: s.dma_start(
                out=c.xstg, in_=xsrc.rearrange("(a p) t -> p a t", p=128)[:, :, sl]),
                sem="ld", inc=16)
            t_bc = self.bcast_mr(em, g, t_mr, t_h)
            em.wait("vector", "ld", t_ld)
            t_h = self.h_slice(em, c.h_hi[:, :, sl], c.h_lo[:, :, sl])
            if self.paranoid & 16:
                em.barrier()
        if self.debug and dbg_idx is not None:
            em.wait("gpsimd", "ev", t_h)
            em.run("gpsimd", lambda g, dbg_idx=dbg_idx: g.dma_start(
                out=self.dbg_bc[dbg_idx], in_=c.bcast), sem="st", inc=16)

        slot_free = {}
        t_wl = {}

        def load_slot(slot, hc):
            em.wait("sync", "mm", slot_free.get(slot, 0))
            em.run("sync", lambda s: s.dma_start(
                out=c.rg[:, slot], in_=c.wg16[hc]), sem="wl", inc=16)
            em.run("sync", lambda s: s.dma_start(
                out=c.ru[:, slot], in_=c.wu16[hc]), sem="wl", inc=16)
            return em.run("sync", lambda s: s.dma_start(
                out=c.rd[:, slot], in_=c.wd16[hc]), sem="wl", inc=16)

        for i in range(BHC):
            t_wl[(0, i)] = load_slot(i % NRS, i)

        em.wait("tensor", "ev", t_h)
        t_act_prev = 0
        t_down_mm_prev = 0
        t_down_ev_prev = 0
        for b in range(NB):
            for tg in range(NFG):
                sl = slice(tg * FG, (tg + 1) * FG)
                t_act = {}
                for i in range(BHC):
                    slot = (b * BHC + i) % NRS
                    em.wait("tensor", "wl", t_wl[(b, i)])
                    pg, pu = (c.pA, c.pB) if i % 2 == 0 else (c.pC, c.pD)
                    em.wait("tensor", "ev", t_act.get(i - 2, t_act_prev))
                    def mm_gu(t, slot=slot, sl=sl, pg=pg, pu=pu):
                        ins = None
                        for src, ps_ in ((c.rg, pg), (c.ru, pu)):
                            for p, hsp in enumerate((c.h_hi, c.h_lo)):
                                for cc in range(CC):
                                    ins = t.matmul(ps_[:, 0:FG], src[:, slot, cc, :],
                                                   hsp[:, cc, sl],
                                                   start=(p == 0 and cc == 0),
                                                   stop=(p == 1 and cc == CC - 1))
                        return ins
                    t_mm = em.run("tensor", mm_gu, sem="mm", inc=1)
                    em.wait("scalar", "mm", t_mm)
                    t_si = em.run("scalar", lambda sc, pg=pg: sc.activation(
                        c.scr[:, 0:2, :].rearrange("p a b -> p (a b)"), pg[:, 0:FG],
                        AF.Silu, scale=QSC), sem="sc", inc=1)
                    em.wait("vector", "sc", t_si)
                    em.wait("vector", "mm", t_down_mm_prev)
                    def mk_act(v, i=i, pu=pu):
                        z = c.scr[:, 2:4, :].rearrange("p a b -> p (a b)")
                        silu = c.scr[:, 0:2, :].rearrange("p a b -> p (a b)")
                        v.scalar_tensor_tensor(z, pu[:, 0:FG], QSC, silu, ALU.mult, ALU.mult)
                        v.tensor_copy(c.act_hi[:, i, :], z)
                        return v.tensor_sub(c.act_lo[:, i, :], z, c.act_hi[:, i, :])
                    t_act[i] = em.run("vector", mk_act, sem="ev", inc=1)
                t_act_prev = t_act[BHC - 1]
                for dh in range(2):
                    em.wait("tensor", "ev", t_down_ev_prev)
                    def mm_down_i(t, b=b, dh=dh, i=0, sub=False):
                        ins = None
                        rng = (i,) if sub else range(BHC)
                        for ii in rng:
                            slot = (b * BHC + ii) % NRS
                            for p, asp in enumerate((c.act_hi, c.act_lo)):
                                for dc in range(4):
                                    dsl = slice(dh * 512 + dc * 128,
                                                dh * 512 + (dc + 1) * 128)
                                    ins = t.matmul(c.big4_d[:, dc, 0:FG],
                                                   c.rd[:, slot, dsl], asp[:, ii, :],
                                                   start=(ii == 0 and p == 0),
                                                   stop=(ii == BHC - 1 and p == 1))
                        return ins
                    if dh == 0:
                        for i in range(BHC):
                            em.wait("tensor", "ev", t_act[i])
                            t_down_mm_prev = em.run(
                                "tensor",
                                (lambda ii, bb=b, dd=dh: (
                                    lambda t: mm_down_i(t, b=bb, dh=dd, i=ii, sub=True)))(i),
                                sem="mm", inc=1)
                    else:
                        t_down_mm_prev = em.run("tensor", mm_down_i, sem="mm", inc=1)
                    em.wait("vector", "mm", t_down_mm_prev)
                    def ev_down(v, b=b, dh=dh, sl=sl):
                        dst = c.dacc[:, dh * 4:(dh + 1) * 4, sl]
                        src = c.big4_d[:, :, 0:FG]
                        if b == 0:
                            return v.tensor_copy(dst, src)
                        return v.tensor_tensor(dst, dst, src, ALU.add)
                    t_down_ev_prev = em.run("vector", ev_down, sem="ev", inc=1)
                if tg == NFG - 1:
                    for i in range(BHC):
                        slot_free[(b * BHC + i) % NRS] = t_down_mm_prev
                if self.paranoid & 8:
                    em.barrier()
            if self.paranoid & 4:
                em.barrier()
            if b + 1 < NB:
                for i in range(BHC):
                    hc = (b + 1) * BHC + i
                    t_wl[(b + 1, i)] = load_slot(hc % NRS, hc)

        if self.debug and dbg_idx is not None:
            em.wait("gpsimd", "ev", t_down_ev_prev)
            em.run("gpsimd", lambda g, dbg_idx=dbg_idx: g.dma_start(
                out=self.dbg_dacc[dbg_idx].rearrange("(a p) t -> p a t", p=128),
                in_=c.dacc), sem="st", inc=16)
            em.run("gpsimd", lambda g, dbg_idx=dbg_idx: g.dma_start(
                out=self.dbg_h[dbg_idx].rearrange("(a p) t -> p a t", p=128),
                in_=c.h_hi), sem="st", inc=16)
        t_f = t_down_ev_prev
        t_fin_st = em.tok("st")
        for g in range(NSG):
            sl = slice(g * SG, (g + 1) * SG)
            em.wait("sync", "ev", t_f)
            t_ld = em.run("sync", lambda s, sl=sl: s.dma_start(
                out=c.xstg, in_=xsrc.rearrange("(a p) t -> p a t", p=128)[:, :, sl]),
                sem="ld", inc=16)
            em.wait("vector", "ld", t_ld)
            em.wait("vector", "ev", t_down_ev_prev)
            em.wait("vector", "st", t_fin_st)
            t_f = em.run("vector", lambda v, sl=sl: v.scalar_tensor_tensor(
                c.scr, c.dacc[:, :, sl], 0.5 * QSC, c.xstg, ALU.mult, ALU.add),
                sem="ev", inc=1)
            em.wait("gpsimd", "ev", t_f)
            t_fin_st = em.run("gpsimd", lambda g_, sl=sl: g_.dma_start(
                out=xdst.rearrange("(a p) t -> p a t", p=128)[:, :, sl], in_=c.scr),
                sem="st", inc=16)

    def attn_phase(self, em, xsrc, xdst, upto=99):
        c = self
        t_mr = self.ln_stats(em, xsrc)

        rw_free = {0: 0, 1: 0}
        wv_free = {0: 0, 1: 0}

        def load_wv(slot, dq):
            em.wait("sync", "mm", wv_free[slot])
            em.run("sync", lambda s: s.dma_start(
                out=c.wvr[:, slot, 0], in_=c.wsp["wv_hi"][dq]), sem="wl", inc=16)
            return em.run("sync", lambda s: s.dma_start(
                out=c.wvr[:, slot, 1], in_=c.wsp["wv_lo"][dq]), sem="wl", inc=16)

        def load_rw(slot, wname, blk):
            em.wait("sync", "mm", rw_free[slot])
            em.run("sync", lambda s: s.dma_start(
                out=c.rw_hi[:, slot], in_=c.wsp[f"{wname}_hi"][blk]), sem="wl", inc=16)
            return em.run("sync", lambda s: s.dma_start(
                out=c.rw_lo[:, slot], in_=c.wsp[f"{wname}_lo"][blk]), sem="wl", inc=16)

        first_vb = True
        t_h = 0
        t_th_cons = 0
        for g in range(NSG):
            sl = slice(g * SG, (g + 1) * SG)
            em.wait("scalar", "ev", t_h)
            em.wait("scalar", "sc", em.state["ln_xstg_free"])
            em.wait("scalar", "mm", em.state["ln_stats_mm"])
            t_ld = em.run("scalar", lambda s, sl=sl: s.dma_start(
                out=c.xstg, in_=xsrc.rearrange("(a p) t -> p a t", p=128)[:, :, sl]),
                sem="ld", inc=16)
            t_bc = self.bcast_mr(em, g, t_mr, t_h)
            em.wait("vector", "ld", t_ld)
            em.wait("vector", "mm", t_th_cons)
            t_h = self.h_slice(em, c.th_hi, c.th_lo)
            em.wait("tensor", "ev", t_h)
            rw_tok = {0: load_rw(0, "wk", 0), 1: load_rw(1, "wk", 1)}
            t_evk = {}
            for dkc in range(CC):
                slot = dkc % 2
                em.wait("tensor", "wl", rw_tok[dkc])
                ps_ = c.pA if dkc % 2 == 0 else c.pB
                em.wait("tensor", "ev", t_evk.get(dkc - 2, 0))
                def mm_k(t, slot=slot, ps_=ps_):
                    ins = None
                    for p, (whalf, hsp) in enumerate((
                            (c.rw_hi, c.th_hi), (c.rw_hi, c.th_lo), (c.rw_lo, c.th_hi))):
                        for cc in range(CC):
                            ins = t.matmul(ps_[:, 0:SG], whalf[:, slot, cc, :],
                                           hsp[:, cc, :],
                                           start=(p == 0 and cc == 0),
                                           stop=(p == 2 and cc == CC - 1))
                    return ins
                t_mm = em.run("tensor", mm_k, sem="mm", inc=1)
                rw_free[slot] = t_mm
                if dkc + 2 < CC:
                    rw_tok[dkc + 2] = load_rw(slot, "wk", dkc + 2)
                em.wait("vector", "mm", t_mm)
                def ev_k(v, dkc=dkc, sl=sl, ps_=ps_):
                    v.tensor_scalar(c.k_hi[:, dkc, sl], ps_[:, 0:SG], 0.1, None, ALU.mult)
                    return v.scalar_tensor_tensor(c.k_lo[:, dkc, sl], ps_[:, 0:SG], 0.1,
                                                  c.k_hi[:, dkc, sl], ALU.mult, ALU.subtract)
                t_evk[dkc] = em.run("vector", ev_k, sem="ev", inc=1)
            t_th_cons = em.tok("mm")
            t_evv = {}
            wv_tok = {0: load_wv(0, 0), 1: load_wv(1, 1)}
            for dq in range(4):
                wslot = dq % 2
                for ti in range(2):
                    tt = g * 2 + ti
                    idx = dq * 2 + ti
                    ps_ = c.pA if idx % 2 == 0 else c.pB
                    em.wait("tensor", "wl", wv_tok[dq])
                    em.wait("tensor", "ev", t_evv.get(idx - 2, t_evk[CC - 1]))
                    def mm_v(t, ti=ti, wslot=wslot, ps_=ps_):
                        ins = None
                        for p, (hsp, wh) in enumerate(((c.th_hi, 0), (c.th_lo, 0),
                                                       (c.th_hi, 1))):
                            for cc in range(CC):
                                ins = t.matmul(ps_[:, 0:256],
                                               hsp[:, cc, ti * 128:(ti + 1) * 128],
                                               c.wvr[:, wslot, wh, cc, :],
                                               start=(p == 0 and cc == 0),
                                               stop=(p == 2 and cc == CC - 1))
                        return ins
                    t_mm = em.run("tensor", mm_v, sem="mm", inc=1)
                    wv_free[wslot] = t_mm
                    if ti == 1 and dq + 2 < 4:
                        wv_tok[dq + 2] = load_wv(wslot, dq + 2)
                    em.wait("vector", "mm", t_mm)
                    def ev_v(v, tt=tt, dq=dq, ps_=ps_):
                        dsl = slice(dq * 256, (dq + 1) * 256)
                        v.tensor_copy(c.v_hi[:, tt, dsl], ps_[:, 0:256])
                        return v.tensor_sub(c.v_lo[:, tt, dsl], ps_[:, 0:256],
                                            c.v_hi[:, tt, dsl])
                    t_evv[idx] = em.run("vector", ev_v, sem="ev", inc=1)
            t_th_cons = em.tok("mm")
            if self.paranoid & 2:
                em.barrier()
        t_vrem = em.tok("ev")
        if upto == 1:
            em.wait("gpsimd", "ev", t_vrem)
            flat = xdst.rearrange("(a p) t -> p a t", p=128)
            for i in range(4):
                src = self.arena[0:128, 8192 * i:8192 * i + 4096].rearrange(
                    "p (a t) -> p a t", a=2)
                em.run("gpsimd", lambda g, i=i, src=src: g.dma_start(
                    out=flat[:, 2 * i:2 * i + 2, :], in_=src), sem="st", inc=16)
            return
        if self.debug:
            em.wait("gpsimd", "ev", t_vrem)
            em.run("gpsimd", lambda g: g.dma_start(out=self.dbg_k, in_=c.k_hi),
                   sem="st", inc=16)
            em.run("gpsimd", lambda g: g.dma_start(out=self.dbg_v, in_=c.v_hi),
                   sem="st", inc=16)

        t_q_cons = 0
        t_y_cons = 0
        t_stg_cons = 0
        t_att = {}
        t_av = {}
        t_den_prev = 0
        t_y_prev = 0
        t_o_st = em.tok("st")
        for qg in range(NQG):
            sl = slice(qg * QG, (qg + 1) * QG)
            kb_max = 2 * qg + 1
            em.wait("scalar", "ev", max(t_stg_cons, t_h))
            t_ld = em.run("scalar", lambda s, sl=sl: s.dma_start(
                out=c.xstg, in_=xsrc.rearrange("(a p) t -> p a t", p=128)[:, :, sl]),
                sem="ld", inc=16)
            t_bc = self.bcast_mr(em, qg, t_mr, max(t_stg_cons, t_h))
            em.wait("vector", "ld", t_ld)
            em.wait("vector", "mm", t_th_cons)
            em.wait("vector", "st", t_o_st)
            t_h = self.h_slice(em, c.th_hi, c.th_lo)
            em.wait("tensor", "ev", t_h)
            rwq = {0: load_rw(0, "wq", 0), 1: load_rw(1, "wq", 1)}
            t_evq = {}
            for dqc in range(CC):
                slot = dqc % 2
                em.wait("tensor", "wl", rwq[dqc])
                ps_ = c.pA if dqc % 2 == 0 else c.pB
                em.wait("tensor", "ev", t_evq.get(dqc - 2, 0))
                def mm_q(t, slot=slot, ps_=ps_):
                    ins = None
                    for p, (whalf, hsp) in enumerate((
                            (c.rw_hi, c.th_hi), (c.rw_hi, c.th_lo), (c.rw_lo, c.th_hi))):
                        for cc in range(CC):
                            ins = t.matmul(ps_[:, 0:QG], whalf[:, slot, cc, :],
                                           hsp[:, cc, :],
                                           start=(p == 0 and cc == 0),
                                           stop=(p == 2 and cc == CC - 1))
                    return ins
                t_mm = em.run("tensor", mm_q, sem="mm", inc=1)
                rw_free[slot] = t_mm
                if dqc + 2 < CC:
                    rwq[dqc + 2] = load_rw(slot, "wq", dqc + 2)
                em.wait("vector", "mm", t_mm)
                em.wait("vector", "mm", t_q_cons)
                def ev_q(v, dqc=dqc, ps_=ps_):
                    v.tensor_scalar(c.q_hi[:, dqc, :], ps_[:, 0:QG], 0.1, None, ALU.mult)
                    return v.scalar_tensor_tensor(c.q_lo[:, dqc, :], ps_[:, 0:QG], 0.1,
                                                  c.q_hi[:, dqc, :], ALU.mult, ALU.subtract)
                t_evq[dqc] = em.run("vector", ev_q, sem="ev", inc=1)
            t_th_cons = em.tok("mm")
            if self.vmode & 32:
                em.barrier()
            em.wait("tensor", "ev", max(t_evq[CC - 1], t_y_prev, t_den_prev, t_vrem))
            av_fns = {}
            prev_kb = None
            for kb in range(kb_max + 1):
                ps_ = c.pA if kb % 2 == 0 else c.pB
                abuf = kb % 2
                em.wait("tensor", "ev", t_att.get(kb - 2, 0))
                def mm_s(t, kb=kb, ps_=ps_):
                    ins = None
                    for p, (ksp, qsp) in enumerate((
                            (c.k_hi, c.q_hi), (c.k_hi, c.q_lo), (c.k_lo, c.q_hi))):
                        for dkc in range(CC):
                            ins = t.matmul(ps_[:, 0:QG],
                                           ksp[:, dkc, kb * 128:(kb + 1) * 128],
                                           qsp[:, dkc, :],
                                           start=(p == 0 and dkc == 0),
                                           stop=(p == 2 and dkc == CC - 1))
                    return ins
                t_mm = em.run("tensor", mm_s, sem="mm", inc=1)
                if prev_kb is not None:
                    em.wait("tensor", "ev", t_att[prev_kb])
                    t_av[prev_kb] = em.run("tensor", av_fns[prev_kb], sem="mm", inc=1)
                em.wait("vector", "mm", max(t_mm, t_av.get(kb - 2, 0)))
                if kb >= 2 * qg:
                    var = kb - 2 * qg
                    def ev_att(v, ps_=ps_, abuf=abuf, var=var):
                        z = c.scr[:, 0, 0:QG]
                        v.tensor_scalar(z, ps_[:, 0:QG], 0.01, 1.0, ALU.mult, ALU.add)
                        v.tensor_tensor(z, z, c.masks[:, var, :], ALU.mult)
                        v.tensor_scalar(c.att_hi[:, abuf, :], z, -1.0, None, ALU.add)
                        return v.scalar_tensor_tensor(c.att_lo[:, abuf, :], z, -1.0,
                                                      c.att_hi[:, abuf, :],
                                                      ALU.add, ALU.subtract)
                else:
                    def ev_att(v, ps_=ps_, abuf=abuf):
                        v.tensor_scalar(c.att_hi[:, abuf, :], ps_[:, 0:QG], 0.01,
                                        None, ALU.mult)
                        return v.scalar_tensor_tensor(c.att_lo[:, abuf, :], ps_[:, 0:QG],
                                                      0.01, c.att_hi[:, abuf, :],
                                                      ALU.mult, ALU.subtract)
                t_att[kb] = em.run("vector", ev_att, sem="ev", inc=1)
                last = (kb == kb_max)
                final = (kb == TT - 1)
                def mm_av(t, kb=kb, abuf=abuf, first=(kb == 0), last=last, final=final):
                    for k_, asp in enumerate((c.att_hi, c.att_lo)):
                        t.matmul(c.pC[0:1, 0:QG], c.ones[:, 0:1], asp[:, abuf, :],
                                 start=(first and k_ == 0), stop=(last and k_ == 1))
                    ins = None
                    for p, (vsp, asp) in enumerate((
                            (c.v_hi, c.att_hi), (c.v_hi, c.att_lo), (c.v_lo, c.att_hi))):
                        for dc in range(CC):
                            ins = t.matmul(c.big4_n[:, dc, :],
                                           vsp[:, kb, dc * 128:(dc + 1) * 128],
                                           asp[:, abuf, :],
                                           start=(first and p == 0 and dc % 2 == 0),
                                           stop=(final and p == 2 and dc % 2 == 1))
                    return ins
                av_fns[kb] = mm_av
                prev_kb = kb
            em.wait("tensor", "ev", t_att[kb_max])
            t_av[kb_max] = em.run("tensor", av_fns[kb_max], sem="mm", inc=1)
            t_corr = t_av[kb_max]
            if kb_max < TT - 1:
                em.wait("vector", "mm", t_av[kb_max - 1])
                t_no = em.run("vector", lambda v: v.memset(c.att_hi[:, 0, :], -1.0),
                              sem="ev", inc=1)
                em.wait("tensor", "ev", t_no)
                def mm_corr(t, kb_max=kb_max):
                    ins = None
                    for ckb in range(kb_max + 1, TT):
                        for p, vsp in enumerate((c.v_hi, c.v_lo)):
                            for dc in range(CC):
                                ins = t.matmul(c.big4_n[:, dc, :],
                                               vsp[:, ckb, dc * 128:(dc + 1) * 128],
                                               c.att_hi[:, 0, :],
                                               start=False,
                                               stop=(ckb == TT - 1 and p == 1 and dc % 2 == 1))
                    return ins
                t_corr = em.run("tensor", mm_corr, sem="mm", inc=1)
            t_q_cons = em.tok("mm")
            em.wait("vector", "mm", t_corr)
            def fin_den(v, kb_max=kb_max):
                const = -(T - 128 * (kb_max + 1)) + EPS
                v.tensor_scalar(c.vecs[64:65, 0:QG], c.pC[0:1, 0:QG],
                                const, None, ALU.add)
                v.reciprocal(c.vecs[64:65, 0:QG], c.vecs[64:65, 0:QG])
                return v.tensor_scalar(c.vecs[64:65, 0:QG], c.vecs[64:65, 0:QG],
                                       1.0 / 64.0, None, ALU.mult)
            t_den = em.run("vector", fin_den, sem="ev", inc=1)
            t_den_prev = t_den
            em.wait("tensor", "ev", max(t_den, em.state.get("rb_ev", 0)))
            def mm_rb(t):
                return t.matmul(c.pD[:, 0:QG], c.sels[0:65, 256:384],
                                c.vecs[0:65, 0:QG], start=True, stop=True)
            t_rbm = em.run("tensor", mm_rb, sem="mm", inc=1)
            em.wait("vector", "mm", t_rbm)
            em.wait("vector", "ev", t_y_prev)
            t_rb = em.run("vector", lambda v: v.tensor_copy(c.recipb, c.pD[:, 0:QG]),
                          sem="ev", inc=1)
            em.state["rb_ev"] = t_rb
            em.wait("vector", "mm", t_y_cons)
            def ev_y(v, kb_max=kb_max):
                ins = None
                for dc in range(CC):
                    z = c.scr[:, 1, 0:QG]
                    v.tensor_tensor(z, c.big4_n[:, dc, :], c.recipb, ALU.mult)
                    v.tensor_copy(c.y_hi[:, dc, :], z)
                    ins = v.tensor_sub(c.y_lo[:, dc, :], z, c.y_hi[:, dc, :])
                return ins
            t_y = em.run("vector", ev_y, sem="ev", inc=1)
            t_y_prev = t_y
            if self.vmode & 16:
                em.wait("gpsimd", "ev", t_y)
                yw = self.arena[0:128, self._yhi_off:self._yhi_off + 2048].rearrange(
                    "p (a t) -> p a t", a=1)
                em.run("gpsimd", lambda g_, qg=qg, yw=yw: g_.dma_start(
                    out=xdst.rearrange("(a p) t -> p a t", p=128)[:, qg:qg + 1, :],
                    in_=yw), sem="st", inc=16)
                em.barrier()
                t_y_cons = em.tok("mm")
                t_stg_cons = t_y
                t_o_st = em.tok("st")
                continue
            if self.debug:
                em.wait("gpsimd", "ev", t_y)
                em.run("gpsimd", lambda g, qg=qg: g.dma_start(
                    out=self.dbg_y[qg], in_=c.y_hi), sem="st", inc=16)
            em.wait("tensor", "ev", t_y)
            rwo = {0: load_rw(0, "wo", 0), 1: load_rw(1, "wo", 1)}
            t_evo = {}
            for doc in range(CC):
                slot = doc % 2
                em.wait("tensor", "wl", rwo[doc])
                ps_ = c.pA if doc % 2 == 0 else c.pB
                em.wait("tensor", "ev", t_evo.get(doc - 2, 0))
                def mm_o(t, slot=slot, ps_=ps_):
                    ins = None
                    for p, (whalf, ysp) in enumerate((
                            (c.rw_hi, c.y_hi), (c.rw_hi, c.y_lo), (c.rw_lo, c.y_hi))):
                        for cc in range(CC):
                            ins = t.matmul(ps_[:, 0:QG], whalf[:, slot, cc, :],
                                           ysp[:, cc, :],
                                           start=(p == 0 and cc == 0),
                                           stop=(p == 2 and cc == CC - 1))
                    return ins
                t_mm = em.run("tensor", mm_o, sem="mm", inc=1)
                rw_free[slot] = t_mm
                if doc + 2 < CC:
                    rwo[doc + 2] = load_rw(slot, "wo", doc + 2)
                em.wait("vector", "mm", t_mm)
                em.wait("vector", "st", t_o_st)
                def ev_o(v, doc=doc, ps_=ps_):
                    return v.scalar_tensor_tensor(c.scr[:, doc, :], ps_[:, 0:QG],
                                                  c.bo_sb[:, doc:doc + 1],
                                                  c.xstg[:, doc, :], ALU.add, ALU.add)
                t_evo[doc] = em.run("vector", ev_o, sem="ev", inc=1)
            t_y_cons = em.tok("mm")
            t_stg_cons = t_evo[CC - 1]
            em.wait("gpsimd", "ev", t_evo[CC - 1])
            t_o_st = em.run("gpsimd", lambda g_, sl=sl: g_.dma_start(
                out=xdst.rearrange("(a p) t -> p a t", p=128)[:, :, sl], in_=c.scr),
                sem="st", inc=16)
            if self.paranoid & 1:
                em.barrier()


def build_nc(n_seq=4, debug=False):
    nc = bass.Bass("TRN2", target_bir_lowering=False)
    core = Core(nc, n_seq=n_seq, debug=debug)
    return core.build()




def kernel(L0_attn_x, L0_attn_Wq, L0_attn_Wk, L0_attn_Wv, L0_attn_Wo, L0_attn_bo,
           L0_ffn_w_gate, L0_ffn_w_up, L0_ffn_w_down):
    n_cores = 8
    n_seq_per_core = 4
    x = np.asarray(L0_attn_x)
    B = x.shape[0]
    assert B == n_cores * n_seq_per_core

    xT = np.ascontiguousarray(x.transpose(0, 2, 1))
    wqt = np.ascontiguousarray(np.asarray(L0_attn_Wq).T)
    wkt = np.ascontiguousarray(np.asarray(L0_attn_Wk).T)
    wvt = np.ascontiguousarray(np.asarray(L0_attn_Wv).T)
    wot = np.ascontiguousarray(np.asarray(L0_attn_Wo).T)
    bo = np.ascontiguousarray(np.asarray(L0_attn_bo))
    wgt = np.ascontiguousarray(np.asarray(L0_ffn_w_gate).T)
    wut = np.ascontiguousarray(np.asarray(L0_ffn_w_up).T)
    wdt = np.ascontiguousarray(np.asarray(L0_ffn_w_down).T)

    nc = bass.Bass("TRN2", target_bir_lowering=False)
    core = Core(nc, n_seq=n_seq_per_core)
    nc = core.build()

    in_maps = []
    for ci in range(n_cores):
        in_maps.append({
            "xT": xT[ci * n_seq_per_core:(ci + 1) * n_seq_per_core],
            "wqt": wqt, "wkt": wkt, "wvt": wvt, "wot": wot, "bo": bo,
            "wgt": wgt, "wut": wut, "wdt": wdt,
        })
    res = run_bass_kernel_spmd(nc, in_maps, core_ids=list(range(n_cores)))
    outs = [r["outT"] for r in res.results]
    out = np.concatenate(outs, axis=0).transpose(0, 2, 1)
    return np.ascontiguousarray(out.astype(np.float32))
